# version 01
# baseline (speedup 1.0000x reference)
import numpy as np
from contextlib import ExitStack

import concourse.bass as bass
import concourse.mybir as mybir
from concourse.bass_utils import run_bass_kernel_spmd

F32 = mybir.dt.float32
BF16 = mybir.dt.bfloat16
I8 = mybir.dt.int8
AF = mybir.ActivationFunctionType
ALU = mybir.AluOpType
AX = mybir.AxisListType.X

NCORE = 8
T = 4096
D = 1024
NJ = 32
NB = 8
CB = 128
EPS = 1e-5

DEBUG_TAPS = []
BUILD_PHASES = 99
DUMP = ""
CONV_ORDER = None
DEBUG_MEMSET = 0


def _pos_embedding(seq_len, dim):
    position = np.arange(seq_len, dtype=np.float64)[:, None]
    div_term = np.exp(np.arange(0, dim, 2, dtype=np.float64) *
                      (-np.log(10000.0) / dim))
    ang = position * div_term
    pe = np.zeros((seq_len, dim), dtype=np.float64)
    pe[:, 0::2] = np.sin(ang)
    pe[:, 1::2] = np.cos(ang)
    return pe


def _build_tables():
    tb = {}
    tb["posT"] = np.ascontiguousarray(_pos_embedding(T, 64).T).astype(np.float32)
    t1 = np.arange(64)[:, None]
    k1 = np.arange(65)[None, :]
    ang = 2.0 * np.pi * (t1 * k1) / 128.0
    m1c, m1s = np.cos(ang), -np.sin(ang)
    tb["M1c"] = np.concatenate([m1c, m1c], 0).astype(np.float32)
    tb["M1s"] = np.concatenate([m1s, m1s], 0).astype(np.float32)

    t2 = np.arange(64)[:, None]
    k2 = np.arange(32)[None, :]
    m2p = np.zeros((65, 128, 64))
    m2m = np.zeros((64, 128, 64))
    for g in range(65):
        th = 2.0 * np.pi * (t2 * g / 8192.0 + t2 * k2 / 64.0)
        c, s = np.cos(th), np.sin(th)
        m2p[g, 0:64, 0:32] = c
        m2p[g, 64:128, 0:32] = s
        m2p[g, 0:64, 32:64] = -s
        m2p[g, 64:128, 32:64] = c
        if 1 <= g <= 63:
            thm = 2.0 * np.pi * (t2 * (128 - g) / 8192.0 + t2 * k2 / 64.0)
            cm, sm = np.cos(thm), np.sin(thm)
            m2m[g, 0:64, 0:32] = cm
            m2m[g, 64:128, 0:32] = -sm
            m2m[g, 0:64, 32:64] = -sm
            m2m[g, 64:128, 32:64] = -cm
    tb["M2P"] = m2p.transpose(1, 0, 2).reshape(128, 65 * 64).astype(np.float32)
    tb["M2M"] = m2m[1:64].transpose(1, 0, 2).reshape(128, 63 * 64).astype(np.float32)
    m4 = np.zeros((128, 2))
    sgn = (-1.0) ** np.arange(64)
    m4[0:64, 0] = sgn
    m4[64:128, 1] = sgn
    tb["M4096"] = m4.astype(np.float32)

    t2p = np.arange(64)[None, :]
    k2c = np.arange(32)[:, None]
    phi = 2.0 * np.pi * (k2c * t2p) / 64.0
    c, s = np.cos(phi), np.sin(phi)
    esa = np.zeros((64, 128))
    esa[0:32, 0:64] = c
    esa[32:64, 0:64] = -s
    esa[0:32, 64:128] = s
    esa[32:64, 64:128] = c
    tb["ESA"] = np.concatenate([esa, esa], 0).astype(np.float32)
    e4 = np.zeros((2, 128))
    sg = 0.5 * (-1.0) ** np.arange(64)
    e4[0, 0:64] = sg
    e4[1, 64:128] = sg
    tb["E4096"] = e4.astype(np.float32)

    k1c = np.arange(128)[:, None]
    t1p = np.arange(64)[None, :]
    mbre = np.zeros((64, 128, 64))
    mbim = np.zeros((64, 128, 64))
    for tp in range(64):
        th = 2.0 * np.pi * ((tp + 64.0 * t1p) * k1c) / 8192.0
        mbre[tp] = (2.0 / 8192.0) * np.cos(th)
        mbim[tp] = -(2.0 / 8192.0) * np.sin(th)
    tb["MBre"] = mbre.transpose(1, 0, 2).reshape(128, 64 * 64).astype(np.float32)
    tb["MBim"] = mbim.transpose(1, 0, 2).reshape(128, 64 * 64).astype(np.float32)
    tb["IDN"] = np.eye(128, dtype=np.float32)
    tb["ONES"] = np.ones((1, 128), dtype=np.float32)
    tb["IDNF"] = np.eye(128, dtype=np.float32)
    return tb


_TABLES = None


def _tables():
    global _TABLES
    if _TABLES is None:
        _TABLES = _build_tables()
    return _TABLES


ENGINES = ("sync", "tensor", "vector", "scalar", "gpsimd")


class KB:
    def __init__(self):
        self.ops = {e: [] for e in ENGINES}
        self.cnt = {}
        self._vfence = 0

    def emit(self, eng, fn, wait=(), inc=(), amt=1, fence=False):
        if eng in ("vector", "scalar"):
            s0 = "v" if eng == "vector" else "s"
            wait = tuple(wait) + ((s0, self.cnt.get(s0, 0)),)
            if s0 not in inc:
                inc = tuple(inc) + (s0,)
        wait = tuple((s, c) for (s, c) in wait if c > 0)
        self.ops[eng].append((fn, wait, tuple(inc), amt))
        out = []
        for s in inc:
            self.cnt[s] = self.cnt.get(s, 0) + amt
            out.append(self.cnt[s])
        return out[0] if len(out) == 1 else (out if out else None)

    def dma(self, eng, dst, src, wait=(), sem=None, slow=False):
        sem = sem or ("g" if eng == "gpsimd" else "d")
        return self.emit(eng, lambda e, d=dst, s=src, sl=slow: e.dma_start(
            out=d, in_=s, allow_slow_non_contiguous=sl),
                         wait=wait, inc=(sem,), amt=16)

    def now(self, sem):
        return self.cnt.get(sem, 0)

    def barrier(self):
        snap = tuple((s, self.now(s)) for s in ("t", "v", "s", "p", "g", "d"))
        own = {"tensor": "t", "vector": "v", "scalar": "s", "gpsimd": "p"}
        for eng in ("tensor", "vector", "scalar", "gpsimd"):
            w = tuple((s, c) for s, c in snap if s != own[eng])
            self.emit(eng, lambda e: e.nop(), wait=w, inc=(own[eng],))

    def replay(self, block, sems):
        for eng_name in ENGINES:
            ops = list(self.ops[eng_name])
            if not ops:
                continue

            def make(ops_):
                def _f(eng):
                    for fn, wait, inc, amt in ops_:
                        for s, c in wait:
                            eng.wait_ge(sems[s], c)
                        instr = fn(eng)
                        for s in inc:
                            instr = instr.then_inc(sems[s], amt)
                return _f
            getattr(block, eng_name)(make(ops))


class PsumPool:
    def __init__(self):
        self.evac = [None] * 8
        self.mm = [0] * 8

    def wait_free(self, bank):
        ev = self.evac[bank]
        return (ev,) if ev else ()

    def set_mm(self, bank, tok):
        self.mm[bank] = tok

    def wait_mm(self, *banks):
        return tuple(("t", self.mm[bk]) for bk in banks)

    def set_evac(self, bank, sem, tok):
        self.evac[bank] = (sem, tok)


def build_nc():
    tb = _tables()
    nc = bass.Bass("TRN2", target_bir_lowering=False,
                   detect_race_conditions=False)

    x_in = nc.dram_tensor("x", [T, D], F32, kind="ExternalInput").ap()
    wt_in = nc.dram_tensor("wt", [D, D], F32, kind="ExternalInput").ap()
    w2_in = nc.dram_tensor("w2blk", [D, CB], F32, kind="ExternalInput").ap()
    w1_in = nc.dram_tensor("w1", [64, D], F32, kind="ExternalInput").ap()
    b1_in = nc.dram_tensor("b1", [128, 8], F32, kind="ExternalInput").ap()
    b2_in = nc.dram_tensor("b2blk", [128, 1], F32, kind="ExternalInput").ap()
    fb_in = nc.dram_tensor("fbias", [128, 8], F32, kind="ExternalInput").ap()
    y_out = nc.dram_tensor("y", [T, D], F32, kind="ExternalOutput").ap()
    tbl_in = {k: nc.dram_tensor("t_" + k, list(tb[k].shape), F32,
                                kind="ExternalInput").ap() for k in tb}
    taps = {name: nc.dram_tensor("dbg_" + name, list(shape), F32,
                                 kind="ExternalOutput").ap()
            for name, shape in DEBUG_TAPS}

    cc1_in = nc.dram_tensor("cc1_in", [1, 8], F32)
    cc1_out = nc.dram_tensor("cc1_out", [1, 8], F32)
    cc2_in = nc.dram_tensor("cc2_in", [1, 8], F32)
    cc2_out = nc.dram_tensor("cc2_out", [1, 8], F32)
    fbmy = nc.dram_tensor("fblob_my", [256, 8192], BF16)
    fball = nc.dram_tensor("fblob_all", [NCORE * 256, 8192], BF16)
    qw_spill = nc.dram_tensor("qw_spill", [128, 8192], BF16)
    sshb = nc.dram_tensor("sshb", [128, 8192], BF16)

    es = ExitStack()
    SBT = lambda name, shape, dt=BF16: es.enter_context(
        nc.sbuf_tensor(name, shape, dt))

    x_tm = SBT("x_tm", [128, NJ * D])
    big = SBT("big", [128, 32768])
    wbuf = SBT("wbuf", [128, 8192])
    cbuf = SBT("cbuf", [128, 8192])
    dbuf = SBT("dbuf", [128, 8192])
    tbuf = SBT("tbuf", [128, 8192])
    nst = SBT("nst", [128, 512])
    sqb = SBT("sqb", [128, 8])
    m1c = SBT("m1c_t", [128, 65])
    m1s = SBT("m1s_t", [128, 65])
    esa = SBT("esa_t", [128, 128])
    e4096 = SBT("e4096_t", [2, 128])
    m4096 = SBT("m4096_t", [128, 2])
    idn = SBT("idn_t", [128, 128])
    ones1 = SBT("ones_t", [1, 128], F32)
    idnf = SBT("idnf_t", [128, 128], F32)
    s4096 = SBT("s4096", [2, CB])
    bias8 = SBT("bias8", [128, 8], F32)
    b1s = SBT("b1s", [128, 8], F32)
    b2s = SBT("b2s", [128, 1], F32)
    st_ = SBT("stats", [128, 320], F32)
    row1 = SBT("row1", [1, 64], F32)
    sc1 = SBT("sc1", [128, 8], F32)
    S_SUMX, S_SUMSQ, S_MAX, S_MIN, S_M, S_R, S_QA, S_QB, S_T1, S_T2 = \
        0, 32, 64, 96, 128, 160, 192, 224, 256, 288

    ps = [es.enter_context(nc.psum_tensor("psb%d" % i, [128, 512], F32))
          for i in range(8)]
    sems = {s: es.enter_context(nc.semaphore("sem_" + s))
            for s in ("d", "g", "t", "v", "s", "p", "c")}

    b = KB()
    pool = PsumPool()
    bigv = big.ap()
    big32 = bigv.bitcast(F32)
    big8 = bigv.bitcast(I8)
    wb32 = wbuf.ap().bitcast(F32)
    sqv = big32[:, 10240:11264]
    ostv = big32[:, 12288:14336]
    xstv = big32[:, 14336:16384]
    qstv = big8[:, 40960:45056]
    qbv = bigv[:, 16384:20480]
    ZTC = 16384

    def tap(name, src, wait):
        if name in taps:
            b.dma("gpsimd", taps[name][:, :], src, wait=wait)

    g_ms = b.emit("gpsimd", lambda e: e.memset(row1[0:1, 0:64], 0.0), inc=("p",),
                  amt=1)
    g_tbl = 0
    for name, dst in (("M1c", m1c), ("M1s", m1s), ("ESA", esa), ("M4096", m4096),
                      ("IDN", idn), ("E4096", e4096), ("ONES", ones1),
                      ("IDNF", idnf)):
        sh = tb[name].shape
        g_tbl = b.dma("gpsimd", dst[0:sh[0], 0:sh[1]], tbl_in[name][:, :])
    g_posT = b.dma("gpsimd", cbuf[0:64, 0:4096], tbl_in["posT"][:, :])
    g_w1 = b.dma("gpsimd", cbuf[0:64, 4096:5120], w1_in[:, :])
    g_w2 = b.dma("gpsimd",
                 cbuf[:, 5120:6144].rearrange("p (rt c) -> p rt c", rt=8, c=CB),
                 w2_in.rearrange("(rt p) c -> p rt c", rt=8, p=128))
    g_b1 = b.dma("sync", b1s[:, 0:8], b1_in[:, :])
    g_fb0 = b.dma("sync", bias8[:, 0:8], fb_in[:, :])
    g_b2 = b.dma("sync", b2s[:, 0:1], b2_in[:, :])
    v_fb = b.emit("vector", lambda e: e.engine_nop(),
                  wait=(("d", g_fb0), ("p", g_ms)), inc=("v",))


    def bcast(src_cols, dst_cols, wait=()):
        a, bb = src_cols
        c, d = dst_cols
        n = bb - a
        tt = b.emit("tensor", lambda e: e.matmul(
            ps[7][:, 0:n], ones1[0:1, :], row1[0:1, a:bb], start=True, stop=True),
            wait=tuple(wait) + pool.wait_free(7) + (("g", g_tbl),), inc=("t",))
        pool.set_mm(7, tt)
        vt = b.emit("vector", lambda e: e.tensor_copy(
            sc1[:, c:d], ps[7][:, 0:n]), wait=(("t", tt),), inc=("v",))
        pool.set_evac(7, "v", vt)
        return vt

    v_abs = 0
    d_wh = 0
    for half in range(2):
        for kt in range(4):
            d_wh = b.dma("sync", wb32[:, kt * 1024:kt * 1024 + 1024],
                         wt_in[(half * 4 + kt) * 128:(half * 4 + kt + 1) * 128, :],
                         wait=(("v", v_abs),) if half == 1 and kt == 0 else ())
        for kt in range(4):
            v_abs = b.emit("vector", lambda e, h_=half, kt_=kt: e.tensor_reduce(
                st_[:, S_T1 + 4 * h_ + kt_:S_T1 + 4 * h_ + kt_ + 1],
                wb32[:, kt_ * 1024:kt_ * 1024 + 1024], axis=AX, op=ALU.add,
                apply_absolute_value=True),
                wait=(("d", d_wh),) if kt == 0 else (), inc=("v",))
    b.emit("vector", lambda e: e.tensor_reduce(
        st_[:, S_T2:S_T2 + 1], st_[:, S_T1:S_T1 + 8], axis=AX, op=ALU.add),
        wait=(("v", v_abs),), inc=("v",), fence=True)
    v_c = b.now("v")
    t_tp = b.emit("tensor", lambda e: e.matmul(
        ps[0][0:1, 0:128], st_[:, S_T2:S_T2 + 1], idnf[:, :], start=True,
        stop=True),
        wait=(("v", v_c), ("g", g_tbl)), inc=("t",))
    pool.set_mm(0, t_tp)
    v_sum = b.emit("vector", lambda e: e.tensor_reduce(
        row1[0:1, 0:1], ps[0][0:1, 0:128], axis=AX, op=ALU.add),
        wait=(("t", t_tp),), inc=("v",), fence=True)
    pool.set_evac(0, "v", v_sum)
    v_thr = b.emit("vector", lambda e: e.tensor_scalar(
        row1[0:1, 1:2], row1[0:1, 0:1], 0.5 / float(D * D), None, ALU.mult),
        inc=("v",))
    v_ws = b.emit("vector", lambda e: e.tensor_scalar(
        row1[0:1, 2:3], row1[0:1, 0:1], 1.0 / float(D * D), None, ALU.mult),
        inc=("v",))

    v_bc0 = bcast((1, 2), (0, 1), wait=(("v", v_thr),))
    q_tok = 0
    for half in range(2):
        d_wh = 0
        for kt in range(4):
            d_wh = b.dma("sync", wb32[:, kt * 1024:kt * 1024 + 1024],
                         wt_in[(half * 4 + kt) * 128:(half * 4 + kt + 1) * 128, :],
                         wait=(("v", q_tok if half else v_thr),))
        for kt in range(4):
            sl = slice(kt * 1024, kt * 1024 + 1024)
            qsl = slice((half * 4 + kt) * 1024, (half * 4 + kt) * 1024 + 1024)
            b.emit("vector", lambda e, sl_=sl: e.tensor_scalar(
                big32[:, 8192:9216], wb32[:, sl_], sc1[:, 0:1], None, ALU.is_gt),
                wait=((("d", d_wh),) if kt == 0 else ()) +
                     ((("v", v_bc0),) if half == 0 and kt == 0 else ()),
                inc=("v",))
            b.emit("vector", lambda e, sl_=sl: e.tensor_scalar(
                big32[:, 9216:10240], wb32[:, sl_], -1.0, sc1[:, 0:1], ALU.mult,
                ALU.is_gt), inc=("v",))
            q_tok = b.emit("vector", lambda e, q_=qsl: e.tensor_tensor(
                tbuf[:, q_], big32[:, 8192:9216], big32[:, 9216:10240],
                ALU.subtract), inc=("v",))
    d_qsp = b.dma("sync", qw_spill.ap()[:, :], tbuf[:, 0:8192],
                  wait=(("v", q_tok),))

    xin_t = x_in.rearrange("(j p) c -> j p c", p=128)
    cast_toks = [0] * NJ
    for j in range(NJ):
        h = (j % 2) * 1024
        d_x = b.dma("sync", xstv[:, h:h + 1024], xin_t[j, :, :],
                    wait=(("s", cast_toks[j - 2]),) if j >= 2 else ())
        cast_toks[j] = b.emit("scalar", lambda e, j_=j, h_=h: e.activation(
            x_tm[:, j_ * D:(j_ + 1) * D], xstv[:, h_:h_ + 1024], AF.Copy,
            accum_out=st_[:, S_SUMX + j_:S_SUMX + j_ + 1]),
            wait=(("d", d_x),), inc=("s",))
    s_xcast = cast_toks[-1]
    s_sq = v_mx = p_mn = 0
    for j in range(NJ):
        w0 = (("s", s_xcast),) if j == 0 else ()
        s_sq = b.emit("scalar", lambda e, j_=j: e.activation(
            sqv[:, 0:1024], x_tm[:, j_ * D:(j_ + 1) * D], AF.Square,
            accum_out=st_[:, S_SUMSQ + j_:S_SUMSQ + j_ + 1]), wait=w0, inc=("s",))
        v_mx = b.emit("vector", lambda e, j_=j: e.tensor_reduce(
            st_[:, S_MAX + j_:S_MAX + j_ + 1], x_tm[:, j_ * D:(j_ + 1) * D],
            axis=AX, op=ALU.max), wait=w0, inc=("v",), fence=True)
        p_mn = b.emit("vector", lambda e, j_=j: e.tensor_reduce(
            st_[:, S_MIN + j_:S_MIN + j_ + 1], x_tm[:, j_ * D:(j_ + 1) * D],
            axis=AX, op=ALU.min), wait=w0, inc=("v",), fence=True)

    def ln_finalize(wait):
        b.emit("vector", lambda e: e.tensor_scalar(
            st_[:, S_M:S_M + 32], st_[:, S_SUMX:S_SUMX + 32], 1.0 / D, None,
            ALU.mult), wait=wait, inc=("v",))
        b.emit("vector", lambda e: e.tensor_tensor(
            st_[:, S_T1:S_T1 + 32], st_[:, S_M:S_M + 32], st_[:, S_M:S_M + 32],
            ALU.mult), inc=("v",))
        b.emit("vector", lambda e: e.tensor_scalar(
            st_[:, S_T2:S_T2 + 32], st_[:, S_SUMSQ:S_SUMSQ + 32], 1.0 / D, None,
            ALU.mult), inc=("v",))
        b.emit("vector", lambda e: e.tensor_tensor(
            st_[:, S_T2:S_T2 + 32], st_[:, S_T2:S_T2 + 32], st_[:, S_T1:S_T1 + 32],
            ALU.subtract), inc=("v",))
        veps = b.emit("vector", lambda e: e.tensor_scalar(
            st_[:, S_T2:S_T2 + 32], st_[:, S_T2:S_T2 + 32], EPS, None, ALU.add),
            inc=("v",))
        vtok = b.emit("vector", lambda e: e.reciprocal(
            st_[:, S_T1:S_T1 + 32], st_[:, S_T2:S_T2 + 32]),
            wait=(("v", veps),), inc=("v",), fence=True)
        return b.emit("scalar", lambda e: e.activation(
            st_[:, S_R:S_R + 32], st_[:, S_T1:S_T1 + 32], AF.Sqrt),
            wait=(("v", vtok),), inc=("s",))

    def ln_localmax(wait):
        b.emit("vector", lambda e: e.tensor_tensor(
            st_[:, S_T1:S_T1 + 32], st_[:, S_MAX:S_MAX + 32], st_[:, S_M:S_M + 32],
            ALU.subtract), wait=wait, inc=("v",))
        b.emit("vector", lambda e: e.tensor_tensor(
            st_[:, S_T2:S_T2 + 32], st_[:, S_M:S_M + 32], st_[:, S_MIN:S_MIN + 32],
            ALU.subtract), inc=("v",))
        b.emit("vector", lambda e: e.tensor_tensor(
            st_[:, S_T1:S_T1 + 32], st_[:, S_T1:S_T1 + 32], st_[:, S_T2:S_T2 + 32],
            ALU.max), inc=("v",))
        vmm = b.emit("vector", lambda e: e.tensor_tensor(
            st_[:, S_T1:S_T1 + 32], st_[:, S_T1:S_T1 + 32], st_[:, S_R:S_R + 32],
            ALU.mult), inc=("v",))
        b.emit("vector", lambda e: e.tensor_reduce(
            st_[:, S_T2:S_T2 + 1], st_[:, S_T1:S_T1 + 32], axis=AX, op=ALU.max),
            wait=(("v", vmm),), inc=("v",), fence=True)
        vc = b.now("v")
        tt = b.emit("tensor", lambda e: e.matmul(
            ps[7][0:1, 0:128], st_[:, S_T2:S_T2 + 1], idnf[:, :], start=True,
            stop=True),
            wait=(("v", vc),) + pool.wait_free(7), inc=("t",))
        pool.set_mm(7, tt)
        vtok = b.emit("vector", lambda e: e.tensor_reduce(
            row1[0:1, 4:5], ps[7][0:1, 0:128], axis=AX, op=ALU.max),
            wait=(("t", tt),), inc=("v",), fence=True)
        pool.set_evac(7, "v", vtok)
        return vtok

    s_fin1 = ln_finalize((("s", s_sq), ("v", v_mx)))
    v_lm1 = ln_localmax((("v", p_mn), ("s", s_fin1)))
    g_cw = b.dma("gpsimd", cc1_in.ap()[0:1, 0:8], row1[0:1, 0:8],
                 wait=(("v", v_lm1), ("p", g_ms)))
    c_cc1 = b.emit("gpsimd", lambda e: e.collective_compute(
        "AllReduce", ALU.max, replica_groups=[list(range(NCORE))],
        ins=[cc1_in.ap()], outs=[cc1_out.ap()]),
        wait=(("g", g_cw),), inc=("c",))
    g_cr1 = b.dma("gpsimd", row1[0:1, 8:16], cc1_out.ap()[0:1, 0:8],
                  wait=(("c", c_cc1),))

    def ffn_scales(cc_read_tok):
        v16 = b.emit("vector", lambda e: e.tensor_scalar(
            row1[0:1, 16:17], row1[0:1, 12:13], 1.0 / 127.0, None, ALU.mult),
            wait=(("g", cc_read_tok),), inc=("v",))
        vrc = b.emit("vector", lambda e: e.reciprocal(
            row1[0:1, 17:18], row1[0:1, 16:17]), wait=(("v", v16),), inc=("v",), fence=True)
        bcast((17, 18), (1, 2), wait=(("v", vrc),))
        b.emit("vector", lambda e: e.tensor_scalar(
            st_[:, S_QA:S_QA + 32], st_[:, S_R:S_R + 32], sc1[:, 1:2], None,
            ALU.mult), inc=("v",))
        b.emit("vector", lambda e: e.tensor_tensor(
            st_[:, S_QB:S_QB + 32], st_[:, S_M:S_M + 32], st_[:, S_QA:S_QA + 32],
            ALU.mult), inc=("v",))
        b.emit("vector", lambda e: e.tensor_scalar(
            st_[:, S_QB:S_QB + 32], st_[:, S_QB:S_QB + 32], -1.0, None, ALU.mult),
            inc=("v",))
        b.emit("vector", lambda e: e.tensor_tensor(
            row1[0:1, 18:19], row1[0:1, 16:17], row1[0:1, 2:3], ALU.mult),
            wait=(("v", v_ws),), inc=("v",))
        return bcast((18, 19), (2, 3), wait=(("v", b.now("v")),))

    p_sc1 = ffn_scales(g_cr1)

    out_state = {"prev": [0, 0]}

    def ffn(gate_v, final):
        prev_cv = 0
        prev_qt_mm = 0
        prev_mm = 0
        prev_tb = 0
        for tc in range(8):
            jb = tc * 4
            vq = 0
            for jj in range(4):
                j = jb + jj
                vq = b.emit("vector", lambda e, j_=j, jj_=jj: e.tensor_scalar(
                    qstv[:, jj_ * D:(jj_ + 1) * D], x_tm[:, j_ * D:(j_ + 1) * D],
                    st_[:, S_QA + j_:S_QA + j_ + 1],
                    st_[:, S_QB + j_:S_QB + j_ + 1], ALU.mult, ALU.add),
                    wait=((("v", gate_v),) if tc == 0 and jj == 0 else ()) +
                         ((("s", prev_cv),) if jj == 0 else ()),
                    inc=("v",))
            s_cv = b.emit("scalar", lambda e: e.activation(
                qbv[:, 0:4 * D], qstv[:, 0:4 * D], AF.Copy),
                wait=(("v", vq), ("t", prev_qt_mm)), inc=("s",))
            prev_cv = s_cv
            for ct in range(8):
                bank = ct % 2
                for jj in range(4):
                    tok = b.emit("tensor", lambda e, ct_=ct, jj_=jj, bk_=bank:
                                 e.matmul(ps[bk_][:, jj_ * 128:jj_ * 128 + 128],
                                          qbv[:, jj_ * D + ct_ * 128:
                                              jj_ * D + ct_ * 128 + 128],
                                          idn[:, :], start=True, stop=True),
                                 wait=(pool.wait_free(bank) + (("s", s_cv),))
                                      if jj == 0 else (),
                                 inc=("t",))
                    pool.set_mm(bank, tok)
                vtok = b.emit("vector", lambda e, ct_=ct, bk_=bank: e.tensor_copy(
                    bigv[:, ct_ * 512:ct_ * 512 + 512], ps[bk_][:, 0:512]),
                    wait=pool.wait_mm(bank) +
                         ((("t", prev_mm),) if ct == 0 else ()),
                    inc=("v",))
                pool.set_evac(bank, "v", vtok)
            prev_qt_mm = pool.mm[1]
            v_qt = b.now("v")
            for ot in range(8):
                bank = 2 + ot % 2
                for kt in range(8):
                    tok = b.emit("tensor", lambda e, ot_=ot, kt_=kt, bk_=bank:
                                 e.matmul(ps[bk_][:, 0:512],
                                          tbuf[:, kt_ * 1024 + ot_ * 128:
                                               kt_ * 1024 + ot_ * 128 + 128],
                                          bigv[:, kt_ * 512:kt_ * 512 + 512],
                                          start=(kt_ == 0), stop=(kt_ == 7)),
                                 wait=(pool.wait_free(bank) + (("v", v_qt),))
                                      if kt == 0 else (),
                                 inc=("t",))
                    pool.set_mm(bank, tok)
                stok = b.emit("scalar", lambda e, ot_=ot, bk_=bank: e.activation(
                    bigv[:, 4096 + ot_ * 512:4096 + ot_ * 512 + 512],
                    ps[bk_][:, 0:512], AF.Identity, scale=sc1[:, 2:3],
                    bias=bias8[:, ot_:ot_ + 1]),
                    wait=pool.wait_mm(bank) +
                         ((("t", prev_tb),) if ot == 0 else ()),
                    inc=("s",))
                pool.set_evac(bank, "s", stok)
            prev_mm = max(pool.mm[2], pool.mm[3])
            s_fo = b.now("s")
            for jj in range(4):
                j = jb + jj
                slot = (j % 2) * D
                vtok = 0
                for oth in range(2):
                    bank = 4 + oth
                    for o4 in range(4):
                        ot = oth * 4 + o4
                        tok = b.emit("tensor",
                                     lambda e, jj_=jj, ot_=ot, o4_=o4, bk_=bank:
                                     e.matmul(
                                         ps[bk_][:, o4_ * 128:o4_ * 128 + 128],
                                         bigv[:, 4096 + ot_ * 512 + jj_ * 128:
                                              4096 + ot_ * 512 + jj_ * 128 + 128],
                                         idn[:, :], start=True, stop=True),
                                     wait=(pool.wait_free(bank) + (("s", s_fo),))
                                          if o4 == 0 else (),
                                     inc=("t",))
                        pool.set_mm(bank, tok)
                    if not final:
                        vtok = b.emit("vector",
                                      lambda e, j_=j, o_=oth, bk_=bank:
                                      e.scalar_tensor_tensor(
                                          x_tm[:, j_ * D + o_ * 512:
                                               j_ * D + o_ * 512 + 512],
                                          ps[bk_][:, 0:512], 0.5,
                                          x_tm[:, j_ * D + o_ * 512:
                                               j_ * D + o_ * 512 + 512],
                                          ALU.mult, ALU.add),
                                      wait=pool.wait_mm(bank), inc=("v",))
                    else:
                        prev = out_state["prev"][j % 2]
                        vtok = b.emit("vector",
                                      lambda e, j_=j, o_=oth, bk_=bank, sl_=slot:
                                      e.scalar_tensor_tensor(
                                          ostv[:, sl_ + o_ * 512:
                                               sl_ + o_ * 512 + 512],
                                          ps[bk_][:, 0:512], 0.5,
                                          x_tm[:, j_ * D + o_ * 512:
                                               j_ * D + o_ * 512 + 512],
                                          ALU.mult, ALU.add),
                                      wait=pool.wait_mm(bank) +
                                           ((("d", prev),) if oth == 0 and prev
                                            else ()),
                                      inc=("v",))
                    pool.set_evac(bank, "v", vtok)
                if final:
                    od = b.dma("sync", y_out[j * 128:(j + 1) * 128, :],
                               ostv[:, slot:slot + D], wait=(("v", vtok),))
                    out_state["prev"][j % 2] = od
            prev_tb = max(pool.mm[4], pool.mm[5])

    ffn(p_sc1, final=False)
    b.barrier()

    if BUILD_PHASES < 2:
        if DUMP == "q":
            yv = y_out.rearrange("(a p) c -> a p c", p=128)
            b.dma("gpsimd", yv[0:4, :, :].rearrange("a p c -> p a c"),
                  qbv[:, 0:4096],
                  wait=tuple((s, b.now(s)) for s in ("t", "v", "s")))
            b.dma("gpsimd", yv[4:8, :, :].rearrange("a p c -> p a c"),
                  bigv[:, 0:4096])
            b.dma("gpsimd", yv[8:12, :, :].rearrange("a p c -> p a c"),
                  bigv[:, 4096:8192])
        if DUMP == "qw":
            b.dma("gpsimd",
                  y_out.rearrange("(a p) c -> a p c", p=128)[0:8, :, :]
                       .rearrange("a p c -> p a c"),
                  tbuf.ap().rearrange("p (a c) -> p a c", a=8, c=1024),
                  wait=tuple((s, b.now(s)) for s in ("t", "v", "s")))
        if DUMP == "stats":
            b.dma("gpsimd", y_out[0:128, 0:320], st_[:, 0:320],
                  wait=tuple((s, b.now(s)) for s in ("t", "v", "s")))
            b.dma("gpsimd", y_out[128:256, 0:3], sc1[:, 0:3])
            b.dma("gpsimd", y_out[256:257, 0:64], row1[0:1, 0:64])
        if DUMP == "":
            w0 = tuple((s, b.now(s)) for s in ("t", "v", "s"))
            for j in range(NJ):
                b.dma("gpsimd", y_out[j * 128:(j + 1) * 128, :],
                      x_tm[:, j * D:(j + 1) * D], wait=w0 if j == 0 else ())
        fin = tuple((s, b.now(s)) for s in ("d", "g", "t", "v", "s", "p", "c"))
        b.emit("gpsimd", lambda e: e.nop(), wait=fin)
        with nc.Block() as block:
            b.replay(block, sems)
        es.close()
        return nc

    def fwd_fft(m2t, p_off, m_off, wait_mid):
        for u in range(32):
            bank = 4 + (u // 4) % 2
            tok = b.emit("tensor", lambda e, u_=u, bk_=bank: e.matmul(
                ps[bk_][:, (u_ % 4) * 128:(u_ % 4) * 128 + 128],
                dbuf[:, u_ * 128:u_ * 128 + 128], idn[:, :],
                start=True, stop=True),
                wait=((pool.wait_free(bank) if u % 4 == 0 else ()) +
                      (wait_mid if u == 0 else ())),
                inc=("t",))
            pool.set_mm(bank, tok)
            if u % 4 == 3:
                vtok = b.emit("vector", lambda e, u_=u, bk_=bank: e.tensor_copy(
                    dbuf[:, 4096 + (u_ - 3) * 128:4096 + (u_ + 1) * 128],
                    ps[bk_][:, 0:512]),
                    wait=pool.wait_mm(bank), inc=("v",))
                pool.set_evac(bank, "v", vtok)
        v_xnf = b.now("v")
        zview = big[:, 0:16384].rearrange("p (c ri t2h par) -> p ri par t2h c",
                                          c=128, ri=2, t2h=32, par=2)
        for u in range(8):
            for (bank, r0, mt) in ((0, 0, m1c), (1, 64, m1c),
                                   (2, 0, m1s), (3, 64, m1s)):
                tok = b.emit("tensor",
                             lambda e, u_=u, bk_=bank, r_=r0, m_=mt: e.matmul(
                                 ps[bk_][0:65, 0:512], m_[r_:r_ + 64, :],
                                 dbuf[r_:r_ + 64,
                                      4096 + u_ * 512:4096 + u_ * 512 + 512],
                                 start=True, stop=True, tile_position=(r_, 0)),
                             wait=(pool.wait_free(bank) +
                                   ((("v", v_xnf),) if u == 0 else ())),
                             inc=("t",))
                pool.set_mm(bank, tok)
            for (bank, ri, par) in ((0, 0, 0), (1, 0, 1), (2, 1, 0), (3, 1, 1)):
                vtok = b.emit("vector",
                              lambda e, u_=u, bk_=bank, ri_=ri, pr_=par:
                              e.tensor_copy(
                                  zview[0:65, ri_, pr_, 4 * u_:4 * u_ + 4, :],
                                  ps[bk_].ap().rearrange(
                                      "p (a c) -> p a c", a=4, c=128)[0:65]),
                              wait=pool.wait_mm(bank), inc=("v",))
                pool.set_evac(bank, "v", vtok)
        v_z = b.now("v")
        for c in range(128):
            bank = (c // 4) % 2
            tok = b.emit("tensor", lambda e, c_=c, bk_=bank: e.matmul(
                ps[bk_][0:128, (c_ % 4) * 65:(c_ % 4) * 65 + 65],
                bigv[0:65, c_ * 128:c_ * 128 + 128], idn[0:65, 0:65],
                start=True, stop=True),
                wait=((pool.wait_free(bank) if c % 4 == 0 else ()) +
                      ((("v", v_z),) if c == 0 else ())),
                inc=("t",))
            pool.set_mm(bank, tok)
            if c % 4 == 3:
                vtok = b.emit("vector", lambda e, c_=c, bk_=bank: e.tensor_copy(
                    bigv[:, ZTC + (c_ - 3) * 65:ZTC + (c_ + 1) * 65],
                    ps[bk_][:, 0:260]),
                    wait=pool.wait_mm(bank), inc=("v",))
                pool.set_evac(bank, "v", vtok)
        v_zt = b.now("v")
        ztv = big[:, ZTC:ZTC + 8320].rearrange("p (cc g) -> p g cc",
                                               cc=128, g=65)

        def zt_rhs(g):
            return ztv[:, g, :]
        for g in range(65):
            bank = 6 + g % 2
            wmm = pool.wait_free(bank) + ((("v", v_zt),) if g <= 1 else ())
            if g < 64:
                tok = b.emit("tensor", lambda e, g_=g, bk_=bank: e.matmul(
                    ps[bk_][0:64, 0:128],
                    m2t[:, p_off + g_ * 64:p_off + g_ * 64 + 64],
                    zt_rhs(g_), start=True, stop=True),
                    wait=wmm, inc=("t",))
            else:
                tok = b.emit("tensor", lambda e, g_=g, bk_=bank: e.matmul(
                    ps[bk_][64:128, 0:128],
                    m2t[:, p_off + g_ * 64:p_off + g_ * 64 + 64],
                    zt_rhs(g_), start=True, stop=True, tile_position=(0, 64)),
                    wait=wmm, inc=("t",))
            pool.set_mm(bank, tok)
            if 1 <= g <= 63:
                tok = b.emit("tensor", lambda e, g_=g, bk_=bank: e.matmul(
                    ps[bk_][64:128, 0:128],
                    m2t[:, m_off + (g_ - 1) * 64:m_off + (g_ - 1) * 64 + 64],
                    zt_rhs(g_), start=True, stop=True, tile_position=(0, 64)),
                    inc=("t",))
                pool.set_mm(bank, tok)
            if g == 0:
                tok = b.emit("tensor", lambda e: e.matmul(
                    ps[5][0:2, 0:128], m4096[:, 0:2], zt_rhs(0),
                    start=True, stop=True),
                    wait=pool.wait_free(5), inc=("t",))
                pool.set_mm(5, tok)
                vtok5 = b.emit("vector", lambda e: e.tensor_copy(
                    s4096[0:2, :], ps[5][0:2, 0:128]),
                    wait=pool.wait_mm(5), inc=("v",))
                pool.set_evac(5, "v", vtok5)
            if g < 64:
                vtok = b.emit("vector", lambda e, g_=g, bk_=bank: e.tensor_copy(
                    cbuf[0:64, g_ * 128:g_ * 128 + 128], ps[bk_][0:64, 0:128]),
                    wait=pool.wait_mm(bank), inc=("v",))
            else:
                vtok = b.emit("vector", lambda e, bk_=bank: e.tensor_copy(
                    cbuf[64:128, 0:128], ps[bk_][64:128, 0:128]),
                    wait=pool.wait_mm(bank), inc=("v",))
            if 1 <= g <= 63:
                vtok = b.emit("vector", lambda e, g_=g, bk_=bank: e.tensor_copy(
                    cbuf[64:128, (64 - g_) * 128:(64 - g_) * 128 + 128],
                    ps[bk_][64:128, 0:128]), inc=("v",))
            pool.set_evac(bank, "v", vtok)
        return b.now("v")

    for rt in range(8):
        for u in range(8):
            bank = u % 2
            tok = b.emit("tensor", lambda e, rt_=rt, u_=u, bk_=bank: e.matmul(
                ps[bk_][:, 0:512],
                cbuf[0:64, 4096 + rt_ * 128:4096 + rt_ * 128 + 128],
                cbuf[0:64, u_ * 512:u_ * 512 + 512], start=True, stop=True),
                wait=(pool.wait_free(bank) +
                      (((("g", g_posT), ("g", g_w1), ("d", g_b1)))
                       if rt == 0 and u == 0 else ())),
                inc=("t",))
            pool.set_mm(bank, tok)
            stok = b.emit("scalar", lambda e, rt_=rt, u_=u, bk_=bank: e.activation(
                bigv[:, rt_ * 4096 + u_ * 512:rt_ * 4096 + u_ * 512 + 512],
                ps[bk_][:, 0:512], AF.Gelu, bias=b1s[:, rt_:rt_ + 1]),
                wait=pool.wait_mm(bank), inc=("s",))
            pool.set_evac(bank, "s", stok)
    for u in range(8):
        bank = 2 + u % 2
        for rt in range(8):
            tok = b.emit("tensor", lambda e, rt_=rt, u_=u, bk_=bank: e.matmul(
                ps[bk_][:, 0:512], cbuf[:, 5120 + rt_ * 128:5120 + rt_ * 128 + 128],
                bigv[:, rt_ * 4096 + u_ * 512:rt_ * 4096 + u_ * 512 + 512],
                start=(rt_ == 0), stop=(rt_ == 7)),
                wait=((pool.wait_free(bank) +
                       ((("g", g_w2), ("d", g_b2), ("s", b.now("s")))
                        if u == 0 else ())) if rt == 0 else ()),
                inc=("t",))
            pool.set_mm(bank, tok)
        stok = b.emit("scalar", lambda e, u_=u, bk_=bank: e.activation(
            dbuf[:, 0:4096].rearrange("p (t2 t1) -> p t2 t1", t2=64, t1=64)
                [:, :, 8 * u_:8 * u_ + 8],
            ps[bk_].ap().rearrange("p (a t2) -> p t2 a", a=8, t2=64),
            AF.Identity, bias=b2s[:, 0:1]),
            wait=pool.wait_mm(bank), inc=("s",))
        pool.set_evac(bank, "s", stok)
    s_fmid = b.now("s")
    b.barrier()
    g_m2p1 = b.dma("gpsimd", bigv[:, 24576:24576 + 4160], tbl_in["M2P"][:, :])
    g_m2m1 = b.dma("gpsimd", bigv[:, 28736:28736 + 4032], tbl_in["M2M"][:, :])
    b.ops["gpsimd"] = b.ops["gpsimd"][:-2]
    b.cnt["g"] -= 32
    g_m2p1 = b.dma("gpsimd", tbuf[:, 0:4160], tbl_in["M2P"][:, :],
                   wait=(("d", d_qsp),))
    g_m2m1 = b.dma("gpsimd", tbuf[:, 4160:4160 + 4032], tbl_in["M2M"][:, :])
    if DUMP == "fmid":
        b.dma("gpsimd", y_out[1024:1536, :].rearrange("(a p) c -> a p c", a=4, p=128)
              .rearrange("a p c -> p a c"),
              dbuf[:, 0:4096].rearrange("p (a c) -> p a c", a=4, c=1024),
              wait=tuple((s, b.now(s)) for s in ("t", "v", "s")))
        b.barrier()
    v_s4f = fwd_fft(tbuf, 0, 4160,
                    (("s", s_fmid), ("g", g_m2m1), ("g", g_m2p1)))
    if DUMP == "fmid":
        b.dma("gpsimd", y_out[0:1024, :].rearrange("(a p) c -> a p c", a=8, p=128)
              .rearrange("a p c -> p a c"),
              cbuf.ap().rearrange("p (a c) -> p a c", a=8, c=1024),
              wait=tuple((s, b.now(s)) for s in ("t", "v", "s")))
        b.dma("gpsimd", y_out[1536:2048, :].rearrange("(a p) c -> a p c", a=4, p=128)
              .rearrange("a p c -> p a c"),
              dbuf[:, 4096:8192].rearrange("p (a c) -> p a c", a=4, c=1024))
        b.barrier()
    v_h0a = b.emit("vector", lambda e: e.tensor_scalar(
        cbuf[0:1, 0:128], cbuf[0:1, 0:128], 0.5, None, ALU.mult),
        wait=(("v", v_s4f),), inc=("v",))
    v_h0b = b.emit("vector", lambda e: e.tensor_scalar(
        cbuf[32:33, 0:128], cbuf[32:33, 0:128], 0.5, None, ALU.mult), inc=("v",))
    v_ng1 = b.emit("vector", lambda e: e.tensor_scalar(
        bigv[32:64, 0:8192], cbuf[32:64, 0:8192], -1.0, None, ALU.mult),
        wait=(("v", v_h0b),), inc=("v",))
    v_ng2 = b.emit("vector", lambda e: e.tensor_scalar(
        bigv[96:128, 0:8192], cbuf[96:128, 0:8192], -1.0, None, ALU.mult),
        inc=("v",))
    fbv = fbmy.ap().rearrange("(two p) f -> two p f", two=2, p=128)
    g_bl = 0
    for (rd, rs, two, neg) in (
            ((0, 32), (0, 32), 0, False), ((32, 64), (0, 32), 0, False),
            ((64, 96), (64, 96), 0, False), ((96, 128), (64, 96), 0, False),
            ((0, 32), (32, 64), 1, True), ((32, 64), (32, 64), 1, False),
            ((64, 96), (96, 128), 1, True), ((96, 128), (96, 128), 1, False)):
        src = (bigv if neg else cbuf.ap())[rs[0]:rs[1], 0:8192]
        g_bl = b.dma("gpsimd", fbv[two, rd[0]:rd[1], :], src,
                     wait=(("v", v_ng2),))
    c_ag = b.emit("gpsimd", lambda e: e.collective_compute(
        "AllGather", ALU.bypass, replica_groups=[list(range(NCORE))],
        ins=[fbmy.ap()], outs=[fball.ap()]),
        wait=(("g", g_bl),), inc=("c",))
    b.barrier()

    s_sq2 = 0
    for j in range(NJ):
        b.emit("scalar", lambda e, j_=j: e.activation(
            sqv[:, 0:1024], x_tm[:, j_ * D:(j_ + 1) * D], AF.Copy,
            accum_out=st_[:, S_SUMX + j_:S_SUMX + j_ + 1]), inc=("s",))
        s_sq2 = b.emit("scalar", lambda e, j_=j: e.activation(
            sqv[:, 0:1024], x_tm[:, j_ * D:(j_ + 1) * D], AF.Square,
            accum_out=st_[:, S_SUMSQ + j_:S_SUMSQ + j_ + 1]), inc=("s",))
    s_fin2 = ln_finalize((("s", s_sq2),))
    b.emit("vector", lambda e: e.tensor_copy(
        st_[:, S_QA:S_QA + 32], st_[:, S_R:S_R + 32]),
        wait=(("s", s_fin2),), inc=("v",))
    b.emit("vector", lambda e: e.tensor_tensor(
        st_[:, S_QB:S_QB + 32], st_[:, S_M:S_M + 32], st_[:, S_R:S_R + 32],
        ALU.mult), inc=("v",))
    v_cs2 = b.emit("vector", lambda e: e.tensor_scalar(
        st_[:, S_QB:S_QB + 32], st_[:, S_QB:S_QB + 32], -1.0, None, ALU.mult),
        inc=("v",))
    b.barrier()

    midv = dbuf[:, 0:4096].rearrange("p (t2 jf tlh) -> p jf tlh t2",
                                     t2=64, jf=32, tlh=2)
    ymcv = dbuf[:, 4096:8192].rearrange("p (t1p t2h par) -> p t2h par t1p",
                                        t1p=64, t2h=32, par=2)
    x_tmv = x_tm.ap().rearrange("p (jf c) -> p jf c", jf=32, c=1024)
    vAview = big[:, 0:16384].rearrange("p (c k) -> p k c", c=128, k=128)
    vTview = big[:, ZTC:ZTC + 16384].rearrange("p (pl t2p c) -> p pl t2p c",
                                           pl=2, t2p=64, c=128)
    fallv = fball.ap().rearrange("(n two p) f -> n two p f",
                                 n=NCORE, two=2, p=128)

    def cdump(row0, srcap, nrows=512):
        w0 = tuple((s, b.now(s)) for s in ("t", "v", "s"))
        for a in range(nrows // 128):
            b.dma("gpsimd", y_out[row0 + a * 128:row0 + (a + 1) * 128, :],
                  srcap[:, a, :], wait=w0 if a == 0 else ())
        b.barrier()

    for bl in (CONV_ORDER or range(NB)):
        if DEBUG_MEMSET:
            mtoks = []
            for buf, cols in ((big, 32768), (cbuf, 8192), (dbuf, 8192)):
                if DEBUG_MEMSET == 1 or (DEBUG_MEMSET == 2 and buf is big) or \
                   (DEBUG_MEMSET == 3 and buf is cbuf) or \
                   (DEBUG_MEMSET == 4 and buf is dbuf):
                    mt = b.emit("gpsimd", lambda e, b_=buf, c_=cols: e.memset(
                        b_[:, 0:c_], 0.0), inc=("p",))
                    mtoks.append(mt)
            if mtoks:
                b.emit("vector", lambda e: e.engine_nop(),
                       wait=(("p", mtoks[-1]),), inc=("v",))
                b.barrier()
        g_m2p = b.dma("gpsimd", tbuf[:, 0:4160], tbl_in["M2P"][:, :],
                      wait=(("t", b.now("t")),))
        g_m2m = b.dma("gpsimd", tbuf[:, 4160:4160 + 4032], tbl_in["M2M"][:, :])
        tr_toks = [0, 0]
        for j in range(NJ):
            bank = (j // 4) % 2
            slot = (j % 2) * 128
            if j % 2 == 0:
                ntok = b.emit("scalar", lambda e, j_=j, sl_=slot, bl_=bl:
                              e.activation(
                    nst[:, sl_:sl_ + 128],
                    x_tm[:, j_ * D + bl_ * CB:j_ * D + bl_ * CB + 128],
                    AF.Identity, scale=st_[:, S_QA + j_:S_QA + j_ + 1],
                    bias=st_[:, S_QB + j_:S_QB + j_ + 1]),
                    wait=(("t", tr_toks[0]),), inc=("s",))
                wsem = ("s", ntok)
            else:
                ntok = b.emit("vector", lambda e, j_=j, sl_=slot, bl_=bl:
                              e.tensor_scalar(
                                  nst[:, sl_:sl_ + 128],
                                  x_tm[:, j_ * D + bl_ * CB:
                                       j_ * D + bl_ * CB + 128],
                                  st_[:, S_QA + j_:S_QA + j_ + 1],
                                  st_[:, S_QB + j_:S_QB + j_ + 1],
                                  ALU.mult, ALU.add),
                              wait=(("t", tr_toks[1]),), inc=("v",))
                wsem = ("v", ntok)
            tok = b.emit("tensor", lambda e, j_=j, sl_=slot, bk_=bank: e.matmul(
                ps[bk_][:, (j_ % 4) * 128:(j_ % 4) * 128 + 128],
                nst[:, sl_:sl_ + 128], idn[:, :], start=True, stop=True),
                wait=(wsem,) + (pool.wait_free(bank) if j % 4 == 0 else ()),
                inc=("t",))
            pool.set_mm(bank, tok)
            tr_toks[j % 2] = tok
            if j % 4 == 3:
                vtok = b.emit("vector", lambda e, j_=j, bk_=bank: e.tensor_copy(
                    midv[:, j_ - 3:j_ + 1, :, :],
                    ps[bk_].ap().rearrange("p (jj tlh t2) -> p jj tlh t2",
                                           jj=4, tlh=2, t2=64)),
                    wait=pool.wait_mm(bank), inc=("v",))
                pool.set_evac(bank, "v", vtok)
        v_mid = b.now("v")
        if DUMP == "fb" and bl == 0:
            w0 = tuple((s, b.now(s)) for s in ("t", "v", "s", "c"))
            fv = fball.ap().rearrange("(n two p) f -> n two p f",
                                      n=NCORE, two=2, p=128)
            for k, slot in enumerate((6, 7)):
                for a in range(8):
                    b.dma("gpsimd", y_out[k * 1024 + a * 128:
                                          k * 1024 + (a + 1) * 128, :],
                          fv[slot, 0, :, a * 1024:(a + 1) * 1024],
                          wait=w0 if (k == 0 and a == 0) else ())
            b.barrier()
        if False and DUMP == "fb" and bl == 0:
            for two in range(2):
                b.dma("gpsimd",
                      y_out[two * 1024:(two + 1) * 1024, :]
                      .rearrange("(a p) c -> a p c", a=8, p=128)
                      .rearrange("a p c -> p a c"),
                      fbmy.ap().rearrange("(two p) f -> two p f", two=2, p=128)
                      [two, :, :].rearrange("p (a c) -> p a c", a=8, c=1024),
                      wait=tuple((s, b.now(s)) for s in ("t", "v", "s", "c")))
                b.dma("gpsimd",
                      y_out[2048 + two * 1024:2048 + (two + 1) * 1024, :]
                      .rearrange("(a p) c -> a p c", a=8, p=128)
                      .rearrange("a p c -> p a c"),
                      fball.ap().rearrange("(n two p) f -> n two p f",
                                           n=NCORE, two=2, p=128)
                      [0, two, :, :].rearrange("p (a c) -> p a c", a=8, c=1024))
            b.barrier()
        if DUMP == "conv" and bl == 0:
            cdump(0, dbuf[:, 0:4096].rearrange("p (a c) -> p a c", a=4, c=1024))
        v_s4 = fwd_fft(tbuf, 0, 4160, (("v", v_mid), ("g", g_m2m)))
        if DUMP == "conv" and bl == 0:
            cdump(512, dbuf[:, 4096:8192].rearrange("p (a c) -> p a c", a=4, c=1024))
            cdump(1024, cbuf.ap().rearrange("p (a c) -> p a c", a=8, c=1024),
                  nrows=1024)
        b.barrier()

        for hf in range(2):
            co = hf * 4096
            fo = (hf % 2) * 0
            g_fre = b.dma("sync", wbuf[:, 0:4096],
                          fallv[bl, 0, :, co:co + 4096],
                          wait=(("c", c_ag),) if bl == 0 and hf == 0 else
                               (("v", b.now("v")),))
            g_fim = b.dma("sync", wbuf[:, 4096:8192],
                          fallv[bl, 1, :, co:co + 4096])
            g_s1 = b.dma("sync", sshb.ap()[0:32, co:co + 4096],
                         cbuf[32:64, co:co + 4096],
                         wait=(("v", v_s4),))
            g_s2 = b.dma("sync", sshb.ap()[32:64, co:co + 4096],
                         cbuf[0:32, co:co + 4096])
            g_s3 = b.dma("sync", sshb.ap()[64:96, co:co + 4096],
                         cbuf[96:128, co:co + 4096])
            g_s4 = b.dma("sync", sshb.ap()[96:128, co:co + 4096],
                         cbuf[64:96, co:co + 4096])
            g_s5 = b.dma("sync", dbuf[:, co:co + 4096],
                         sshb.ap()[:, co:co + 4096],
                         wait=(("d", g_s4),))
            v_w1 = b.emit("vector", lambda e, co_=co: e.tensor_tensor(
                cbuf[:, co_:co_ + 4096], cbuf[:, co_:co_ + 4096],
                wbuf[:, 0:4096], ALU.mult),
                wait=(("d", g_fim), ("d", g_s5)), inc=("v",))
            v_w2 = b.emit("vector", lambda e, co_=co: e.tensor_tensor(
                bigv[:, ZTC:ZTC + 4096], dbuf[:, co_:co_ + 4096],
                wbuf[:, 4096:8192], ALU.mult), inc=("v",))
            v_w3 = b.emit("vector", lambda e, co_=co: e.tensor_tensor(
                cbuf[:, co_:co_ + 4096], cbuf[:, co_:co_ + 4096],
                bigv[:, ZTC:ZTC + 4096], ALU.add), inc=("v",))
        v_pw = b.now("v")
        b.barrier()
        if DUMP == "conv" and bl == 0:
            cdump(2048, cbuf.ap().rearrange("p (a c) -> p a c", a=8, c=1024),
                  nrows=1024)

        for gh in range(64):
            bkA = (gh // 4) % 2
            bankA = 0 + 2 * bkA
            bankB = 1 + 2 * bkA
            cA = (gh % 4) * 128
            if gh == 0:
                tok = b.emit("tensor", lambda e: e.matmul(
                    ps[0][:, 0:128], esa[0:64, :], cbuf[0:64, 0:128],
                    start=True, stop=False, tile_position=(0, 0)),
                    wait=pool.wait_free(0) + pool.wait_free(5) + (("v", v_pw),),
                    inc=("t",))
                tok = b.emit("tensor", lambda e: e.matmul(
                    ps[0][:, 0:128], e4096[0:2, :], s4096[0:2, :],
                    start=False, stop=True, tile_position=(0, 0)), inc=("t",))
                pool.set_mm(0, tok)
            else:
                tok = b.emit("tensor", lambda e, g_=gh, c_=cA, bk_=bankA: e.matmul(
                    ps[bk_][:, c_:c_ + 128], esa[0:64, :],
                    cbuf[0:64, g_ * 128:g_ * 128 + 128],
                    start=True, stop=True, tile_position=(0, 0)),
                    wait=(pool.wait_free(bankA) if gh % 4 == 0 else ()) +
                         ((("v", v_pw),) if gh == 1 else ()),
                    inc=("t",))
                pool.set_mm(bankA, tok)
            tok = b.emit("tensor", lambda e, g_=gh, c_=cA, bk_=bankB: e.matmul(
                ps[bk_][:, c_:c_ + 128], esa[64:128, :],
                cbuf[64:128, g_ * 128:g_ * 128 + 128],
                start=True, stop=True, tile_position=(64, 0)),
                wait=(pool.wait_free(bankB) if gh % 4 == 0 else ()) +
                     ((("v", v_pw),) if gh == 0 else ()),
                inc=("t",))
            pool.set_mm(bankB, tok)
            if gh % 4 == 3:
                g0 = gh - 3
                vtok = b.emit("vector", lambda e, g0_=g0, bk_=bankA: e.tensor_copy(
                    vAview[:, g0_:g0_ + 4, :],
                    ps[bk_].ap().rearrange("p (a c) -> p a c", a=4, c=128)),
                    wait=pool.wait_mm(bankA), inc=("v",))
                pool.set_evac(bankA, "v", vtok)
                vtok = b.emit("vector", lambda e, g0_=g0, bk_=bankB: e.tensor_copy(
                    vAview[:, 64 + g0_:64 + g0_ + 4, :],
                    ps[bk_].ap().rearrange("p (a c) -> p a c", a=4, c=128)),
                    wait=pool.wait_mm(bankB), inc=("v",))
                pool.set_evac(bankB, "v", vtok)
        v_sa = b.now("v")
        if DUMP == "conv" and bl == 0:
            cdump(3072, big[:, 0:4096].rearrange("p (a c) -> p a c", a=4, c=1024))
        g_mbr = b.dma("gpsimd", tbuf[:, 0:4096], tbl_in["MBre"][:, :],
                      wait=(("t", b.now("t")),))
        g_mbi = b.dma("gpsimd", tbuf[:, 4096:8192], tbl_in["MBim"][:, :])

        for c in range(128):
            bank = 4 + (c // 4) % 2
            tok = b.emit("tensor", lambda e, c_=c, bk_=bank: e.matmul(
                ps[bk_][:, (c_ % 4) * 128:(c_ % 4) * 128 + 128],
                bigv[:, c_ * 128:c_ * 128 + 128], idn[:, :],
                start=True, stop=True),
                wait=(pool.wait_free(bank) if c % 4 == 0 else ()) +
                     ((("v", v_sa),) if c == 0 else ()),
                inc=("t",))
            pool.set_mm(bank, tok)
            if c % 4 == 3:
                c0 = c - 3
                vtok = b.emit("vector", lambda e, c0_=c0, bk_=bank: e.tensor_copy(
                    vTview[:, :, :, c0_:c0_ + 4],
                    ps[bk_].ap().rearrange("p (a pl t2p) -> p pl t2p a",
                                           a=4, pl=2, t2p=64)),
                    wait=pool.wait_mm(bank), inc=("v",))
                pool.set_evac(bank, "v", vtok)
        v_vt = b.now("v")

        for u in range(32):
            bank = 6 + (u // 4) % 2
            for pr in range(2):
                t2p = 2 * u + pr
                r0 = 64 * pr
                tok = b.emit("tensor", lambda e, t_=t2p, r_=r0, u_=u, bk_=bank:
                             e.matmul(
                                 ps[bk_][r_:r_ + 64, (u_ % 4) * 128:
                                         (u_ % 4) * 128 + 128],
                                 tbuf[:, t_ * 64:t_ * 64 + 64],
                                 bigv[:, ZTC + 0 * 8192 + t_ * 128:
                                      ZTC + t_ * 128 + 128],
                                 start=True, stop=False, tile_position=(0, r_)),
                             wait=(pool.wait_free(bank) if u % 4 == 0 and pr == 0
                                   else ()) +
                                  ((("v", v_vt), ("g", g_mbi)) if u == 0 and
                                   pr == 0 else ()),
                             inc=("t",))
                tok = b.emit("tensor", lambda e, t_=t2p, r_=r0, u_=u, bk_=bank:
                             e.matmul(
                                 ps[bk_][r_:r_ + 64, (u_ % 4) * 128:
                                         (u_ % 4) * 128 + 128],
                                 tbuf[:, 4096 + t_ * 64:4096 + t_ * 64 + 64],
                                 bigv[:, ZTC + 8192 + t_ * 128:
                                      ZTC + 8192 + t_ * 128 + 128],
                                 start=False, stop=True, tile_position=(0, r_)),
                             inc=("t",))
                pool.set_mm(bank, tok)
            if u % 4 == 3:
                u0 = u - 3
                vtok = b.emit("vector", lambda e, u0_=u0, bk_=bank: e.tensor_copy(
                    dbuf[:, u0_ * 128:u0_ * 128 + 512], ps[bk_][:, 0:512]),
                    wait=pool.wait_mm(bank), inc=("v",))
                pool.set_evac(bank, "v", vtok)
        v_ym = b.now("v")
        if DUMP == "conv" and bl == 0:
            cdump(3584, dbuf[:, 0:4096].rearrange("p (a c) -> p a c", a=4, c=1024))

        for u in range(32):
            bank = (u // 4) % 2
            tok = b.emit("tensor", lambda e, u_=u, bk_=bank: e.matmul(
                ps[bk_][:, (u_ % 4) * 128:(u_ % 4) * 128 + 128],
                dbuf[:, u_ * 128:u_ * 128 + 128], idn[:, :],
                start=True, stop=True),
                wait=(pool.wait_free(bank) if u % 4 == 0 else ()) +
                     ((("v", v_ym),) if u == 0 else ()),
                inc=("t",))
            pool.set_mm(bank, tok)
            if u % 4 == 3:
                u0 = u - 3
                vtok = b.emit("vector", lambda e, u0_=u0, bk_=bank: e.tensor_copy(
                    ymcv[:, u0_:u0_ + 4, :, :],
                    ps[bk_].ap().rearrange("p (a par t1p) -> p a par t1p",
                                           a=4, par=2, t1p=64)),
                    wait=pool.wait_mm(bank), inc=("v",))
                pool.set_evac(bank, "v", vtok)
        v_ymc = b.now("v")

        for j in range(NJ):
            bank = 2 + (j // 4) % 2
            tok = b.emit("tensor", lambda e, j_=j, bk_=bank: e.matmul(
                ps[bk_][:, (j_ % 4) * 128:(j_ % 4) * 128 + 128],
                dbuf[:, 4096 + j_ * 128:4096 + j_ * 128 + 128], idn[:, :],
                start=True, stop=True),
                wait=(pool.wait_free(bank) if j % 4 == 0 else ()) +
                     ((("v", v_ymc),) if j == 0 else ()),
                inc=("t",))
            pool.set_mm(bank, tok)
            if j % 4 == 3:
                j0 = j - 3
                vtok = b.emit("vector", lambda e, j0_=j0, bk_=bank, bl_=bl:
                              e.tensor_tensor(
                                  x_tmv[:, j0_:j0_ + 4, bl_ * CB:bl_ * CB + 128],
                                  x_tmv[:, j0_:j0_ + 4, bl_ * CB:bl_ * CB + 128],
                                  ps[bk_].ap().rearrange("p (a c) -> p a c",
                                                         a=4, c=128), ALU.add),
                              wait=pool.wait_mm(bank), inc=("v",))
                pool.set_evac(bank, "v", vtok)
        b.barrier()

    if BUILD_PHASES < 3:
        gd = b.dma("gpsimd",
                   y_out.rearrange("(j p) c -> p j c", p=128),
                   x_tm.ap().rearrange("p (j c) -> p j c", j=NJ, c=D),
                   wait=tuple((s, b.now(s)) for s in ("t", "v", "s")))
        fin = tuple((s, b.now(s)) for s in ("d", "g", "t", "v", "s", "p", "c"))
        b.emit("gpsimd", lambda e: e.nop(), wait=fin)
        with nc.Block() as block:
            b.replay(block, sems)
        es.close()
        return nc

    s_sq3 = v_mx3 = p_mn3 = 0
    for j in range(NJ):
        b.emit("scalar", lambda e, j_=j: e.activation(
            sqv[:, 0:1024], x_tm[:, j_ * D:(j_ + 1) * D], AF.Copy,
            accum_out=st_[:, S_SUMX + j_:S_SUMX + j_ + 1]), inc=("s",))
        s_sq3 = b.emit("scalar", lambda e, j_=j: e.activation(
            sqv[:, 0:1024], x_tm[:, j_ * D:(j_ + 1) * D], AF.Square,
            accum_out=st_[:, S_SUMSQ + j_:S_SUMSQ + j_ + 1]), inc=("s",))
        v_mx3 = b.emit("vector", lambda e, j_=j: e.tensor_reduce(
            st_[:, S_MAX + j_:S_MAX + j_ + 1], x_tm[:, j_ * D:(j_ + 1) * D],
            axis=AX, op=ALU.max), inc=("v",), fence=True)
        p_mn3 = b.emit("vector", lambda e, j_=j: e.tensor_reduce(
            st_[:, S_MIN + j_:S_MIN + j_ + 1], x_tm[:, j_ * D:(j_ + 1) * D],
            axis=AX, op=ALU.min), inc=("v",), fence=True)
    s_fin3 = ln_finalize((("s", s_sq3), ("v", v_mx3)))
    v_lm3 = ln_localmax((("v", p_mn3), ("s", s_fin3)))
    g_cw2 = b.dma("gpsimd", cc2_in.ap()[0:1, 0:8], row1[0:1, 0:8],
                  wait=(("v", v_lm3),))
    c_cc2 = b.emit("gpsimd", lambda e: e.collective_compute(
        "AllReduce", ALU.max, replica_groups=[list(range(NCORE))],
        ins=[cc2_in.ap()], outs=[cc2_out.ap()]),
        wait=(("g", g_cw2),), inc=("c",))
    g_cr2 = b.dma("gpsimd", row1[0:1, 8:16], cc2_out.ap()[0:1, 0:8],
                  wait=(("c", c_cc2),))
    g_qw = b.dma("gpsimd", tbuf[:, 0:8192], qw_spill.ap()[:, :],
                 wait=(("t", b.now("t")),))
    p_sc2 = ffn_scales(g_cr2)
    p_gate = b.emit("vector", lambda e: e.engine_nop(),
                    wait=(("g", g_qw), ("v", p_sc2)), inc=("v",))
    ffn(p_gate, final=True)

    fin = tuple((s, b.now(s)) for s in ("d", "g", "t", "v", "s", "p", "c"))
    b.emit("gpsimd", lambda e: e.nop(), wait=fin)

    with nc.Block() as block:
        b.replay(block, sems)
    es.close()
    return nc


def kernel(L0_attn_x, L0_attn_W1, L0_attn_b1, L0_attn_W2, L0_attn_b2,
           L0_ffn_weight, L0_ffn_bias):
    tb = _tables()
    nc = build_nc()
    wt = np.ascontiguousarray(np.asarray(L0_ffn_weight).T)
    in_maps = []
    for core in range(NCORE):
        m = {"x": np.ascontiguousarray(np.asarray(L0_attn_x)[core]),
             "wt": wt,
             "w2blk": np.ascontiguousarray(
                 np.asarray(L0_attn_W2)[:, core * CB:(core + 1) * CB]),
             "w1": np.ascontiguousarray(np.asarray(L0_attn_W1)),
             "b1": np.ascontiguousarray(
                 np.asarray(L0_attn_b1).reshape(8, 128).T),
             "b2blk": np.ascontiguousarray(
                 np.asarray(L0_attn_b2)[core * CB:(core + 1) * CB]).reshape(128, 1),
             "fbias": np.ascontiguousarray(
                 np.asarray(L0_ffn_bias).reshape(8, 128).T)}
        for k, v in tb.items():
            m["t_" + k] = v
        in_maps.append(m)
    res = run_bass_kernel_spmd(nc, in_maps, core_ids=list(range(NCORE)))
    return np.stack([res.results[i]["y"] for i in range(NCORE)], axis=0)


# version 03
# speedup vs baseline: 1.0084x; 1.0084x over previous
import numpy as np
from contextlib import ExitStack

import concourse.bass as bass
import concourse.mybir as mybir
from concourse.bass_utils import run_bass_kernel_spmd

F32 = mybir.dt.float32
BF16 = mybir.dt.bfloat16
I8 = mybir.dt.int8
AF = mybir.ActivationFunctionType
ALU = mybir.AluOpType
AX = mybir.AxisListType.X

NCORE = 8
T = 4096
D = 1024
NJ = 32
NB = 8
CB = 128
EPS = 1e-5

DEBUG_TAPS = []
BUILD_PHASES = 99
DUMP = ""
CONV_ORDER = None
DEBUG_MEMSET = 0


def _pos_embedding(seq_len, dim):
    position = np.arange(seq_len, dtype=np.float64)[:, None]
    div_term = np.exp(np.arange(0, dim, 2, dtype=np.float64) *
                      (-np.log(10000.0) / dim))
    ang = position * div_term
    pe = np.zeros((seq_len, dim), dtype=np.float64)
    pe[:, 0::2] = np.sin(ang)
    pe[:, 1::2] = np.cos(ang)
    return pe


def _build_tables():
    tb = {}
    tb["posT"] = np.ascontiguousarray(_pos_embedding(T, 64).T).astype(np.float32)
    t1 = np.arange(64)[:, None]
    k1 = np.arange(65)[None, :]
    ang = 2.0 * np.pi * (t1 * k1) / 128.0
    m1c, m1s = np.cos(ang), -np.sin(ang)
    tb["M1c"] = np.concatenate([m1c, m1c], 0).astype(np.float32)
    tb["M1s"] = np.concatenate([m1s, m1s], 0).astype(np.float32)

    t2 = np.arange(64)[:, None]
    k2 = np.arange(32)[None, :]
    m2p = np.zeros((65, 128, 64))
    m2m = np.zeros((64, 128, 64))
    for g in range(65):
        th = 2.0 * np.pi * (t2 * g / 8192.0 + t2 * k2 / 64.0)
        c, s = np.cos(th), np.sin(th)
        m2p[g, 0:64, 0:32] = c
        m2p[g, 64:128, 0:32] = s
        m2p[g, 0:64, 32:64] = -s
        m2p[g, 64:128, 32:64] = c
        if 1 <= g <= 63:
            thm = 2.0 * np.pi * (t2 * (128 - g) / 8192.0 + t2 * k2 / 64.0)
            cm, sm = np.cos(thm), np.sin(thm)
            m2m[g, 0:64, 0:32] = cm
            m2m[g, 64:128, 0:32] = -sm
            m2m[g, 0:64, 32:64] = -sm
            m2m[g, 64:128, 32:64] = -cm
    tb["M2P"] = m2p.transpose(1, 0, 2).reshape(128, 65 * 64).astype(np.float32)
    tb["M2M"] = m2m[1:64].transpose(1, 0, 2).reshape(128, 63 * 64).astype(np.float32)
    m4 = np.zeros((128, 2))
    sgn = (-1.0) ** np.arange(64)
    m4[0:64, 0] = sgn
    m4[64:128, 1] = sgn
    tb["M4096"] = m4.astype(np.float32)

    t2p = np.arange(64)[None, :]
    k2c = np.arange(32)[:, None]
    phi = 2.0 * np.pi * (k2c * t2p) / 64.0
    c, s = np.cos(phi), np.sin(phi)
    esa = np.zeros((64, 128))
    esa[0:32, 0:64] = c
    esa[32:64, 0:64] = -s
    esa[0:32, 64:128] = s
    esa[32:64, 64:128] = c
    tb["ESA"] = np.concatenate([esa, esa], 0).astype(np.float32)
    e4 = np.zeros((2, 128))
    sg = 0.5 * (-1.0) ** np.arange(64)
    e4[0, 0:64] = sg
    e4[1, 64:128] = sg
    tb["E4096"] = e4.astype(np.float32)

    k1c = np.arange(128)[:, None]
    t1p = np.arange(64)[None, :]
    mbre = np.zeros((64, 128, 64))
    mbim = np.zeros((64, 128, 64))
    for tp in range(64):
        th = 2.0 * np.pi * ((tp + 64.0 * t1p) * k1c) / 8192.0
        mbre[tp] = (2.0 / 8192.0) * np.cos(th)
        mbim[tp] = -(2.0 / 8192.0) * np.sin(th)
    tb["MBre"] = mbre.transpose(1, 0, 2).reshape(128, 64 * 64).astype(np.float32)
    tb["MBim"] = mbim.transpose(1, 0, 2).reshape(128, 64 * 64).astype(np.float32)
    tb["IDN"] = np.eye(128, dtype=np.float32)
    tb["ONES"] = np.ones((1, 128), dtype=np.float32)
    tb["IDNF"] = np.eye(128, dtype=np.float32)
    return tb


_TABLES = None


def _tables():
    global _TABLES
    if _TABLES is None:
        _TABLES = _build_tables()
    return _TABLES


ENGINES = ("sync", "tensor", "vector", "scalar", "gpsimd")


class KB:
    def __init__(self):
        self.ops = {e: [] for e in ENGINES}
        self.cnt = {}
        self._vfence = 0

    def emit(self, eng, fn, wait=(), inc=(), amt=1, fence=False):
        if eng == "vector":
            wait = tuple(wait) + (("v", self.cnt.get("v", 0)),)
            if "v" not in inc:
                inc = tuple(inc) + ("v",)
        wait = tuple((s, c) for (s, c) in wait if c > 0)
        self.ops[eng].append((fn, wait, tuple(inc), amt))
        out = []
        for s in inc:
            self.cnt[s] = self.cnt.get(s, 0) + amt
            out.append(self.cnt[s])
        if eng == "vector" and fence:
            self._vfence = self.cnt["v"]
        return out[0] if len(out) == 1 else (out if out else None)

    def dma(self, eng, dst, src, wait=(), sem=None, slow=False):
        sem = sem or ("g" if eng == "gpsimd" else "d")
        return self.emit(eng, lambda e, d=dst, s=src, sl=slow: e.dma_start(
            out=d, in_=s, allow_slow_non_contiguous=sl),
                         wait=wait, inc=(sem,), amt=16)

    def now(self, sem):
        return self.cnt.get(sem, 0)

    def barrier(self):
        snap = tuple((s, self.now(s)) for s in ("t", "v", "s", "p", "g", "d"))
        own = {"tensor": "t", "vector": "v", "scalar": "s", "gpsimd": "p"}
        for eng in ("tensor", "vector", "scalar", "gpsimd"):
            w = tuple((s, c) for s, c in snap if s != own[eng])
            self.emit(eng, lambda e: e.nop(), wait=w, inc=(own[eng],))

    def replay(self, block, sems):
        for eng_name in ENGINES:
            ops = list(self.ops[eng_name])
            if not ops:
                continue

            def make(ops_):
                def _f(eng):
                    for fn, wait, inc, amt in ops_:
                        for s, c in wait:
                            eng.wait_ge(sems[s], c)
                        instr = fn(eng)
                        for s in inc:
                            instr = instr.then_inc(sems[s], amt)
                return _f
            getattr(block, eng_name)(make(ops))


class PsumPool:
    def __init__(self):
        self.evac = [None] * 8
        self.mm = [0] * 8

    def wait_free(self, bank):
        ev = self.evac[bank]
        return (ev,) if ev else ()

    def set_mm(self, bank, tok):
        self.mm[bank] = tok

    def wait_mm(self, *banks):
        return tuple(("t", self.mm[bk]) for bk in banks)

    def set_evac(self, bank, sem, tok):
        self.evac[bank] = (sem, tok)


def build_nc():
    tb = _tables()
    nc = bass.Bass("TRN2", target_bir_lowering=False,
                   detect_race_conditions=False)

    x_in = nc.dram_tensor("x", [T, D], F32, kind="ExternalInput").ap()
    wt_in = nc.dram_tensor("wt", [D, D], F32, kind="ExternalInput").ap()
    w2_in = nc.dram_tensor("w2blk", [D, CB], F32, kind="ExternalInput").ap()
    w1_in = nc.dram_tensor("w1", [64, D], F32, kind="ExternalInput").ap()
    b1_in = nc.dram_tensor("b1", [128, 8], F32, kind="ExternalInput").ap()
    b2_in = nc.dram_tensor("b2blk", [128, 1], F32, kind="ExternalInput").ap()
    fb_in = nc.dram_tensor("fbias", [128, 8], F32, kind="ExternalInput").ap()
    y_out = nc.dram_tensor("y", [T, D], F32, kind="ExternalOutput").ap()
    tbl_in = {k: nc.dram_tensor("t_" + k, list(tb[k].shape), F32,
                                kind="ExternalInput").ap() for k in tb}
    taps = {name: nc.dram_tensor("dbg_" + name, list(shape), F32,
                                 kind="ExternalOutput").ap()
            for name, shape in DEBUG_TAPS}

    cc1_in = nc.dram_tensor("cc1_in", [1, 8], F32)
    cc1_out = nc.dram_tensor("cc1_out", [1, 8], F32)
    cc2_in = nc.dram_tensor("cc2_in", [1, 8], F32)
    cc2_out = nc.dram_tensor("cc2_out", [1, 8], F32)
    fbmy = nc.dram_tensor("fblob_my", [256, 8192], BF16)
    fball = nc.dram_tensor("fblob_all", [NCORE * 256, 8192], BF16)
    qw_spill = nc.dram_tensor("qw_spill", [128, 8192], BF16)
    sshb = nc.dram_tensor("sshb", [128, 8192], BF16)

    es = ExitStack()
    SBT = lambda name, shape, dt=BF16: es.enter_context(
        nc.sbuf_tensor(name, shape, dt))

    x_tm = SBT("x_tm", [128, NJ * D])
    big = SBT("big", [128, 32768])
    wbuf = SBT("wbuf", [128, 8192])
    cbuf = SBT("cbuf", [128, 8192])
    dbuf = SBT("dbuf", [128, 8192])
    tbuf = SBT("tbuf", [128, 8192])
    nst = SBT("nst", [128, 512])
    sqb = SBT("sqb", [128, 8])
    m1c = SBT("m1c_t", [128, 65])
    m1s = SBT("m1s_t", [128, 65])
    esa = SBT("esa_t", [128, 128])
    e4096 = SBT("e4096_t", [2, 128])
    m4096 = SBT("m4096_t", [128, 2])
    idn = SBT("idn_t", [128, 128])
    ones1 = SBT("ones_t", [1, 128], F32)
    idnf = SBT("idnf_t", [128, 128], F32)
    s4096 = SBT("s4096", [2, CB])
    bias8 = SBT("bias8", [128, 8], F32)
    b1s = SBT("b1s", [128, 8], F32)
    b2s = SBT("b2s", [128, 1], F32)
    st_ = SBT("stats", [128, 320], F32)
    row1 = SBT("row1", [1, 64], F32)
    sc1 = SBT("sc1", [128, 8], F32)
    S_SUMX, S_SUMSQ, S_MAX, S_MIN, S_M, S_R, S_QA, S_QB, S_T1, S_T2 = \
        0, 32, 64, 96, 128, 160, 192, 224, 256, 288

    ps = [es.enter_context(nc.psum_tensor("psb%d" % i, [128, 512], F32))
          for i in range(8)]
    sems = {s: es.enter_context(nc.semaphore("sem_" + s))
            for s in ("d", "g", "t", "v", "s", "p", "c")}

    b = KB()
    pool = PsumPool()
    bigv = big.ap()
    big32 = bigv.bitcast(F32)
    big8 = bigv.bitcast(I8)
    wb32 = wbuf.ap().bitcast(F32)
    sqv = big32[:, 10240:11264]
    ostv = big32[:, 12288:14336]
    xstv = big32[:, 14336:16384]
    qstv = big8[:, 40960:45056]
    qbv = bigv[:, 16384:20480]
    ZTC = 16384

    def tap(name, src, wait):
        if name in taps:
            b.dma("gpsimd", taps[name][:, :], src, wait=wait)

    g_ms = b.emit("gpsimd", lambda e: e.memset(row1[0:1, 0:64], 0.0), inc=("p",),
                  amt=1)
    g_tbl = 0
    for name, dst in (("M1c", m1c), ("M1s", m1s), ("ESA", esa), ("M4096", m4096),
                      ("IDN", idn), ("E4096", e4096), ("ONES", ones1),
                      ("IDNF", idnf)):
        sh = tb[name].shape
        g_tbl = b.dma("gpsimd", dst[0:sh[0], 0:sh[1]], tbl_in[name][:, :])
    g_posT = b.dma("gpsimd", cbuf[0:64, 0:4096], tbl_in["posT"][:, :])
    g_w1 = b.dma("gpsimd", cbuf[0:64, 4096:5120], w1_in[:, :])
    g_w2 = b.dma("gpsimd",
                 cbuf[:, 5120:6144].rearrange("p (rt c) -> p rt c", rt=8, c=CB),
                 w2_in.rearrange("(rt p) c -> p rt c", rt=8, p=128))
    g_b1 = b.dma("sync", b1s[:, 0:8], b1_in[:, :])
    g_fb0 = b.dma("sync", bias8[:, 0:8], fb_in[:, :])
    g_b2 = b.dma("sync", b2s[:, 0:1], b2_in[:, :])
    v_fb = b.emit("vector", lambda e: e.engine_nop(),
                  wait=(("d", g_fb0), ("p", g_ms)), inc=("v",))


    def bcast(src_cols, dst_cols, wait=()):
        a, bb = src_cols
        c, d = dst_cols
        n = bb - a
        tt = b.emit("tensor", lambda e: e.matmul(
            ps[7][:, 0:n], ones1[0:1, :], row1[0:1, a:bb], start=True, stop=True),
            wait=tuple(wait) + pool.wait_free(7) + (("g", g_tbl),), inc=("t",))
        pool.set_mm(7, tt)
        vt = b.emit("vector", lambda e: e.tensor_copy(
            sc1[:, c:d], ps[7][:, 0:n]), wait=(("t", tt),), inc=("v",))
        pool.set_evac(7, "v", vt)
        return vt

    v_abs = 0
    d_wh = 0
    for half in range(2):
        for kt in range(4):
            d_wh = b.dma("sync", wb32[:, kt * 1024:kt * 1024 + 1024],
                         wt_in[(half * 4 + kt) * 128:(half * 4 + kt + 1) * 128, :],
                         wait=(("v", v_abs),) if half == 1 and kt == 0 else ())
        for kt in range(4):
            v_abs = b.emit("vector", lambda e, h_=half, kt_=kt: e.tensor_reduce(
                st_[:, S_T1 + 4 * h_ + kt_:S_T1 + 4 * h_ + kt_ + 1],
                wb32[:, kt_ * 1024:kt_ * 1024 + 1024], axis=AX, op=ALU.add,
                apply_absolute_value=True),
                wait=(("d", d_wh),) if kt == 0 else (), inc=("v",))
    b.emit("vector", lambda e: e.tensor_reduce(
        st_[:, S_T2:S_T2 + 1], st_[:, S_T1:S_T1 + 8], axis=AX, op=ALU.add),
        wait=(("v", v_abs),), inc=("v",), fence=True)
    v_c = b.now("v")
    t_tp = b.emit("tensor", lambda e: e.matmul(
        ps[0][0:1, 0:128], st_[:, S_T2:S_T2 + 1], idnf[:, :], start=True,
        stop=True),
        wait=(("v", v_c), ("g", g_tbl)), inc=("t",))
    pool.set_mm(0, t_tp)
    v_sum = b.emit("vector", lambda e: e.tensor_reduce(
        row1[0:1, 0:1], ps[0][0:1, 0:128], axis=AX, op=ALU.add),
        wait=(("t", t_tp),), inc=("v",), fence=True)
    pool.set_evac(0, "v", v_sum)
    v_thr = b.emit("vector", lambda e: e.tensor_scalar(
        row1[0:1, 1:2], row1[0:1, 0:1], 0.5 / float(D * D), None, ALU.mult),
        inc=("v",))
    v_ws = b.emit("vector", lambda e: e.tensor_scalar(
        row1[0:1, 2:3], row1[0:1, 0:1], 1.0 / float(D * D), None, ALU.mult),
        inc=("v",))

    v_bc0 = bcast((1, 2), (0, 1), wait=(("v", v_thr),))
    q_tok = 0
    for half in range(2):
        d_wh = 0
        for kt in range(4):
            d_wh = b.dma("sync", wb32[:, kt * 1024:kt * 1024 + 1024],
                         wt_in[(half * 4 + kt) * 128:(half * 4 + kt + 1) * 128, :],
                         wait=(("v", q_tok if half else v_thr),))
        for kt in range(4):
            sl = slice(kt * 1024, kt * 1024 + 1024)
            qsl = slice((half * 4 + kt) * 1024, (half * 4 + kt) * 1024 + 1024)
            b.emit("vector", lambda e, sl_=sl: e.tensor_scalar(
                big32[:, 8192:9216], wb32[:, sl_], sc1[:, 0:1], None, ALU.is_gt),
                wait=((("d", d_wh),) if kt == 0 else ()) +
                     ((("v", v_bc0),) if half == 0 and kt == 0 else ()),
                inc=("v",))
            b.emit("vector", lambda e, sl_=sl: e.tensor_scalar(
                big32[:, 9216:10240], wb32[:, sl_], -1.0, sc1[:, 0:1], ALU.mult,
                ALU.is_gt), inc=("v",))
            q_tok = b.emit("vector", lambda e, q_=qsl: e.tensor_tensor(
                tbuf[:, q_], big32[:, 8192:9216], big32[:, 9216:10240],
                ALU.subtract), inc=("v",))
    d_qsp = b.dma("sync", qw_spill.ap()[:, :], tbuf[:, 0:8192],
                  wait=(("v", q_tok),))

    xin_t = x_in.rearrange("(j p) c -> j p c", p=128)
    cast_toks = [0] * NJ
    for j in range(NJ):
        h = (j % 2) * 1024
        d_x = b.dma("sync", xstv[:, h:h + 1024], xin_t[j, :, :],
                    wait=(("s", cast_toks[j - 2]),) if j >= 2 else ())
        cast_toks[j] = b.emit("scalar", lambda e, j_=j, h_=h: e.activation(
            x_tm[:, j_ * D:(j_ + 1) * D], xstv[:, h_:h_ + 1024], AF.Copy,
            accum_out=st_[:, S_SUMX + j_:S_SUMX + j_ + 1]),
            wait=(("d", d_x),), inc=("s",))
    s_xcast = cast_toks[-1]
    s_sq = v_mx = p_mn = 0
    for j in range(NJ):
        w0 = (("s", s_xcast),) if j == 0 else ()
        s_sq = b.emit("scalar", lambda e, j_=j: e.activation(
            sqv[:, 0:1024], x_tm[:, j_ * D:(j_ + 1) * D], AF.Square,
            accum_out=st_[:, S_SUMSQ + j_:S_SUMSQ + j_ + 1]), wait=w0, inc=("s",))
        v_mx = b.emit("vector", lambda e, j_=j: e.tensor_reduce(
            st_[:, S_MAX + j_:S_MAX + j_ + 1], x_tm[:, j_ * D:(j_ + 1) * D],
            axis=AX, op=ALU.max), wait=w0, inc=("v",), fence=True)
        p_mn = b.emit("vector", lambda e, j_=j: e.tensor_reduce(
            st_[:, S_MIN + j_:S_MIN + j_ + 1], x_tm[:, j_ * D:(j_ + 1) * D],
            axis=AX, op=ALU.min), wait=w0, inc=("v",), fence=True)

    def ln_finalize(wait):
        b.emit("vector", lambda e: e.tensor_scalar(
            st_[:, S_M:S_M + 32], st_[:, S_SUMX:S_SUMX + 32], 1.0 / D, None,
            ALU.mult), wait=wait, inc=("v",))
        b.emit("vector", lambda e: e.tensor_tensor(
            st_[:, S_T1:S_T1 + 32], st_[:, S_M:S_M + 32], st_[:, S_M:S_M + 32],
            ALU.mult), inc=("v",))
        b.emit("vector", lambda e: e.tensor_scalar(
            st_[:, S_T2:S_T2 + 32], st_[:, S_SUMSQ:S_SUMSQ + 32], 1.0 / D, None,
            ALU.mult), inc=("v",))
        b.emit("vector", lambda e: e.tensor_tensor(
            st_[:, S_T2:S_T2 + 32], st_[:, S_T2:S_T2 + 32], st_[:, S_T1:S_T1 + 32],
            ALU.subtract), inc=("v",))
        veps = b.emit("vector", lambda e: e.tensor_scalar(
            st_[:, S_T2:S_T2 + 32], st_[:, S_T2:S_T2 + 32], EPS, None, ALU.add),
            inc=("v",))
        vtok = b.emit("vector", lambda e: e.reciprocal(
            st_[:, S_T1:S_T1 + 32], st_[:, S_T2:S_T2 + 32]),
            wait=(("v", veps),), inc=("v",), fence=True)
        return b.emit("scalar", lambda e: e.activation(
            st_[:, S_R:S_R + 32], st_[:, S_T1:S_T1 + 32], AF.Sqrt),
            wait=(("v", vtok),), inc=("s",))

    def ln_localmax(wait):
        b.emit("vector", lambda e: e.tensor_tensor(
            st_[:, S_T1:S_T1 + 32], st_[:, S_MAX:S_MAX + 32], st_[:, S_M:S_M + 32],
            ALU.subtract), wait=wait, inc=("v",))
        b.emit("vector", lambda e: e.tensor_tensor(
            st_[:, S_T2:S_T2 + 32], st_[:, S_M:S_M + 32], st_[:, S_MIN:S_MIN + 32],
            ALU.subtract), inc=("v",))
        b.emit("vector", lambda e: e.tensor_tensor(
            st_[:, S_T1:S_T1 + 32], st_[:, S_T1:S_T1 + 32], st_[:, S_T2:S_T2 + 32],
            ALU.max), inc=("v",))
        vmm = b.emit("vector", lambda e: e.tensor_tensor(
            st_[:, S_T1:S_T1 + 32], st_[:, S_T1:S_T1 + 32], st_[:, S_R:S_R + 32],
            ALU.mult), inc=("v",))
        b.emit("vector", lambda e: e.tensor_reduce(
            st_[:, S_T2:S_T2 + 1], st_[:, S_T1:S_T1 + 32], axis=AX, op=ALU.max),
            wait=(("v", vmm),), inc=("v",), fence=True)
        vc = b.now("v")
        tt = b.emit("tensor", lambda e: e.matmul(
            ps[7][0:1, 0:128], st_[:, S_T2:S_T2 + 1], idnf[:, :], start=True,
            stop=True),
            wait=(("v", vc),) + pool.wait_free(7), inc=("t",))
        pool.set_mm(7, tt)
        vtok = b.emit("vector", lambda e: e.tensor_reduce(
            row1[0:1, 4:5], ps[7][0:1, 0:128], axis=AX, op=ALU.max),
            wait=(("t", tt),), inc=("v",), fence=True)
        pool.set_evac(7, "v", vtok)
        return vtok

    s_fin1 = ln_finalize((("s", s_sq), ("v", v_mx)))
    v_lm1 = ln_localmax((("v", p_mn), ("s", s_fin1)))
    g_cw = b.dma("gpsimd", cc1_in.ap()[0:1, 0:8], row1[0:1, 0:8],
                 wait=(("v", v_lm1), ("p", g_ms)))
    c_cc1 = b.emit("gpsimd", lambda e: e.collective_compute(
        "AllReduce", ALU.max, replica_groups=[list(range(NCORE))],
        ins=[cc1_in.ap()], outs=[cc1_out.ap()]),
        wait=(("g", g_cw),), inc=("c",))
    g_cr1 = b.dma("gpsimd", row1[0:1, 8:16], cc1_out.ap()[0:1, 0:8],
                  wait=(("c", c_cc1),))

    def ffn_scales(cc_read_tok):
        v16 = b.emit("vector", lambda e: e.tensor_scalar(
            row1[0:1, 16:17], row1[0:1, 12:13], 1.0 / 127.0, None, ALU.mult),
            wait=(("g", cc_read_tok),), inc=("v",))
        vrc = b.emit("vector", lambda e: e.reciprocal(
            row1[0:1, 17:18], row1[0:1, 16:17]), wait=(("v", v16),), inc=("v",), fence=True)
        bcast((17, 18), (1, 2), wait=(("v", vrc),))
        b.emit("vector", lambda e: e.tensor_scalar(
            st_[:, S_QA:S_QA + 32], st_[:, S_R:S_R + 32], sc1[:, 1:2], None,
            ALU.mult), inc=("v",))
        b.emit("vector", lambda e: e.tensor_tensor(
            st_[:, S_QB:S_QB + 32], st_[:, S_M:S_M + 32], st_[:, S_QA:S_QA + 32],
            ALU.mult), inc=("v",))
        b.emit("vector", lambda e: e.tensor_scalar(
            st_[:, S_QB:S_QB + 32], st_[:, S_QB:S_QB + 32], -1.0, None, ALU.mult),
            inc=("v",))
        b.emit("vector", lambda e: e.tensor_tensor(
            row1[0:1, 18:19], row1[0:1, 16:17], row1[0:1, 2:3], ALU.mult),
            wait=(("v", v_ws),), inc=("v",))
        return bcast((18, 19), (2, 3), wait=(("v", b.now("v")),))

    p_sc1 = ffn_scales(g_cr1)

    out_state = {"prev": [0, 0]}

    def ffn(gate_v, final):
        prev_cv = 0
        prev_qt_mm = 0
        prev_mm = 0
        prev_tb = 0
        for tc in range(8):
            jb = tc * 4
            vq = 0
            for jj in range(4):
                j = jb + jj
                vq = b.emit("vector", lambda e, j_=j, jj_=jj: e.tensor_scalar(
                    qstv[:, jj_ * D:(jj_ + 1) * D], x_tm[:, j_ * D:(j_ + 1) * D],
                    st_[:, S_QA + j_:S_QA + j_ + 1],
                    st_[:, S_QB + j_:S_QB + j_ + 1], ALU.mult, ALU.add),
                    wait=((("v", gate_v),) if tc == 0 and jj == 0 else ()) +
                         ((("s", prev_cv),) if jj == 0 else ()),
                    inc=("v",))
            s_cv = b.emit("scalar", lambda e: e.activation(
                qbv[:, 0:4 * D], qstv[:, 0:4 * D], AF.Copy),
                wait=(("v", vq), ("t", prev_qt_mm)), inc=("s",))
            prev_cv = s_cv
            for ct in range(8):
                bank = ct % 2
                for jj in range(4):
                    tok = b.emit("tensor", lambda e, ct_=ct, jj_=jj, bk_=bank:
                                 e.matmul(ps[bk_][:, jj_ * 128:jj_ * 128 + 128],
                                          qbv[:, jj_ * D + ct_ * 128:
                                              jj_ * D + ct_ * 128 + 128],
                                          idn[:, :], start=True, stop=True),
                                 wait=(pool.wait_free(bank) + (("s", s_cv),))
                                      if jj == 0 else (),
                                 inc=("t",))
                    pool.set_mm(bank, tok)
                vtok = b.emit("vector", lambda e, ct_=ct, bk_=bank: e.tensor_copy(
                    bigv[:, ct_ * 512:ct_ * 512 + 512], ps[bk_][:, 0:512]),
                    wait=pool.wait_mm(bank) +
                         ((("t", prev_mm),) if ct == 0 else ()),
                    inc=("v",))
                pool.set_evac(bank, "v", vtok)
            prev_qt_mm = pool.mm[1]
            v_qt = b.now("v")
            for ot in range(8):
                bank = 2 + ot % 2
                for kt in range(8):
                    tok = b.emit("tensor", lambda e, ot_=ot, kt_=kt, bk_=bank:
                                 e.matmul(ps[bk_][:, 0:512],
                                          tbuf[:, kt_ * 1024 + ot_ * 128:
                                               kt_ * 1024 + ot_ * 128 + 128],
                                          bigv[:, kt_ * 512:kt_ * 512 + 512],
                                          start=(kt_ == 0), stop=(kt_ == 7)),
                                 wait=(pool.wait_free(bank) + (("v", v_qt),))
                                      if kt == 0 else (),
                                 inc=("t",))
                    pool.set_mm(bank, tok)
                stok = b.emit("scalar", lambda e, ot_=ot, bk_=bank: e.activation(
                    bigv[:, 4096 + ot_ * 512:4096 + ot_ * 512 + 512],
                    ps[bk_][:, 0:512], AF.Identity, scale=sc1[:, 2:3],
                    bias=bias8[:, ot_:ot_ + 1]),
                    wait=pool.wait_mm(bank) +
                         ((("t", prev_tb),) if ot == 0 else ()),
                    inc=("s",))
                pool.set_evac(bank, "s", stok)
            prev_mm = max(pool.mm[2], pool.mm[3])
            s_fo = b.now("s")
            for jj in range(4):
                j = jb + jj
                slot = (j % 2) * D
                vtok = 0
                for oth in range(2):
                    bank = 4 + oth
                    for o4 in range(4):
                        ot = oth * 4 + o4
                        tok = b.emit("tensor",
                                     lambda e, jj_=jj, ot_=ot, o4_=o4, bk_=bank:
                                     e.matmul(
                                         ps[bk_][:, o4_ * 128:o4_ * 128 + 128],
                                         bigv[:, 4096 + ot_ * 512 + jj_ * 128:
                                              4096 + ot_ * 512 + jj_ * 128 + 128],
                                         idn[:, :], start=True, stop=True),
                                     wait=(pool.wait_free(bank) + (("s", s_fo),))
                                          if o4 == 0 else (),
                                     inc=("t",))
                        pool.set_mm(bank, tok)
                    if not final:
                        vtok = b.emit("vector",
                                      lambda e, j_=j, o_=oth, bk_=bank:
                                      e.scalar_tensor_tensor(
                                          x_tm[:, j_ * D + o_ * 512:
                                               j_ * D + o_ * 512 + 512],
                                          ps[bk_][:, 0:512], 0.5,
                                          x_tm[:, j_ * D + o_ * 512:
                                               j_ * D + o_ * 512 + 512],
                                          ALU.mult, ALU.add),
                                      wait=pool.wait_mm(bank), inc=("v",))
                    else:
                        prev = out_state["prev"][j % 2]
                        vtok = b.emit("vector",
                                      lambda e, j_=j, o_=oth, bk_=bank, sl_=slot:
                                      e.scalar_tensor_tensor(
                                          ostv[:, sl_ + o_ * 512:
                                               sl_ + o_ * 512 + 512],
                                          ps[bk_][:, 0:512], 0.5,
                                          x_tm[:, j_ * D + o_ * 512:
                                               j_ * D + o_ * 512 + 512],
                                          ALU.mult, ALU.add),
                                      wait=pool.wait_mm(bank) +
                                           ((("d", prev),) if oth == 0 and prev
                                            else ()),
                                      inc=("v",))
                    pool.set_evac(bank, "v", vtok)
                if final:
                    od = b.dma("sync", y_out[j * 128:(j + 1) * 128, :],
                               ostv[:, slot:slot + D], wait=(("v", vtok),))
                    out_state["prev"][j % 2] = od
            prev_tb = max(pool.mm[4], pool.mm[5])

    ffn(p_sc1, final=False)
    b.barrier()

    if BUILD_PHASES < 2:
        if DUMP == "q":
            yv = y_out.rearrange("(a p) c -> a p c", p=128)
            b.dma("gpsimd", yv[0:4, :, :].rearrange("a p c -> p a c"),
                  qbv[:, 0:4096],
                  wait=tuple((s, b.now(s)) for s in ("t", "v", "s")))
            b.dma("gpsimd", yv[4:8, :, :].rearrange("a p c -> p a c"),
                  bigv[:, 0:4096])
            b.dma("gpsimd", yv[8:12, :, :].rearrange("a p c -> p a c"),
                  bigv[:, 4096:8192])
        if DUMP == "qw":
            b.dma("gpsimd",
                  y_out.rearrange("(a p) c -> a p c", p=128)[0:8, :, :]
                       .rearrange("a p c -> p a c"),
                  tbuf.ap().rearrange("p (a c) -> p a c", a=8, c=1024),
                  wait=tuple((s, b.now(s)) for s in ("t", "v", "s")))
        if DUMP == "stats":
            b.dma("gpsimd", y_out[0:128, 0:320], st_[:, 0:320],
                  wait=tuple((s, b.now(s)) for s in ("t", "v", "s")))
            b.dma("gpsimd", y_out[128:256, 0:3], sc1[:, 0:3])
            b.dma("gpsimd", y_out[256:257, 0:64], row1[0:1, 0:64])
        if DUMP == "":
            w0 = tuple((s, b.now(s)) for s in ("t", "v", "s"))
            for j in range(NJ):
                b.dma("gpsimd", y_out[j * 128:(j + 1) * 128, :],
                      x_tm[:, j * D:(j + 1) * D], wait=w0 if j == 0 else ())
        fin = tuple((s, b.now(s)) for s in ("d", "g", "t", "v", "s", "p", "c"))
        b.emit("gpsimd", lambda e: e.nop(), wait=fin)
        with nc.Block() as block:
            b.replay(block, sems)
        es.close()
        return nc

    def fwd_fft(m2t, p_off, m_off, wait_mid):
        for u in range(32):
            bank = 4 + (u // 4) % 2
            tok = b.emit("tensor", lambda e, u_=u, bk_=bank: e.matmul(
                ps[bk_][:, (u_ % 4) * 128:(u_ % 4) * 128 + 128],
                dbuf[:, u_ * 128:u_ * 128 + 128], idn[:, :],
                start=True, stop=True),
                wait=((pool.wait_free(bank) if u % 4 == 0 else ()) +
                      (wait_mid if u == 0 else ())),
                inc=("t",))
            pool.set_mm(bank, tok)
            if u % 4 == 3:
                vtok = b.emit("vector", lambda e, u_=u, bk_=bank: e.tensor_copy(
                    dbuf[:, 4096 + (u_ - 3) * 128:4096 + (u_ + 1) * 128],
                    ps[bk_][:, 0:512]),
                    wait=pool.wait_mm(bank), inc=("v",))
                pool.set_evac(bank, "v", vtok)
        v_xnf = b.now("v")
        zview = big[:, 0:16384].rearrange("p (c ri t2h par) -> p ri par t2h c",
                                          c=128, ri=2, t2h=32, par=2)
        for u in range(8):
            for (bank, r0, mt) in ((0, 0, m1c), (1, 64, m1c),
                                   (2, 0, m1s), (3, 64, m1s)):
                tok = b.emit("tensor",
                             lambda e, u_=u, bk_=bank, r_=r0, m_=mt: e.matmul(
                                 ps[bk_][0:65, 0:512], m_[r_:r_ + 64, :],
                                 dbuf[r_:r_ + 64,
                                      4096 + u_ * 512:4096 + u_ * 512 + 512],
                                 start=True, stop=True, tile_position=(r_, 0)),
                             wait=(pool.wait_free(bank) +
                                   ((("v", v_xnf),) if u == 0 else ())),
                             inc=("t",))
                pool.set_mm(bank, tok)
            for (bank, ri, par) in ((0, 0, 0), (1, 0, 1), (2, 1, 0), (3, 1, 1)):
                vtok = b.emit("vector",
                              lambda e, u_=u, bk_=bank, ri_=ri, pr_=par:
                              e.tensor_copy(
                                  zview[0:65, ri_, pr_, 4 * u_:4 * u_ + 4, :],
                                  ps[bk_].ap().rearrange(
                                      "p (a c) -> p a c", a=4, c=128)[0:65]),
                              wait=pool.wait_mm(bank), inc=("v",))
                pool.set_evac(bank, "v", vtok)
        v_z = b.now("v")
        for c in range(128):
            bank = (c // 4) % 2
            tok = b.emit("tensor", lambda e, c_=c, bk_=bank: e.matmul(
                ps[bk_][0:128, (c_ % 4) * 65:(c_ % 4) * 65 + 65],
                bigv[0:65, c_ * 128:c_ * 128 + 128], idn[0:65, 0:65],
                start=True, stop=True),
                wait=((pool.wait_free(bank) if c % 4 == 0 else ()) +
                      ((("v", v_z),) if c == 0 else ())),
                inc=("t",))
            pool.set_mm(bank, tok)
            if c % 4 == 3:
                vtok = b.emit("vector", lambda e, c_=c, bk_=bank: e.tensor_copy(
                    bigv[:, ZTC + (c_ - 3) * 65:ZTC + (c_ + 1) * 65],
                    ps[bk_][:, 0:260]),
                    wait=pool.wait_mm(bank), inc=("v",))
                pool.set_evac(bank, "v", vtok)
        v_zt = b.now("v")
        ztv = big[:, ZTC:ZTC + 8320].rearrange("p (cc g) -> p g cc",
                                               cc=128, g=65)

        def zt_rhs(g):
            return ztv[:, g, :]
        for g in range(65):
            bank = 6 + g % 2
            wmm = pool.wait_free(bank) + ((("v", v_zt),) if g <= 1 else ())
            if g < 64:
                tok = b.emit("tensor", lambda e, g_=g, bk_=bank: e.matmul(
                    ps[bk_][0:64, 0:128],
                    m2t[:, p_off + g_ * 64:p_off + g_ * 64 + 64],
                    zt_rhs(g_), start=True, stop=True),
                    wait=wmm, inc=("t",))
            else:
                tok = b.emit("tensor", lambda e, g_=g, bk_=bank: e.matmul(
                    ps[bk_][64:128, 0:128],
                    m2t[:, p_off + g_ * 64:p_off + g_ * 64 + 64],
                    zt_rhs(g_), start=True, stop=True, tile_position=(0, 64)),
                    wait=wmm, inc=("t",))
            pool.set_mm(bank, tok)
            if 1 <= g <= 63:
                tok = b.emit("tensor", lambda e, g_=g, bk_=bank: e.matmul(
                    ps[bk_][64:128, 0:128],
                    m2t[:, m_off + (g_ - 1) * 64:m_off + (g_ - 1) * 64 + 64],
                    zt_rhs(g_), start=True, stop=True, tile_position=(0, 64)),
                    inc=("t",))
                pool.set_mm(bank, tok)
            if g == 0:
                tok = b.emit("tensor", lambda e: e.matmul(
                    ps[5][0:2, 0:128], m4096[:, 0:2], zt_rhs(0),
                    start=True, stop=True),
                    wait=pool.wait_free(5), inc=("t",))
                pool.set_mm(5, tok)
                vtok5 = b.emit("vector", lambda e: e.tensor_copy(
                    s4096[0:2, :], ps[5][0:2, 0:128]),
                    wait=pool.wait_mm(5), inc=("v",))
                pool.set_evac(5, "v", vtok5)
            if g < 64:
                vtok = b.emit("vector", lambda e, g_=g, bk_=bank: e.tensor_copy(
                    cbuf[0:64, g_ * 128:g_ * 128 + 128], ps[bk_][0:64, 0:128]),
                    wait=pool.wait_mm(bank), inc=("v",))
            else:
                vtok = b.emit("vector", lambda e, bk_=bank: e.tensor_copy(
                    cbuf[64:128, 0:128], ps[bk_][64:128, 0:128]),
                    wait=pool.wait_mm(bank), inc=("v",))
            if 1 <= g <= 63:
                vtok = b.emit("vector", lambda e, g_=g, bk_=bank: e.tensor_copy(
                    cbuf[64:128, (64 - g_) * 128:(64 - g_) * 128 + 128],
                    ps[bk_][64:128, 0:128]), inc=("v",))
            pool.set_evac(bank, "v", vtok)
        return b.now("v")

    for rt in range(8):
        for u in range(8):
            bank = u % 2
            tok = b.emit("tensor", lambda e, rt_=rt, u_=u, bk_=bank: e.matmul(
                ps[bk_][:, 0:512],
                cbuf[0:64, 4096 + rt_ * 128:4096 + rt_ * 128 + 128],
                cbuf[0:64, u_ * 512:u_ * 512 + 512], start=True, stop=True),
                wait=(pool.wait_free(bank) +
                      (((("g", g_posT), ("g", g_w1), ("d", g_b1)))
                       if rt == 0 and u == 0 else ())),
                inc=("t",))
            pool.set_mm(bank, tok)
            stok = b.emit("scalar", lambda e, rt_=rt, u_=u, bk_=bank: e.activation(
                bigv[:, rt_ * 4096 + u_ * 512:rt_ * 4096 + u_ * 512 + 512],
                ps[bk_][:, 0:512], AF.Gelu, bias=b1s[:, rt_:rt_ + 1]),
                wait=pool.wait_mm(bank), inc=("s",))
            pool.set_evac(bank, "s", stok)
    for u in range(8):
        bank = 2 + u % 2
        for rt in range(8):
            tok = b.emit("tensor", lambda e, rt_=rt, u_=u, bk_=bank: e.matmul(
                ps[bk_][:, 0:512], cbuf[:, 5120 + rt_ * 128:5120 + rt_ * 128 + 128],
                bigv[:, rt_ * 4096 + u_ * 512:rt_ * 4096 + u_ * 512 + 512],
                start=(rt_ == 0), stop=(rt_ == 7)),
                wait=((pool.wait_free(bank) +
                       ((("g", g_w2), ("d", g_b2), ("s", b.now("s")))
                        if u == 0 else ())) if rt == 0 else ()),
                inc=("t",))
            pool.set_mm(bank, tok)
        stok = b.emit("scalar", lambda e, u_=u, bk_=bank: e.activation(
            dbuf[:, 0:4096].rearrange("p (t2 t1) -> p t2 t1", t2=64, t1=64)
                [:, :, 8 * u_:8 * u_ + 8],
            ps[bk_].ap().rearrange("p (a t2) -> p t2 a", a=8, t2=64),
            AF.Identity, bias=b2s[:, 0:1]),
            wait=pool.wait_mm(bank), inc=("s",))
        pool.set_evac(bank, "s", stok)
    s_fmid = b.now("s")
    b.barrier()
    g_m2p1 = b.dma("gpsimd", bigv[:, 24576:24576 + 4160], tbl_in["M2P"][:, :])
    g_m2m1 = b.dma("gpsimd", bigv[:, 28736:28736 + 4032], tbl_in["M2M"][:, :])
    b.ops["gpsimd"] = b.ops["gpsimd"][:-2]
    b.cnt["g"] -= 32
    g_m2p1 = b.dma("gpsimd", tbuf[:, 0:4160], tbl_in["M2P"][:, :],
                   wait=(("d", d_qsp),))
    g_m2m1 = b.dma("gpsimd", tbuf[:, 4160:4160 + 4032], tbl_in["M2M"][:, :])
    if DUMP == "fmid":
        b.dma("gpsimd", y_out[1024:1536, :].rearrange("(a p) c -> a p c", a=4, p=128)
              .rearrange("a p c -> p a c"),
              dbuf[:, 0:4096].rearrange("p (a c) -> p a c", a=4, c=1024),
              wait=tuple((s, b.now(s)) for s in ("t", "v", "s")))
        b.barrier()
    v_s4f = fwd_fft(tbuf, 0, 4160,
                    (("s", s_fmid), ("g", g_m2m1), ("g", g_m2p1)))
    if DUMP == "fmid":
        b.dma("gpsimd", y_out[0:1024, :].rearrange("(a p) c -> a p c", a=8, p=128)
              .rearrange("a p c -> p a c"),
              cbuf.ap().rearrange("p (a c) -> p a c", a=8, c=1024),
              wait=tuple((s, b.now(s)) for s in ("t", "v", "s")))
        b.dma("gpsimd", y_out[1536:2048, :].rearrange("(a p) c -> a p c", a=4, p=128)
              .rearrange("a p c -> p a c"),
              dbuf[:, 4096:8192].rearrange("p (a c) -> p a c", a=4, c=1024))
        b.barrier()
    v_h0a = b.emit("vector", lambda e: e.tensor_scalar(
        cbuf[0:1, 0:128], cbuf[0:1, 0:128], 0.5, None, ALU.mult),
        wait=(("v", v_s4f),), inc=("v",))
    v_h0b = b.emit("vector", lambda e: e.tensor_scalar(
        cbuf[32:33, 0:128], cbuf[32:33, 0:128], 0.5, None, ALU.mult), inc=("v",))
    v_ng1 = b.emit("vector", lambda e: e.tensor_scalar(
        bigv[32:64, 0:8192], cbuf[32:64, 0:8192], -1.0, None, ALU.mult),
        wait=(("v", v_h0b),), inc=("v",))
    v_ng2 = b.emit("vector", lambda e: e.tensor_scalar(
        bigv[96:128, 0:8192], cbuf[96:128, 0:8192], -1.0, None, ALU.mult),
        inc=("v",))
    fbv = fbmy.ap().rearrange("(two p) f -> two p f", two=2, p=128)
    g_bl = 0
    for (rd, rs, two, neg) in (
            ((0, 32), (0, 32), 0, False), ((32, 64), (0, 32), 0, False),
            ((64, 96), (64, 96), 0, False), ((96, 128), (64, 96), 0, False),
            ((0, 32), (32, 64), 1, True), ((32, 64), (32, 64), 1, False),
            ((64, 96), (96, 128), 1, True), ((96, 128), (96, 128), 1, False)):
        src = (bigv if neg else cbuf.ap())[rs[0]:rs[1], 0:8192]
        g_bl = b.dma("gpsimd", fbv[two, rd[0]:rd[1], :], src,
                     wait=(("v", v_ng2),))
    c_ag = b.emit("gpsimd", lambda e: e.collective_compute(
        "AllGather", ALU.bypass, replica_groups=[list(range(NCORE))],
        ins=[fbmy.ap()], outs=[fball.ap()]),
        wait=(("g", g_bl),), inc=("c",))
    b.barrier()

    s_sq2 = 0
    for j in range(NJ):
        b.emit("scalar", lambda e, j_=j: e.activation(
            sqv[:, 0:1024], x_tm[:, j_ * D:(j_ + 1) * D], AF.Copy,
            accum_out=st_[:, S_SUMX + j_:S_SUMX + j_ + 1]), inc=("s",))
        s_sq2 = b.emit("scalar", lambda e, j_=j: e.activation(
            sqv[:, 0:1024], x_tm[:, j_ * D:(j_ + 1) * D], AF.Square,
            accum_out=st_[:, S_SUMSQ + j_:S_SUMSQ + j_ + 1]), inc=("s",))
    s_fin2 = ln_finalize((("s", s_sq2),))
    b.emit("vector", lambda e: e.tensor_copy(
        st_[:, S_QA:S_QA + 32], st_[:, S_R:S_R + 32]),
        wait=(("s", s_fin2),), inc=("v",))
    b.emit("vector", lambda e: e.tensor_tensor(
        st_[:, S_QB:S_QB + 32], st_[:, S_M:S_M + 32], st_[:, S_R:S_R + 32],
        ALU.mult), inc=("v",))
    v_cs2 = b.emit("vector", lambda e: e.tensor_scalar(
        st_[:, S_QB:S_QB + 32], st_[:, S_QB:S_QB + 32], -1.0, None, ALU.mult),
        inc=("v",))
    b.barrier()

    midv = dbuf[:, 0:4096].rearrange("p (t2 jf tlh) -> p jf tlh t2",
                                     t2=64, jf=32, tlh=2)
    ymcv = dbuf[:, 4096:8192].rearrange("p (t1p t2h par) -> p t2h par t1p",
                                        t1p=64, t2h=32, par=2)
    x_tmv = x_tm.ap().rearrange("p (jf c) -> p jf c", jf=32, c=1024)
    vAview = big[:, 0:16384].rearrange("p (c k) -> p k c", c=128, k=128)
    vTview = big[:, ZTC:ZTC + 16384].rearrange("p (pl t2p c) -> p pl t2p c",
                                           pl=2, t2p=64, c=128)
    fallv = fball.ap().rearrange("(n two p) f -> n two p f",
                                 n=NCORE, two=2, p=128)

    def cdump(row0, srcap, nrows=512):
        w0 = tuple((s, b.now(s)) for s in ("t", "v", "s"))
        for a in range(nrows // 128):
            b.dma("gpsimd", y_out[row0 + a * 128:row0 + (a + 1) * 128, :],
                  srcap[:, a, :], wait=w0 if a == 0 else ())
        b.barrier()

    for bl in (CONV_ORDER or range(NB)):
        if DEBUG_MEMSET:
            mtoks = []
            for buf, cols in ((big, 32768), (cbuf, 8192), (dbuf, 8192)):
                if DEBUG_MEMSET == 1 or (DEBUG_MEMSET == 2 and buf is big) or \
                   (DEBUG_MEMSET == 3 and buf is cbuf) or \
                   (DEBUG_MEMSET == 4 and buf is dbuf):
                    mt = b.emit("gpsimd", lambda e, b_=buf, c_=cols: e.memset(
                        b_[:, 0:c_], 0.0), inc=("p",))
                    mtoks.append(mt)
            if mtoks:
                b.emit("vector", lambda e: e.engine_nop(),
                       wait=(("p", mtoks[-1]),), inc=("v",))
                b.barrier()
        g_m2p = b.dma("gpsimd", tbuf[:, 0:4160], tbl_in["M2P"][:, :],
                      wait=(("t", b.now("t")),))
        g_m2m = b.dma("gpsimd", tbuf[:, 4160:4160 + 4032], tbl_in["M2M"][:, :])
        tr_toks = [0, 0]
        for j in range(NJ):
            bank = (j // 4) % 2
            slot = (j % 2) * 128
            if j % 2 == 0:
                ntok = b.emit("scalar", lambda e, j_=j, sl_=slot, bl_=bl:
                              e.activation(
                    nst[:, sl_:sl_ + 128],
                    x_tm[:, j_ * D + bl_ * CB:j_ * D + bl_ * CB + 128],
                    AF.Identity, scale=st_[:, S_QA + j_:S_QA + j_ + 1],
                    bias=st_[:, S_QB + j_:S_QB + j_ + 1]),
                    wait=(("t", tr_toks[0]),), inc=("s",))
                wsem = ("s", ntok)
            else:
                ntok = b.emit("vector", lambda e, j_=j, sl_=slot, bl_=bl:
                              e.tensor_scalar(
                                  nst[:, sl_:sl_ + 128],
                                  x_tm[:, j_ * D + bl_ * CB:
                                       j_ * D + bl_ * CB + 128],
                                  st_[:, S_QA + j_:S_QA + j_ + 1],
                                  st_[:, S_QB + j_:S_QB + j_ + 1],
                                  ALU.mult, ALU.add),
                              wait=(("t", tr_toks[1]),), inc=("v",))
                wsem = ("v", ntok)
            tok = b.emit("tensor", lambda e, j_=j, sl_=slot, bk_=bank: e.matmul(
                ps[bk_][:, (j_ % 4) * 128:(j_ % 4) * 128 + 128],
                nst[:, sl_:sl_ + 128], idn[:, :], start=True, stop=True),
                wait=(wsem,) + (pool.wait_free(bank) if j % 4 == 0 else ()),
                inc=("t",))
            pool.set_mm(bank, tok)
            tr_toks[j % 2] = tok
            if j % 4 == 3:
                vtok = b.emit("vector", lambda e, j_=j, bk_=bank: e.tensor_copy(
                    midv[:, j_ - 3:j_ + 1, :, :],
                    ps[bk_].ap().rearrange("p (jj tlh t2) -> p jj tlh t2",
                                           jj=4, tlh=2, t2=64)),
                    wait=pool.wait_mm(bank), inc=("v",))
                pool.set_evac(bank, "v", vtok)
        v_mid = b.now("v")
        if DUMP == "fb" and bl == 0:
            w0 = tuple((s, b.now(s)) for s in ("t", "v", "s", "c"))
            fv = fball.ap().rearrange("(n two p) f -> n two p f",
                                      n=NCORE, two=2, p=128)
            for k, slot in enumerate((6, 7)):
                for a in range(8):
                    b.dma("gpsimd", y_out[k * 1024 + a * 128:
                                          k * 1024 + (a + 1) * 128, :],
                          fv[slot, 0, :, a * 1024:(a + 1) * 1024],
                          wait=w0 if (k == 0 and a == 0) else ())
            b.barrier()
        if False and DUMP == "fb" and bl == 0:
            for two in range(2):
                b.dma("gpsimd",
                      y_out[two * 1024:(two + 1) * 1024, :]
                      .rearrange("(a p) c -> a p c", a=8, p=128)
                      .rearrange("a p c -> p a c"),
                      fbmy.ap().rearrange("(two p) f -> two p f", two=2, p=128)
                      [two, :, :].rearrange("p (a c) -> p a c", a=8, c=1024),
                      wait=tuple((s, b.now(s)) for s in ("t", "v", "s", "c")))
                b.dma("gpsimd",
                      y_out[2048 + two * 1024:2048 + (two + 1) * 1024, :]
                      .rearrange("(a p) c -> a p c", a=8, p=128)
                      .rearrange("a p c -> p a c"),
                      fball.ap().rearrange("(n two p) f -> n two p f",
                                           n=NCORE, two=2, p=128)
                      [0, two, :, :].rearrange("p (a c) -> p a c", a=8, c=1024))
            b.barrier()
        if DUMP == "conv" and bl == 0:
            cdump(0, dbuf[:, 0:4096].rearrange("p (a c) -> p a c", a=4, c=1024))
        v_s4 = fwd_fft(tbuf, 0, 4160, (("v", v_mid), ("g", g_m2m)))
        if DUMP == "conv" and bl == 0:
            cdump(512, dbuf[:, 4096:8192].rearrange("p (a c) -> p a c", a=4, c=1024))
            cdump(1024, cbuf.ap().rearrange("p (a c) -> p a c", a=8, c=1024),
                  nrows=1024)
        b.barrier()

        for hf in range(2):
            co = hf * 4096
            fo = (hf % 2) * 0
            g_fre = b.dma("sync", wbuf[:, 0:4096],
                          fallv[bl, 0, :, co:co + 4096],
                          wait=(("c", c_ag),) if bl == 0 and hf == 0 else
                               (("v", b.now("v")),))
            g_fim = b.dma("sync", wbuf[:, 4096:8192],
                          fallv[bl, 1, :, co:co + 4096])
            g_s1 = b.dma("sync", sshb.ap()[0:32, co:co + 4096],
                         cbuf[32:64, co:co + 4096],
                         wait=(("v", v_s4),))
            g_s2 = b.dma("sync", sshb.ap()[32:64, co:co + 4096],
                         cbuf[0:32, co:co + 4096])
            g_s3 = b.dma("sync", sshb.ap()[64:96, co:co + 4096],
                         cbuf[96:128, co:co + 4096])
            g_s4 = b.dma("sync", sshb.ap()[96:128, co:co + 4096],
                         cbuf[64:96, co:co + 4096])
            g_s5 = b.dma("sync", dbuf[:, co:co + 4096],
                         sshb.ap()[:, co:co + 4096],
                         wait=(("d", g_s4),))
            v_w1 = b.emit("vector", lambda e, co_=co: e.tensor_tensor(
                cbuf[:, co_:co_ + 4096], cbuf[:, co_:co_ + 4096],
                wbuf[:, 0:4096], ALU.mult),
                wait=(("d", g_fim), ("d", g_s5)), inc=("v",))
            v_w2 = b.emit("vector", lambda e, co_=co: e.tensor_tensor(
                bigv[:, ZTC:ZTC + 4096], dbuf[:, co_:co_ + 4096],
                wbuf[:, 4096:8192], ALU.mult), inc=("v",))
            v_w3 = b.emit("vector", lambda e, co_=co: e.tensor_tensor(
                cbuf[:, co_:co_ + 4096], cbuf[:, co_:co_ + 4096],
                bigv[:, ZTC:ZTC + 4096], ALU.add), inc=("v",))
        v_pw = b.now("v")
        b.barrier()
        if DUMP == "conv" and bl == 0:
            cdump(2048, cbuf.ap().rearrange("p (a c) -> p a c", a=8, c=1024),
                  nrows=1024)

        for gh in range(64):
            bkA = (gh // 4) % 2
            bankA = 0 + 2 * bkA
            bankB = 1 + 2 * bkA
            cA = (gh % 4) * 128
            if gh == 0:
                tok = b.emit("tensor", lambda e: e.matmul(
                    ps[0][:, 0:128], esa[0:64, :], cbuf[0:64, 0:128],
                    start=True, stop=False, tile_position=(0, 0)),
                    wait=pool.wait_free(0) + pool.wait_free(5) + (("v", v_pw),),
                    inc=("t",))
                tok = b.emit("tensor", lambda e: e.matmul(
                    ps[0][:, 0:128], e4096[0:2, :], s4096[0:2, :],
                    start=False, stop=True, tile_position=(0, 0)), inc=("t",))
                pool.set_mm(0, tok)
            else:
                tok = b.emit("tensor", lambda e, g_=gh, c_=cA, bk_=bankA: e.matmul(
                    ps[bk_][:, c_:c_ + 128], esa[0:64, :],
                    cbuf[0:64, g_ * 128:g_ * 128 + 128],
                    start=True, stop=True, tile_position=(0, 0)),
                    wait=(pool.wait_free(bankA) if gh % 4 == 0 else ()) +
                         ((("v", v_pw),) if gh == 1 else ()),
                    inc=("t",))
                pool.set_mm(bankA, tok)
            tok = b.emit("tensor", lambda e, g_=gh, c_=cA, bk_=bankB: e.matmul(
                ps[bk_][:, c_:c_ + 128], esa[64:128, :],
                cbuf[64:128, g_ * 128:g_ * 128 + 128],
                start=True, stop=True, tile_position=(64, 0)),
                wait=(pool.wait_free(bankB) if gh % 4 == 0 else ()) +
                     ((("v", v_pw),) if gh == 0 else ()),
                inc=("t",))
            pool.set_mm(bankB, tok)
            if gh % 4 == 3:
                g0 = gh - 3
                vtok = b.emit("vector", lambda e, g0_=g0, bk_=bankA: e.tensor_copy(
                    vAview[:, g0_:g0_ + 4, :],
                    ps[bk_].ap().rearrange("p (a c) -> p a c", a=4, c=128)),
                    wait=pool.wait_mm(bankA), inc=("v",))
                pool.set_evac(bankA, "v", vtok)
                vtok = b.emit("vector", lambda e, g0_=g0, bk_=bankB: e.tensor_copy(
                    vAview[:, 64 + g0_:64 + g0_ + 4, :],
                    ps[bk_].ap().rearrange("p (a c) -> p a c", a=4, c=128)),
                    wait=pool.wait_mm(bankB), inc=("v",))
                pool.set_evac(bankB, "v", vtok)
        v_sa = b.now("v")
        if DUMP == "conv" and bl == 0:
            cdump(3072, big[:, 0:4096].rearrange("p (a c) -> p a c", a=4, c=1024))
        g_mbr = b.dma("gpsimd", tbuf[:, 0:4096], tbl_in["MBre"][:, :],
                      wait=(("t", b.now("t")),))
        g_mbi = b.dma("gpsimd", tbuf[:, 4096:8192], tbl_in["MBim"][:, :])

        for c in range(128):
            bank = 4 + (c // 4) % 2
            tok = b.emit("tensor", lambda e, c_=c, bk_=bank: e.matmul(
                ps[bk_][:, (c_ % 4) * 128:(c_ % 4) * 128 + 128],
                bigv[:, c_ * 128:c_ * 128 + 128], idn[:, :],
                start=True, stop=True),
                wait=(pool.wait_free(bank) if c % 4 == 0 else ()) +
                     ((("v", v_sa),) if c == 0 else ()),
                inc=("t",))
            pool.set_mm(bank, tok)
            if c % 4 == 3:
                c0 = c - 3
                vtok = b.emit("vector", lambda e, c0_=c0, bk_=bank: e.tensor_copy(
                    vTview[:, :, :, c0_:c0_ + 4],
                    ps[bk_].ap().rearrange("p (a pl t2p) -> p pl t2p a",
                                           a=4, pl=2, t2p=64)),
                    wait=pool.wait_mm(bank), inc=("v",))
                pool.set_evac(bank, "v", vtok)
        v_vt = b.now("v")

        for u in range(32):
            bank = 6 + (u // 4) % 2
            for pr in range(2):
                t2p = 2 * u + pr
                r0 = 64 * pr
                tok = b.emit("tensor", lambda e, t_=t2p, r_=r0, u_=u, bk_=bank:
                             e.matmul(
                                 ps[bk_][r_:r_ + 64, (u_ % 4) * 128:
                                         (u_ % 4) * 128 + 128],
                                 tbuf[:, t_ * 64:t_ * 64 + 64],
                                 bigv[:, ZTC + 0 * 8192 + t_ * 128:
                                      ZTC + t_ * 128 + 128],
                                 start=True, stop=False, tile_position=(0, r_)),
                             wait=(pool.wait_free(bank) if u % 4 == 0 and pr == 0
                                   else ()) +
                                  ((("v", v_vt), ("g", g_mbi)) if u == 0 and
                                   pr == 0 else ()),
                             inc=("t",))
                tok = b.emit("tensor", lambda e, t_=t2p, r_=r0, u_=u, bk_=bank:
                             e.matmul(
                                 ps[bk_][r_:r_ + 64, (u_ % 4) * 128:
                                         (u_ % 4) * 128 + 128],
                                 tbuf[:, 4096 + t_ * 64:4096 + t_ * 64 + 64],
                                 bigv[:, ZTC + 8192 + t_ * 128:
                                      ZTC + 8192 + t_ * 128 + 128],
                                 start=False, stop=True, tile_position=(0, r_)),
                             inc=("t",))
                pool.set_mm(bank, tok)
            if u % 4 == 3:
                u0 = u - 3
                vtok = b.emit("vector", lambda e, u0_=u0, bk_=bank: e.tensor_copy(
                    dbuf[:, u0_ * 128:u0_ * 128 + 512], ps[bk_][:, 0:512]),
                    wait=pool.wait_mm(bank), inc=("v",))
                pool.set_evac(bank, "v", vtok)
        v_ym = b.now("v")
        if DUMP == "conv" and bl == 0:
            cdump(3584, dbuf[:, 0:4096].rearrange("p (a c) -> p a c", a=4, c=1024))

        for u in range(32):
            bank = (u // 4) % 2
            tok = b.emit("tensor", lambda e, u_=u, bk_=bank: e.matmul(
                ps[bk_][:, (u_ % 4) * 128:(u_ % 4) * 128 + 128],
                dbuf[:, u_ * 128:u_ * 128 + 128], idn[:, :],
                start=True, stop=True),
                wait=(pool.wait_free(bank) if u % 4 == 0 else ()) +
                     ((("v", v_ym),) if u == 0 else ()),
                inc=("t",))
            pool.set_mm(bank, tok)
            if u % 4 == 3:
                u0 = u - 3
                vtok = b.emit("vector", lambda e, u0_=u0, bk_=bank: e.tensor_copy(
                    ymcv[:, u0_:u0_ + 4, :, :],
                    ps[bk_].ap().rearrange("p (a par t1p) -> p a par t1p",
                                           a=4, par=2, t1p=64)),
                    wait=pool.wait_mm(bank), inc=("v",))
                pool.set_evac(bank, "v", vtok)
        v_ymc = b.now("v")

        for j in range(NJ):
            bank = 2 + (j // 4) % 2
            tok = b.emit("tensor", lambda e, j_=j, bk_=bank: e.matmul(
                ps[bk_][:, (j_ % 4) * 128:(j_ % 4) * 128 + 128],
                dbuf[:, 4096 + j_ * 128:4096 + j_ * 128 + 128], idn[:, :],
                start=True, stop=True),
                wait=(pool.wait_free(bank) if j % 4 == 0 else ()) +
                     ((("v", v_ymc),) if j == 0 else ()),
                inc=("t",))
            pool.set_mm(bank, tok)
            if j % 4 == 3:
                j0 = j - 3
                vtok = b.emit("vector", lambda e, j0_=j0, bk_=bank, bl_=bl:
                              e.tensor_tensor(
                                  x_tmv[:, j0_:j0_ + 4, bl_ * CB:bl_ * CB + 128],
                                  x_tmv[:, j0_:j0_ + 4, bl_ * CB:bl_ * CB + 128],
                                  ps[bk_].ap().rearrange("p (a c) -> p a c",
                                                         a=4, c=128), ALU.add),
                              wait=pool.wait_mm(bank), inc=("v",))
                pool.set_evac(bank, "v", vtok)
        b.barrier()

    if BUILD_PHASES < 3:
        gd = b.dma("gpsimd",
                   y_out.rearrange("(j p) c -> p j c", p=128),
                   x_tm.ap().rearrange("p (j c) -> p j c", j=NJ, c=D),
                   wait=tuple((s, b.now(s)) for s in ("t", "v", "s")))
        fin = tuple((s, b.now(s)) for s in ("d", "g", "t", "v", "s", "p", "c"))
        b.emit("gpsimd", lambda e: e.nop(), wait=fin)
        with nc.Block() as block:
            b.replay(block, sems)
        es.close()
        return nc

    s_sq3 = v_mx3 = p_mn3 = 0
    for j in range(NJ):
        b.emit("scalar", lambda e, j_=j: e.activation(
            sqv[:, 0:1024], x_tm[:, j_ * D:(j_ + 1) * D], AF.Copy,
            accum_out=st_[:, S_SUMX + j_:S_SUMX + j_ + 1]), inc=("s",))
        s_sq3 = b.emit("scalar", lambda e, j_=j: e.activation(
            sqv[:, 0:1024], x_tm[:, j_ * D:(j_ + 1) * D], AF.Square,
            accum_out=st_[:, S_SUMSQ + j_:S_SUMSQ + j_ + 1]), inc=("s",))
        v_mx3 = b.emit("vector", lambda e, j_=j: e.tensor_reduce(
            st_[:, S_MAX + j_:S_MAX + j_ + 1], x_tm[:, j_ * D:(j_ + 1) * D],
            axis=AX, op=ALU.max), inc=("v",), fence=True)
        p_mn3 = b.emit("vector", lambda e, j_=j: e.tensor_reduce(
            st_[:, S_MIN + j_:S_MIN + j_ + 1], x_tm[:, j_ * D:(j_ + 1) * D],
            axis=AX, op=ALU.min), inc=("v",), fence=True)
    s_fin3 = ln_finalize((("s", s_sq3), ("v", v_mx3)))
    v_lm3 = ln_localmax((("v", p_mn3), ("s", s_fin3)))
    g_cw2 = b.dma("gpsimd", cc2_in.ap()[0:1, 0:8], row1[0:1, 0:8],
                  wait=(("v", v_lm3),))
    c_cc2 = b.emit("gpsimd", lambda e: e.collective_compute(
        "AllReduce", ALU.max, replica_groups=[list(range(NCORE))],
        ins=[cc2_in.ap()], outs=[cc2_out.ap()]),
        wait=(("g", g_cw2),), inc=("c",))
    g_cr2 = b.dma("gpsimd", row1[0:1, 8:16], cc2_out.ap()[0:1, 0:8],
                  wait=(("c", c_cc2),))
    g_qw = b.dma("gpsimd", tbuf[:, 0:8192], qw_spill.ap()[:, :],
                 wait=(("t", b.now("t")),))
    p_sc2 = ffn_scales(g_cr2)
    p_gate = b.emit("vector", lambda e: e.engine_nop(),
                    wait=(("g", g_qw), ("v", p_sc2)), inc=("v",))
    ffn(p_gate, final=True)

    fin = tuple((s, b.now(s)) for s in ("d", "g", "t", "v", "s", "p", "c"))
    b.emit("gpsimd", lambda e: e.nop(), wait=fin)

    with nc.Block() as block:
        b.replay(block, sems)
    es.close()
    return nc


def kernel(L0_attn_x, L0_attn_W1, L0_attn_b1, L0_attn_W2, L0_attn_b2,
           L0_ffn_weight, L0_ffn_bias):
    tb = _tables()
    nc = build_nc()
    wt = np.ascontiguousarray(np.asarray(L0_ffn_weight).T)
    in_maps = []
    for core in range(NCORE):
        m = {"x": np.ascontiguousarray(np.asarray(L0_attn_x)[core]),
             "wt": wt,
             "w2blk": np.ascontiguousarray(
                 np.asarray(L0_attn_W2)[:, core * CB:(core + 1) * CB]),
             "w1": np.ascontiguousarray(np.asarray(L0_attn_W1)),
             "b1": np.ascontiguousarray(
                 np.asarray(L0_attn_b1).reshape(8, 128).T),
             "b2blk": np.ascontiguousarray(
                 np.asarray(L0_attn_b2)[core * CB:(core + 1) * CB]).reshape(128, 1),
             "fbias": np.ascontiguousarray(
                 np.asarray(L0_ffn_bias).reshape(8, 128).T)}
        for k, v in tb.items():
            m["t_" + k] = v
        in_maps.append(m)
    res = run_bass_kernel_spmd(nc, in_maps, core_ids=list(range(NCORE)))
    return np.stack([res.results[i]["y"] for i in range(NCORE)], axis=0)


# version 06 (final)
# speedup vs baseline: 1.0639x; 1.0550x over previous
import numpy as np
from contextlib import ExitStack

import concourse.bass as bass
import concourse.mybir as mybir
from concourse.bass_utils import run_bass_kernel_spmd

F32 = mybir.dt.float32
BF16 = mybir.dt.bfloat16
I8 = mybir.dt.int8
AF = mybir.ActivationFunctionType
ALU = mybir.AluOpType
AX = mybir.AxisListType.X

NCORE = 8
T = 4096
D = 1024
NJ = 32
NB = 8
CB = 128
EPS = 1e-5

DEBUG_TAPS = []
BUILD_PHASES = 99
DUMP = ""
CONV_ORDER = None
DEBUG_MEMSET = 0


def _pos_embedding(seq_len, dim):
    position = np.arange(seq_len, dtype=np.float64)[:, None]
    div_term = np.exp(np.arange(0, dim, 2, dtype=np.float64) *
                      (-np.log(10000.0) / dim))
    ang = position * div_term
    pe = np.zeros((seq_len, dim), dtype=np.float64)
    pe[:, 0::2] = np.sin(ang)
    pe[:, 1::2] = np.cos(ang)
    return pe


def _build_tables():
    tb = {}
    tb["posT"] = np.ascontiguousarray(_pos_embedding(T, 64).T).astype(np.float32)
    t1 = np.arange(64)[:, None]
    k1 = np.arange(65)[None, :]
    ang = 2.0 * np.pi * (t1 * k1) / 128.0
    m1c, m1s = np.cos(ang), -np.sin(ang)
    tb["M1c"] = np.concatenate([m1c, m1c], 0).astype(np.float32)
    tb["M1s"] = np.concatenate([m1s, m1s], 0).astype(np.float32)

    t2 = np.arange(64)[:, None]
    k2 = np.arange(32)[None, :]
    m2p = np.zeros((65, 128, 64))
    m2m = np.zeros((64, 128, 64))
    for g in range(65):
        th = 2.0 * np.pi * (t2 * g / 8192.0 + t2 * k2 / 64.0)
        c, s = np.cos(th), np.sin(th)
        m2p[g, 0:64, 0:32] = c
        m2p[g, 64:128, 0:32] = s
        m2p[g, 0:64, 32:64] = -s
        m2p[g, 64:128, 32:64] = c
        if 1 <= g <= 63:
            thm = 2.0 * np.pi * (t2 * (128 - g) / 8192.0 + t2 * k2 / 64.0)
            cm, sm = np.cos(thm), np.sin(thm)
            m2m[g, 0:64, 0:32] = cm
            m2m[g, 64:128, 0:32] = -sm
            m2m[g, 0:64, 32:64] = -sm
            m2m[g, 64:128, 32:64] = -cm
    tb["M2P"] = m2p.transpose(1, 0, 2).reshape(128, 65 * 64).astype(np.float32)
    tb["M2M"] = m2m[1:64].transpose(1, 0, 2).reshape(128, 63 * 64).astype(np.float32)
    m4 = np.zeros((128, 2))
    sgn = (-1.0) ** np.arange(64)
    m4[0:64, 0] = sgn
    m4[64:128, 1] = sgn
    tb["M4096"] = m4.astype(np.float32)

    t2p = np.arange(64)[None, :]
    k2c = np.arange(32)[:, None]
    phi = 2.0 * np.pi * (k2c * t2p) / 64.0
    c, s = np.cos(phi), np.sin(phi)
    esa = np.zeros((64, 128))
    esa[0:32, 0:64] = c
    esa[32:64, 0:64] = -s
    esa[0:32, 64:128] = s
    esa[32:64, 64:128] = c
    tb["ESA"] = np.concatenate([esa, esa], 0).astype(np.float32)
    e4 = np.zeros((2, 128))
    sg = 0.5 * (-1.0) ** np.arange(64)
    e4[0, 0:64] = sg
    e4[1, 64:128] = sg
    tb["E4096"] = e4.astype(np.float32)

    k1c = np.arange(128)[:, None]
    t1p = np.arange(64)[None, :]
    mbre = np.zeros((64, 128, 64))
    mbim = np.zeros((64, 128, 64))
    for tp in range(64):
        th = 2.0 * np.pi * ((tp + 64.0 * t1p) * k1c) / 8192.0
        mbre[tp] = (2.0 / 8192.0) * np.cos(th)
        mbim[tp] = -(2.0 / 8192.0) * np.sin(th)
    tb["MBre"] = mbre.transpose(1, 0, 2).reshape(128, 64 * 64).astype(np.float32)
    tb["MBim"] = mbim.transpose(1, 0, 2).reshape(128, 64 * 64).astype(np.float32)
    tb["IDN"] = np.eye(128, dtype=np.float32)
    tb["ONES"] = np.ones((1, 128), dtype=np.float32)
    tb["IDNF"] = np.eye(128, dtype=np.float32)
    return tb


_TABLES = None


def _tables():
    global _TABLES
    if _TABLES is None:
        _TABLES = _build_tables()
    return _TABLES


ENGINES = ("sync", "tensor", "vector", "scalar", "gpsimd")


class KB:
    def __init__(self):
        self.ops = {e: [] for e in ENGINES}
        self.cnt = {}
        self._vfence = 0

    def emit(self, eng, fn, wait=(), inc=(), amt=1, fence=False):
        if eng == "vector":
            wait = tuple(wait) + (("v", self.cnt.get("v", 0)),)
            if "v" not in inc:
                inc = tuple(inc) + ("v",)
        wait = tuple((s, c) for (s, c) in wait if c > 0)
        self.ops[eng].append((fn, wait, tuple(inc), amt))
        out = []
        for s in inc:
            self.cnt[s] = self.cnt.get(s, 0) + amt
            out.append(self.cnt[s])
        if eng == "vector" and fence:
            self._vfence = self.cnt["v"]
        return out[0] if len(out) == 1 else (out if out else None)

    def dma(self, eng, dst, src, wait=(), sem=None, slow=False):
        sem = sem or ("g" if eng == "gpsimd" else "d")
        return self.emit(eng, lambda e, d=dst, s=src, sl=slow: e.dma_start(
            out=d, in_=s, allow_slow_non_contiguous=sl),
                         wait=wait, inc=(sem,), amt=16)

    def now(self, sem):
        return self.cnt.get(sem, 0)

    def barrier(self):
        snap = tuple((s, self.now(s)) for s in ("t", "v", "s", "p", "g", "d"))
        own = {"tensor": "t", "vector": "v", "scalar": "s", "gpsimd": "p"}
        for eng in ("tensor", "vector", "scalar", "gpsimd"):
            w = tuple((s, c) for s, c in snap if s != own[eng])
            self.emit(eng, lambda e: e.nop(), wait=w, inc=(own[eng],))

    def replay(self, block, sems):
        for eng_name in ENGINES:
            ops = list(self.ops[eng_name])
            if not ops:
                continue

            def make(ops_):
                def _f(eng):
                    for fn, wait, inc, amt in ops_:
                        for s, c in wait:
                            eng.wait_ge(sems[s], c)
                        instr = fn(eng)
                        for s in inc:
                            instr = instr.then_inc(sems[s], amt)
                return _f
            getattr(block, eng_name)(make(ops))


class PsumPool:
    def __init__(self):
        self.evac = [None] * 8
        self.mm = [0] * 8

    def wait_free(self, bank):
        ev = self.evac[bank]
        return (ev,) if ev else ()

    def set_mm(self, bank, tok):
        self.mm[bank] = tok

    def wait_mm(self, *banks):
        return tuple(("t", self.mm[bk]) for bk in banks)

    def set_evac(self, bank, sem, tok):
        self.evac[bank] = (sem, tok)


def build_nc():
    tb = _tables()
    nc = bass.Bass("TRN2", target_bir_lowering=False,
                   detect_race_conditions=False)

    x_in = nc.dram_tensor("x", [T, D], F32, kind="ExternalInput").ap()
    wt_in = nc.dram_tensor("wt", [D, D], F32, kind="ExternalInput").ap()
    w2_in = nc.dram_tensor("w2blk", [D, CB], F32, kind="ExternalInput").ap()
    w1_in = nc.dram_tensor("w1", [64, D], F32, kind="ExternalInput").ap()
    b1_in = nc.dram_tensor("b1", [128, 8], F32, kind="ExternalInput").ap()
    b2_in = nc.dram_tensor("b2blk", [128, 1], F32, kind="ExternalInput").ap()
    fb_in = nc.dram_tensor("fbias", [128, 8], F32, kind="ExternalInput").ap()
    y_out = nc.dram_tensor("y", [T, D], F32, kind="ExternalOutput").ap()
    tbl_in = {k: nc.dram_tensor("t_" + k, list(tb[k].shape), F32,
                                kind="ExternalInput").ap() for k in tb}
    taps = {name: nc.dram_tensor("dbg_" + name, list(shape), F32,
                                 kind="ExternalOutput").ap()
            for name, shape in DEBUG_TAPS}

    cc1_in = nc.dram_tensor("cc1_in", [1, 8], F32)
    cc1_out = nc.dram_tensor("cc1_out", [1, 8], F32)
    cc2_in = nc.dram_tensor("cc2_in", [1, 8], F32)
    cc2_out = nc.dram_tensor("cc2_out", [1, 8], F32)
    fbmy = nc.dram_tensor("fblob_my", [256, 8192], BF16)
    fball = nc.dram_tensor("fblob_all", [NCORE * 256, 8192], BF16)
    qw_spill = nc.dram_tensor("qw_spill", [128, 8192], BF16)
    sshb = nc.dram_tensor("sshb", [128, 8192], BF16)

    es = ExitStack()
    SBT = lambda name, shape, dt=BF16: es.enter_context(
        nc.sbuf_tensor(name, shape, dt))

    x_tm = SBT("x_tm", [128, NJ * D])
    big = SBT("big", [128, 32768])
    wbuf = SBT("wbuf", [128, 8192])
    cbuf = SBT("cbuf", [128, 8192])
    dbuf = SBT("dbuf", [128, 8192])
    tbuf = SBT("tbuf", [128, 8192])
    nst = SBT("nst", [128, 512])
    sqb = SBT("sqb", [128, 8])
    m1c = SBT("m1c_t", [128, 65])
    m1s = SBT("m1s_t", [128, 65])
    esa = SBT("esa_t", [128, 128])
    e4096 = SBT("e4096_t", [2, 128])
    m4096 = SBT("m4096_t", [128, 2])
    idn = SBT("idn_t", [128, 128])
    ones1 = SBT("ones_t", [1, 128], F32)
    idnf = SBT("idnf_t", [128, 128], F32)
    s4096 = SBT("s4096", [2, CB])
    bias8 = SBT("bias8", [128, 8], F32)
    b1s = SBT("b1s", [128, 8], F32)
    b2s = SBT("b2s", [128, 1], F32)
    st_ = SBT("stats", [128, 320], F32)
    row1 = SBT("row1", [1, 64], F32)
    sc1 = SBT("sc1", [128, 8], F32)
    S_SUMX, S_SUMSQ, S_MAX, S_MIN, S_M, S_R, S_QA, S_QB, S_T1, S_T2 = \
        0, 32, 64, 96, 128, 160, 192, 224, 256, 288

    ps = [es.enter_context(nc.psum_tensor("psb%d" % i, [128, 512], F32))
          for i in range(8)]
    sems = {s: es.enter_context(nc.semaphore("sem_" + s))
            for s in ("d", "g", "t", "v", "s", "p", "c")}

    b = KB()
    pool = PsumPool()
    bigv = big.ap()
    big32 = bigv.bitcast(F32)
    big8 = bigv.bitcast(I8)
    wb32 = wbuf.ap().bitcast(F32)
    sqv = big32[:, 10240:11264]
    ostv = big32[:, 12288:14336]
    xstv = big32[:, 14336:16384]
    qstv = big8[:, 40960:45056]
    qbv = bigv[:, 16384:20480]
    ZTC = 16384

    def tap(name, src, wait):
        if name in taps:
            b.dma("gpsimd", taps[name][:, :], src, wait=wait)

    g_ms = b.emit("gpsimd", lambda e: e.memset(row1[0:1, 0:64], 0.0), inc=("p",),
                  amt=1)
    g_tbl = 0
    for name, dst in (("M1c", m1c), ("M1s", m1s), ("ESA", esa), ("M4096", m4096),
                      ("IDN", idn), ("E4096", e4096), ("ONES", ones1),
                      ("IDNF", idnf)):
        sh = tb[name].shape
        g_tbl = b.dma("gpsimd", dst[0:sh[0], 0:sh[1]], tbl_in[name][:, :])
    g_posT = b.dma("gpsimd", cbuf[0:64, 0:4096], tbl_in["posT"][:, :])
    g_w1 = b.dma("gpsimd", cbuf[0:64, 4096:5120], w1_in[:, :])
    g_w2 = b.dma("gpsimd",
                 cbuf[:, 5120:6144].rearrange("p (rt c) -> p rt c", rt=8, c=CB),
                 w2_in.rearrange("(rt p) c -> p rt c", rt=8, p=128))
    g_b1 = b.dma("sync", b1s[:, 0:8], b1_in[:, :])
    g_fb0 = b.dma("sync", bias8[:, 0:8], fb_in[:, :])
    g_b2 = b.dma("sync", b2s[:, 0:1], b2_in[:, :])
    v_fb = b.emit("vector", lambda e: e.engine_nop(),
                  wait=(("d", g_fb0), ("p", g_ms)), inc=("v",))


    def bcast(src_cols, dst_cols, wait=()):
        a, bb = src_cols
        c, d = dst_cols
        n = bb - a
        tt = b.emit("tensor", lambda e: e.matmul(
            ps[7][:, 0:n], ones1[0:1, :], row1[0:1, a:bb], start=True, stop=True),
            wait=tuple(wait) + pool.wait_free(7) + (("g", g_tbl),), inc=("t",))
        pool.set_mm(7, tt)
        vt = b.emit("vector", lambda e: e.tensor_copy(
            sc1[:, c:d], ps[7][:, 0:n]), wait=(("t", tt),), inc=("v",))
        pool.set_evac(7, "v", vt)
        return vt

    v_abs = 0
    d_wh = 0
    for half in range(2):
        for kt in range(4):
            d_wh = b.dma("sync", wb32[:, kt * 1024:kt * 1024 + 1024],
                         wt_in[(half * 4 + kt) * 128:(half * 4 + kt + 1) * 128, :],
                         wait=(("v", v_abs),) if half == 1 and kt == 0 else ())
        for kt in range(4):
            v_abs = b.emit("vector", lambda e, h_=half, kt_=kt: e.tensor_reduce(
                st_[:, S_T1 + 4 * h_ + kt_:S_T1 + 4 * h_ + kt_ + 1],
                wb32[:, kt_ * 1024:kt_ * 1024 + 1024], axis=AX, op=ALU.add,
                apply_absolute_value=True),
                wait=(("d", d_wh),) if kt == 0 else (), inc=("v",))
    b.emit("vector", lambda e: e.tensor_reduce(
        st_[:, S_T2:S_T2 + 1], st_[:, S_T1:S_T1 + 8], axis=AX, op=ALU.add),
        wait=(("v", v_abs),), inc=("v",), fence=True)
    v_c = b.now("v")
    t_tp = b.emit("tensor", lambda e: e.matmul(
        ps[0][0:1, 0:128], st_[:, S_T2:S_T2 + 1], idnf[:, :], start=True,
        stop=True),
        wait=(("v", v_c), ("g", g_tbl)), inc=("t",))
    pool.set_mm(0, t_tp)
    v_sum = b.emit("vector", lambda e: e.tensor_reduce(
        row1[0:1, 0:1], ps[0][0:1, 0:128], axis=AX, op=ALU.add),
        wait=(("t", t_tp),), inc=("v",), fence=True)
    pool.set_evac(0, "v", v_sum)
    v_thr = b.emit("vector", lambda e: e.tensor_scalar(
        row1[0:1, 1:2], row1[0:1, 0:1], 0.5 / float(D * D), None, ALU.mult),
        inc=("v",))
    v_ws = b.emit("vector", lambda e: e.tensor_scalar(
        row1[0:1, 2:3], row1[0:1, 0:1], 1.0 / float(D * D), None, ALU.mult),
        inc=("v",))

    v_bc0 = bcast((1, 2), (0, 1), wait=(("v", v_thr),))
    q_tok = 0
    for half in range(2):
        d_wh = 0
        for kt in range(4):
            d_wh = b.dma("sync", wb32[:, kt * 1024:kt * 1024 + 1024],
                         wt_in[(half * 4 + kt) * 128:(half * 4 + kt + 1) * 128, :],
                         wait=(("v", q_tok if half else v_thr),))
        for kt in range(4):
            sl = slice(kt * 1024, kt * 1024 + 1024)
            qsl = slice((half * 4 + kt) * 1024, (half * 4 + kt) * 1024 + 1024)
            b.emit("vector", lambda e, sl_=sl: e.tensor_scalar(
                big32[:, 8192:9216], wb32[:, sl_], sc1[:, 0:1], None, ALU.is_gt),
                wait=((("d", d_wh),) if kt == 0 else ()) +
                     ((("v", v_bc0),) if half == 0 and kt == 0 else ()),
                inc=("v",))
            b.emit("vector", lambda e, sl_=sl: e.tensor_scalar(
                big32[:, 9216:10240], wb32[:, sl_], -1.0, sc1[:, 0:1], ALU.mult,
                ALU.is_gt), inc=("v",))
            q_tok = b.emit("vector", lambda e, q_=qsl: e.tensor_tensor(
                tbuf[:, q_], big32[:, 8192:9216], big32[:, 9216:10240],
                ALU.subtract), inc=("v",))
    d_qsp = b.dma("sync", qw_spill.ap()[:, :], tbuf[:, 0:8192],
                  wait=(("v", q_tok),))

    xin_t = x_in.rearrange("(j p) c -> j p c", p=128)
    cast_toks = [0] * NJ
    for j in range(NJ):
        h = (j % 2) * 1024
        d_x = b.dma("sync", xstv[:, h:h + 1024], xin_t[j, :, :],
                    wait=(("s", cast_toks[j - 2]),) if j >= 2 else ())
        cast_toks[j] = b.emit("scalar", lambda e, j_=j, h_=h: e.activation(
            x_tm[:, j_ * D:(j_ + 1) * D], xstv[:, h_:h_ + 1024], AF.Copy,
            accum_out=st_[:, S_SUMX + j_:S_SUMX + j_ + 1]),
            wait=(("d", d_x),), inc=("s",))
    s_xcast = cast_toks[-1]
    s_sq = v_mx = p_mn = 0
    for j in range(NJ):
        w0 = (("s", s_xcast),) if j == 0 else ()
        s_sq = b.emit("scalar", lambda e, j_=j: e.activation(
            sqv[:, 0:1024], x_tm[:, j_ * D:(j_ + 1) * D], AF.Square,
            accum_out=st_[:, S_SUMSQ + j_:S_SUMSQ + j_ + 1]), wait=w0, inc=("s",))
        v_mx = b.emit("vector", lambda e, j_=j: e.tensor_reduce(
            st_[:, S_MAX + j_:S_MAX + j_ + 1], x_tm[:, j_ * D:(j_ + 1) * D],
            axis=AX, op=ALU.max), wait=w0, inc=("v",), fence=True)
        p_mn = b.emit("vector", lambda e, j_=j: e.tensor_reduce(
            st_[:, S_MIN + j_:S_MIN + j_ + 1], x_tm[:, j_ * D:(j_ + 1) * D],
            axis=AX, op=ALU.min), wait=w0, inc=("v",), fence=True)

    def ln_finalize(wait):
        b.emit("vector", lambda e: e.tensor_scalar(
            st_[:, S_M:S_M + 32], st_[:, S_SUMX:S_SUMX + 32], 1.0 / D, None,
            ALU.mult), wait=wait, inc=("v",))
        b.emit("vector", lambda e: e.tensor_tensor(
            st_[:, S_T1:S_T1 + 32], st_[:, S_M:S_M + 32], st_[:, S_M:S_M + 32],
            ALU.mult), inc=("v",))
        b.emit("vector", lambda e: e.tensor_scalar(
            st_[:, S_T2:S_T2 + 32], st_[:, S_SUMSQ:S_SUMSQ + 32], 1.0 / D, None,
            ALU.mult), inc=("v",))
        b.emit("vector", lambda e: e.tensor_tensor(
            st_[:, S_T2:S_T2 + 32], st_[:, S_T2:S_T2 + 32], st_[:, S_T1:S_T1 + 32],
            ALU.subtract), inc=("v",))
        veps = b.emit("vector", lambda e: e.tensor_scalar(
            st_[:, S_T2:S_T2 + 32], st_[:, S_T2:S_T2 + 32], EPS, None, ALU.add),
            inc=("v",))
        vtok = b.emit("vector", lambda e: e.reciprocal(
            st_[:, S_T1:S_T1 + 32], st_[:, S_T2:S_T2 + 32]),
            wait=(("v", veps),), inc=("v",), fence=True)
        return b.emit("scalar", lambda e: e.activation(
            st_[:, S_R:S_R + 32], st_[:, S_T1:S_T1 + 32], AF.Sqrt),
            wait=(("v", vtok),), inc=("s",))

    def ln_localmax(wait):
        b.emit("vector", lambda e: e.tensor_tensor(
            st_[:, S_T1:S_T1 + 32], st_[:, S_MAX:S_MAX + 32], st_[:, S_M:S_M + 32],
            ALU.subtract), wait=wait, inc=("v",))
        b.emit("vector", lambda e: e.tensor_tensor(
            st_[:, S_T2:S_T2 + 32], st_[:, S_M:S_M + 32], st_[:, S_MIN:S_MIN + 32],
            ALU.subtract), inc=("v",))
        b.emit("vector", lambda e: e.tensor_tensor(
            st_[:, S_T1:S_T1 + 32], st_[:, S_T1:S_T1 + 32], st_[:, S_T2:S_T2 + 32],
            ALU.max), inc=("v",))
        vmm = b.emit("vector", lambda e: e.tensor_tensor(
            st_[:, S_T1:S_T1 + 32], st_[:, S_T1:S_T1 + 32], st_[:, S_R:S_R + 32],
            ALU.mult), inc=("v",))
        b.emit("vector", lambda e: e.tensor_reduce(
            st_[:, S_T2:S_T2 + 1], st_[:, S_T1:S_T1 + 32], axis=AX, op=ALU.max),
            wait=(("v", vmm),), inc=("v",), fence=True)
        vc = b.now("v")
        tt = b.emit("tensor", lambda e: e.matmul(
            ps[7][0:1, 0:128], st_[:, S_T2:S_T2 + 1], idnf[:, :], start=True,
            stop=True),
            wait=(("v", vc),) + pool.wait_free(7), inc=("t",))
        pool.set_mm(7, tt)
        vtok = b.emit("vector", lambda e: e.tensor_reduce(
            row1[0:1, 4:5], ps[7][0:1, 0:128], axis=AX, op=ALU.max),
            wait=(("t", tt),), inc=("v",), fence=True)
        pool.set_evac(7, "v", vtok)
        return vtok

    s_fin1 = ln_finalize((("s", s_sq), ("v", v_mx)))
    v_lm1 = ln_localmax((("v", p_mn), ("s", s_fin1)))
    g_cw = b.dma("gpsimd", cc1_in.ap()[0:1, 0:8], row1[0:1, 0:8],
                 wait=(("v", v_lm1), ("p", g_ms)))
    c_cc1 = b.emit("gpsimd", lambda e: e.collective_compute(
        "AllReduce", ALU.max, replica_groups=[list(range(NCORE))],
        ins=[cc1_in.ap()], outs=[cc1_out.ap()]),
        wait=(("g", g_cw),), inc=("c",))
    g_cr1 = b.dma("gpsimd", row1[0:1, 8:16], cc1_out.ap()[0:1, 0:8],
                  wait=(("c", c_cc1),))

    def ffn_scales(cc_read_tok):
        v16 = b.emit("vector", lambda e: e.tensor_scalar(
            row1[0:1, 16:17], row1[0:1, 12:13], 1.0 / 127.0, None, ALU.mult),
            wait=(("g", cc_read_tok),), inc=("v",))
        vrc = b.emit("vector", lambda e: e.reciprocal(
            row1[0:1, 17:18], row1[0:1, 16:17]), wait=(("v", v16),), inc=("v",), fence=True)
        bcast((17, 18), (1, 2), wait=(("v", vrc),))
        b.emit("vector", lambda e: e.tensor_scalar(
            st_[:, S_QA:S_QA + 32], st_[:, S_R:S_R + 32], sc1[:, 1:2], None,
            ALU.mult), inc=("v",))
        b.emit("vector", lambda e: e.tensor_tensor(
            st_[:, S_QB:S_QB + 32], st_[:, S_M:S_M + 32], st_[:, S_QA:S_QA + 32],
            ALU.mult), inc=("v",))
        b.emit("vector", lambda e: e.tensor_scalar(
            st_[:, S_QB:S_QB + 32], st_[:, S_QB:S_QB + 32], -1.0, None, ALU.mult),
            inc=("v",))
        b.emit("vector", lambda e: e.tensor_tensor(
            row1[0:1, 18:19], row1[0:1, 16:17], row1[0:1, 2:3], ALU.mult),
            wait=(("v", v_ws),), inc=("v",))
        return bcast((18, 19), (2, 3), wait=(("v", b.now("v")),))

    p_sc1 = ffn_scales(g_cr1)

    out_state = {"prev": [0, 0]}

    def ffn(gate_v, final):
        prev_cv = 0
        prev_qt_mm = 0
        prev_mm = 0
        prev_tb = 0
        for tc in range(8):
            jb = tc * 4
            vq = 0
            for jj in range(4):
                j = jb + jj
                vq = b.emit("vector", lambda e, j_=j, jj_=jj: e.tensor_scalar(
                    qstv[:, jj_ * D:(jj_ + 1) * D], x_tm[:, j_ * D:(j_ + 1) * D],
                    st_[:, S_QA + j_:S_QA + j_ + 1],
                    st_[:, S_QB + j_:S_QB + j_ + 1], ALU.mult, ALU.add),
                    wait=((("v", gate_v),) if tc == 0 and jj == 0 else ()) +
                         ((("s", prev_cv),) if jj == 0 else ()),
                    inc=("v",))
            s_cv = b.emit("scalar", lambda e: e.activation(
                qbv[:, 0:4 * D], qstv[:, 0:4 * D], AF.Copy),
                wait=(("v", vq), ("t", prev_qt_mm)), inc=("s",))
            prev_cv = s_cv
            for ct in range(8):
                bank = ct % 2
                for jj in range(4):
                    tok = b.emit("tensor", lambda e, ct_=ct, jj_=jj, bk_=bank:
                                 e.matmul(ps[bk_][:, jj_ * 128:jj_ * 128 + 128],
                                          qbv[:, jj_ * D + ct_ * 128:
                                              jj_ * D + ct_ * 128 + 128],
                                          idn[:, :], start=True, stop=True),
                                 wait=(pool.wait_free(bank) + (("s", s_cv),))
                                      if jj == 0 else (),
                                 inc=("t",))
                    pool.set_mm(bank, tok)
                vtok = b.emit("vector", lambda e, ct_=ct, bk_=bank: e.tensor_copy(
                    bigv[:, ct_ * 512:ct_ * 512 + 512], ps[bk_][:, 0:512]),
                    wait=pool.wait_mm(bank) +
                         ((("t", prev_mm),) if ct == 0 else ()),
                    inc=("v",))
                pool.set_evac(bank, "v", vtok)
            prev_qt_mm = pool.mm[1]
            v_qt = b.now("v")
            for ot in range(8):
                bank = 2 + ot % 2
                for kt in range(8):
                    tok = b.emit("tensor", lambda e, ot_=ot, kt_=kt, bk_=bank:
                                 e.matmul(ps[bk_][:, 0:512],
                                          tbuf[:, kt_ * 1024 + ot_ * 128:
                                               kt_ * 1024 + ot_ * 128 + 128],
                                          bigv[:, kt_ * 512:kt_ * 512 + 512],
                                          start=(kt_ == 0), stop=(kt_ == 7)),
                                 wait=(pool.wait_free(bank) + (("v", v_qt),))
                                      if kt == 0 else (),
                                 inc=("t",))
                    pool.set_mm(bank, tok)
                stok = b.emit("scalar", lambda e, ot_=ot, bk_=bank: e.activation(
                    bigv[:, 4096 + ot_ * 512:4096 + ot_ * 512 + 512],
                    ps[bk_][:, 0:512], AF.Identity, scale=sc1[:, 2:3],
                    bias=bias8[:, ot_:ot_ + 1]),
                    wait=pool.wait_mm(bank) +
                         ((("t", prev_tb),) if ot == 0 else ()),
                    inc=("s",))
                pool.set_evac(bank, "s", stok)
            prev_mm = max(pool.mm[2], pool.mm[3])
            s_fo = b.now("s")
            for jj in range(4):
                j = jb + jj
                slot = (j % 2) * D
                vtok = 0
                for oth in range(2):
                    bank = 4 + oth
                    for o4 in range(4):
                        ot = oth * 4 + o4
                        tok = b.emit("tensor",
                                     lambda e, jj_=jj, ot_=ot, o4_=o4, bk_=bank:
                                     e.matmul(
                                         ps[bk_][:, o4_ * 128:o4_ * 128 + 128],
                                         bigv[:, 4096 + ot_ * 512 + jj_ * 128:
                                              4096 + ot_ * 512 + jj_ * 128 + 128],
                                         idn[:, :], start=True, stop=True),
                                     wait=(pool.wait_free(bank) + (("s", s_fo),))
                                          if o4 == 0 else (),
                                     inc=("t",))
                        pool.set_mm(bank, tok)
                    if not final:
                        vtok = b.emit("vector",
                                      lambda e, j_=j, o_=oth, bk_=bank:
                                      e.scalar_tensor_tensor(
                                          x_tm[:, j_ * D + o_ * 512:
                                               j_ * D + o_ * 512 + 512],
                                          ps[bk_][:, 0:512], 0.5,
                                          x_tm[:, j_ * D + o_ * 512:
                                               j_ * D + o_ * 512 + 512],
                                          ALU.mult, ALU.add),
                                      wait=pool.wait_mm(bank), inc=("v",))
                    else:
                        prev = out_state["prev"][j % 2]
                        vtok = b.emit("vector",
                                      lambda e, j_=j, o_=oth, bk_=bank, sl_=slot:
                                      e.scalar_tensor_tensor(
                                          ostv[:, sl_ + o_ * 512:
                                               sl_ + o_ * 512 + 512],
                                          ps[bk_][:, 0:512], 0.5,
                                          x_tm[:, j_ * D + o_ * 512:
                                               j_ * D + o_ * 512 + 512],
                                          ALU.mult, ALU.add),
                                      wait=pool.wait_mm(bank) +
                                           ((("d", prev),) if oth == 0 and prev
                                            else ()),
                                      inc=("v",))
                    pool.set_evac(bank, "v", vtok)
                if final:
                    od = b.dma("sync", y_out[j * 128:(j + 1) * 128, :],
                               ostv[:, slot:slot + D], wait=(("v", vtok),))
                    out_state["prev"][j % 2] = od
            prev_tb = max(pool.mm[4], pool.mm[5])

    ffn(p_sc1, final=False)
    b.barrier()

    if BUILD_PHASES < 2:
        if DUMP == "q":
            yv = y_out.rearrange("(a p) c -> a p c", p=128)
            b.dma("gpsimd", yv[0:4, :, :].rearrange("a p c -> p a c"),
                  qbv[:, 0:4096],
                  wait=tuple((s, b.now(s)) for s in ("t", "v", "s")))
            b.dma("gpsimd", yv[4:8, :, :].rearrange("a p c -> p a c"),
                  bigv[:, 0:4096])
            b.dma("gpsimd", yv[8:12, :, :].rearrange("a p c -> p a c"),
                  bigv[:, 4096:8192])
        if DUMP == "qw":
            b.dma("gpsimd",
                  y_out.rearrange("(a p) c -> a p c", p=128)[0:8, :, :]
                       .rearrange("a p c -> p a c"),
                  tbuf.ap().rearrange("p (a c) -> p a c", a=8, c=1024),
                  wait=tuple((s, b.now(s)) for s in ("t", "v", "s")))
        if DUMP == "stats":
            b.dma("gpsimd", y_out[0:128, 0:320], st_[:, 0:320],
                  wait=tuple((s, b.now(s)) for s in ("t", "v", "s")))
            b.dma("gpsimd", y_out[128:256, 0:3], sc1[:, 0:3])
            b.dma("gpsimd", y_out[256:257, 0:64], row1[0:1, 0:64])
        if DUMP == "":
            w0 = tuple((s, b.now(s)) for s in ("t", "v", "s"))
            for j in range(NJ):
                b.dma("gpsimd", y_out[j * 128:(j + 1) * 128, :],
                      x_tm[:, j * D:(j + 1) * D], wait=w0 if j == 0 else ())
        fin = tuple((s, b.now(s)) for s in ("d", "g", "t", "v", "s", "p", "c"))
        b.emit("gpsimd", lambda e: e.nop(), wait=fin)
        with nc.Block() as block:
            b.replay(block, sems)
        es.close()
        return nc

    def fwd_fft(m2t, p_off, m_off, wait_mid):
        for u in range(32):
            bank = 4 + (u // 4) % 2
            tok = b.emit("tensor", lambda e, u_=u, bk_=bank: e.matmul(
                ps[bk_][:, (u_ % 4) * 128:(u_ % 4) * 128 + 128],
                dbuf[:, u_ * 128:u_ * 128 + 128], idn[:, :],
                start=True, stop=True),
                wait=((pool.wait_free(bank) if u % 4 == 0 else ()) +
                      (wait_mid if u == 0 else ())),
                inc=("t",))
            pool.set_mm(bank, tok)
            if u % 4 == 3:
                vtok = b.emit("scalar", lambda e, u_=u, bk_=bank: e.copy(
                    dbuf[:, 4096 + (u_ - 3) * 128:4096 + (u_ + 1) * 128],
                    ps[bk_][:, 0:512]),
                    wait=pool.wait_mm(bank), inc=("s",))
                pool.set_evac(bank, "s", vtok)
        v_xnf = ("s", b.now("s"))
        zview = big[:, 0:16384].rearrange("p (c ri t2h par) -> p ri par t2h c",
                                          c=128, ri=2, t2h=32, par=2)
        for u in range(8):
            for (bank, r0, mt) in ((0, 0, m1c), (1, 64, m1c),
                                   (2, 0, m1s), (3, 64, m1s)):
                tok = b.emit("tensor",
                             lambda e, u_=u, bk_=bank, r_=r0, m_=mt: e.matmul(
                                 ps[bk_][0:65, 0:512], m_[r_:r_ + 64, :],
                                 dbuf[r_:r_ + 64,
                                      4096 + u_ * 512:4096 + u_ * 512 + 512],
                                 start=True, stop=True, tile_position=(r_, 0)),
                             wait=(pool.wait_free(bank) +
                                   ((v_xnf,) if u == 0 else ())),
                             inc=("t",))
                pool.set_mm(bank, tok)
            for (bank, ri, par) in ((0, 0, 0), (1, 0, 1), (2, 1, 0), (3, 1, 1)):
                vtok = b.emit("scalar",
                              lambda e, u_=u, bk_=bank, ri_=ri, pr_=par:
                              e.copy(
                                  zview[0:65, ri_, pr_, 4 * u_:4 * u_ + 4, :],
                                  ps[bk_].ap().rearrange(
                                      "p (a c) -> p a c", a=4, c=128)[0:65]),
                              wait=pool.wait_mm(bank), inc=("s",))
                pool.set_evac(bank, "s", vtok)
        v_z = ("s", b.now("s"))
        for c in range(128):
            bank = (c // 4) % 2
            tok = b.emit("tensor", lambda e, c_=c, bk_=bank: e.matmul(
                ps[bk_][0:128, (c_ % 4) * 65:(c_ % 4) * 65 + 65],
                bigv[0:65, c_ * 128:c_ * 128 + 128], idn[0:65, 0:65],
                start=True, stop=True),
                wait=((pool.wait_free(bank) if c % 4 == 0 else ()) +
                      ((v_z,) if c == 0 else ())),
                inc=("t",))
            pool.set_mm(bank, tok)
            if c % 4 == 3:
                vtok = b.emit("scalar", lambda e, c_=c, bk_=bank: e.copy(
                    bigv[:, ZTC + (c_ - 3) * 65:ZTC + (c_ + 1) * 65],
                    ps[bk_][:, 0:260]),
                    wait=pool.wait_mm(bank), inc=("s",))
                pool.set_evac(bank, "s", vtok)
        v_zt = ("s", b.now("s"))
        ztv = big[:, ZTC:ZTC + 8320].rearrange("p (cc g) -> p g cc",
                                               cc=128, g=65)

        def zt_rhs(g):
            return ztv[:, g, :]
        for g in range(65):
            bank = 6 + g % 2
            wmm = pool.wait_free(bank) + ((v_zt,) if g <= 1 else ())
            if g < 64:
                tok = b.emit("tensor", lambda e, g_=g, bk_=bank: e.matmul(
                    ps[bk_][0:64, 0:128],
                    m2t[:, p_off + g_ * 64:p_off + g_ * 64 + 64],
                    zt_rhs(g_), start=True, stop=True),
                    wait=wmm, inc=("t",))
            else:
                tok = b.emit("tensor", lambda e, g_=g, bk_=bank: e.matmul(
                    ps[bk_][64:128, 0:128],
                    m2t[:, p_off + g_ * 64:p_off + g_ * 64 + 64],
                    zt_rhs(g_), start=True, stop=True, tile_position=(0, 64)),
                    wait=wmm, inc=("t",))
            pool.set_mm(bank, tok)
            if 1 <= g <= 63:
                tok = b.emit("tensor", lambda e, g_=g, bk_=bank: e.matmul(
                    ps[bk_][64:128, 0:128],
                    m2t[:, m_off + (g_ - 1) * 64:m_off + (g_ - 1) * 64 + 64],
                    zt_rhs(g_), start=True, stop=True, tile_position=(0, 64)),
                    inc=("t",))
                pool.set_mm(bank, tok)
            if g == 0:
                tok = b.emit("tensor", lambda e: e.matmul(
                    ps[5][0:2, 0:128], m4096[:, 0:2], zt_rhs(0),
                    start=True, stop=True),
                    wait=pool.wait_free(5), inc=("t",))
                pool.set_mm(5, tok)
                vtok5 = b.emit("scalar", lambda e: e.copy(
                    s4096[0:2, :], ps[5][0:2, 0:128]),
                    wait=pool.wait_mm(5), inc=("s",))
                pool.set_evac(5, "s", vtok5)
            if g < 64:
                vtok = b.emit("scalar", lambda e, g_=g, bk_=bank: e.copy(
                    cbuf[0:64, g_ * 128:g_ * 128 + 128], ps[bk_][0:64, 0:128]),
                    wait=pool.wait_mm(bank), inc=("s",))
            else:
                vtok = b.emit("scalar", lambda e, bk_=bank: e.copy(
                    cbuf[64:128, 0:128], ps[bk_][64:128, 0:128]),
                    wait=pool.wait_mm(bank), inc=("s",))
            if 1 <= g <= 63:
                vtok = b.emit("scalar", lambda e, g_=g, bk_=bank: e.copy(
                    cbuf[64:128, (64 - g_) * 128:(64 - g_) * 128 + 128],
                    ps[bk_][64:128, 0:128]), inc=("s",))
            pool.set_evac(bank, "s", vtok)
        return b.now("s")

    for rt in range(8):
        for u in range(8):
            bank = u % 2
            tok = b.emit("tensor", lambda e, rt_=rt, u_=u, bk_=bank: e.matmul(
                ps[bk_][:, 0:512],
                cbuf[0:64, 4096 + rt_ * 128:4096 + rt_ * 128 + 128],
                cbuf[0:64, u_ * 512:u_ * 512 + 512], start=True, stop=True),
                wait=(pool.wait_free(bank) +
                      (((("g", g_posT), ("g", g_w1), ("d", g_b1)))
                       if rt == 0 and u == 0 else ())),
                inc=("t",))
            pool.set_mm(bank, tok)
            stok = b.emit("scalar", lambda e, rt_=rt, u_=u, bk_=bank: e.activation(
                bigv[:, rt_ * 4096 + u_ * 512:rt_ * 4096 + u_ * 512 + 512],
                ps[bk_][:, 0:512], AF.Gelu, bias=b1s[:, rt_:rt_ + 1]),
                wait=pool.wait_mm(bank), inc=("s",))
            pool.set_evac(bank, "s", stok)
    for u in range(8):
        bank = 2 + u % 2
        for rt in range(8):
            tok = b.emit("tensor", lambda e, rt_=rt, u_=u, bk_=bank: e.matmul(
                ps[bk_][:, 0:512], cbuf[:, 5120 + rt_ * 128:5120 + rt_ * 128 + 128],
                bigv[:, rt_ * 4096 + u_ * 512:rt_ * 4096 + u_ * 512 + 512],
                start=(rt_ == 0), stop=(rt_ == 7)),
                wait=((pool.wait_free(bank) +
                       ((("g", g_w2), ("d", g_b2), ("s", b.now("s")))
                        if u == 0 else ())) if rt == 0 else ()),
                inc=("t",))
            pool.set_mm(bank, tok)
        stok = b.emit("scalar", lambda e, u_=u, bk_=bank: e.activation(
            dbuf[:, 0:4096].rearrange("p (t2 t1) -> p t2 t1", t2=64, t1=64)
                [:, :, 8 * u_:8 * u_ + 8],
            ps[bk_].ap().rearrange("p (a t2) -> p t2 a", a=8, t2=64),
            AF.Identity, bias=b2s[:, 0:1]),
            wait=pool.wait_mm(bank), inc=("s",))
        pool.set_evac(bank, "s", stok)
    s_fmid = b.now("s")
    b.barrier()
    g_m2p1 = b.dma("gpsimd", bigv[:, 24576:24576 + 4160], tbl_in["M2P"][:, :])
    g_m2m1 = b.dma("gpsimd", bigv[:, 28736:28736 + 4032], tbl_in["M2M"][:, :])
    b.ops["gpsimd"] = b.ops["gpsimd"][:-2]
    b.cnt["g"] -= 32
    g_m2p1 = b.dma("gpsimd", tbuf[:, 0:4160], tbl_in["M2P"][:, :],
                   wait=(("d", d_qsp),))
    g_m2m1 = b.dma("gpsimd", tbuf[:, 4160:4160 + 4032], tbl_in["M2M"][:, :])
    if DUMP == "fmid":
        b.dma("gpsimd", y_out[1024:1536, :].rearrange("(a p) c -> a p c", a=4, p=128)
              .rearrange("a p c -> p a c"),
              dbuf[:, 0:4096].rearrange("p (a c) -> p a c", a=4, c=1024),
              wait=tuple((s, b.now(s)) for s in ("t", "v", "s")))
        b.barrier()
    s4f_tok = fwd_fft(tbuf, 0, 4160,
                    (("s", s_fmid), ("g", g_m2m1), ("g", g_m2p1)))
    if DUMP == "fmid":
        b.dma("gpsimd", y_out[0:1024, :].rearrange("(a p) c -> a p c", a=8, p=128)
              .rearrange("a p c -> p a c"),
              cbuf.ap().rearrange("p (a c) -> p a c", a=8, c=1024),
              wait=tuple((s, b.now(s)) for s in ("t", "v", "s")))
        b.dma("gpsimd", y_out[1536:2048, :].rearrange("(a p) c -> a p c", a=4, p=128)
              .rearrange("a p c -> p a c"),
              dbuf[:, 4096:8192].rearrange("p (a c) -> p a c", a=4, c=1024))
        b.barrier()
    v_h0a = b.emit("vector", lambda e: e.tensor_scalar(
        cbuf[0:1, 0:128], cbuf[0:1, 0:128], 0.5, None, ALU.mult),
        wait=(("s", s4f_tok),), inc=("v",))
    v_h0b = b.emit("vector", lambda e: e.tensor_scalar(
        cbuf[32:33, 0:128], cbuf[32:33, 0:128], 0.5, None, ALU.mult), inc=("v",))
    v_ng1 = b.emit("vector", lambda e: e.tensor_scalar(
        bigv[32:64, 0:8192], cbuf[32:64, 0:8192], -1.0, None, ALU.mult),
        wait=(("v", v_h0b),), inc=("v",))
    v_ng2 = b.emit("vector", lambda e: e.tensor_scalar(
        bigv[96:128, 0:8192], cbuf[96:128, 0:8192], -1.0, None, ALU.mult),
        inc=("v",))
    fbv = fbmy.ap().rearrange("(two p) f -> two p f", two=2, p=128)
    g_bl = 0
    for (rd, rs, two, neg) in (
            ((0, 32), (0, 32), 0, False), ((32, 64), (0, 32), 0, False),
            ((64, 96), (64, 96), 0, False), ((96, 128), (64, 96), 0, False),
            ((0, 32), (32, 64), 1, True), ((32, 64), (32, 64), 1, False),
            ((64, 96), (96, 128), 1, True), ((96, 128), (96, 128), 1, False)):
        src = (bigv if neg else cbuf.ap())[rs[0]:rs[1], 0:8192]
        g_bl = b.dma("gpsimd", fbv[two, rd[0]:rd[1], :], src,
                     wait=(("v", v_ng2),))
    c_ag = b.emit("gpsimd", lambda e: e.collective_compute(
        "AllGather", ALU.bypass, replica_groups=[list(range(NCORE))],
        ins=[fbmy.ap()], outs=[fball.ap()]),
        wait=(("g", g_bl),), inc=("c",))
    b.barrier()

    s_sq2 = 0
    for j in range(NJ):
        b.emit("scalar", lambda e, j_=j: e.activation(
            sqv[:, 0:1024], x_tm[:, j_ * D:(j_ + 1) * D], AF.Copy,
            accum_out=st_[:, S_SUMX + j_:S_SUMX + j_ + 1]), inc=("s",))
        s_sq2 = b.emit("scalar", lambda e, j_=j: e.activation(
            sqv[:, 0:1024], x_tm[:, j_ * D:(j_ + 1) * D], AF.Square,
            accum_out=st_[:, S_SUMSQ + j_:S_SUMSQ + j_ + 1]), inc=("s",))
    s_fin2 = ln_finalize((("s", s_sq2),))
    b.emit("vector", lambda e: e.tensor_copy(
        st_[:, S_QA:S_QA + 32], st_[:, S_R:S_R + 32]),
        wait=(("s", s_fin2),), inc=("v",))
    b.emit("vector", lambda e: e.tensor_tensor(
        st_[:, S_QB:S_QB + 32], st_[:, S_M:S_M + 32], st_[:, S_R:S_R + 32],
        ALU.mult), inc=("v",))
    v_cs2 = b.emit("vector", lambda e: e.tensor_scalar(
        st_[:, S_QB:S_QB + 32], st_[:, S_QB:S_QB + 32], -1.0, None, ALU.mult),
        inc=("v",))
    b.barrier()

    midv = dbuf[:, 0:4096].rearrange("p (t2 jf tlh) -> p jf tlh t2",
                                     t2=64, jf=32, tlh=2)
    ymcv = dbuf[:, 4096:8192].rearrange("p (t1p t2h par) -> p t2h par t1p",
                                        t1p=64, t2h=32, par=2)
    x_tmv = x_tm.ap().rearrange("p (jf c) -> p jf c", jf=32, c=1024)
    vAview = big[:, 0:16384].rearrange("p (c k) -> p k c", c=128, k=128)
    vTview = big[:, ZTC:ZTC + 16384].rearrange("p (pl t2p c) -> p pl t2p c",
                                           pl=2, t2p=64, c=128)
    fallv = fball.ap().rearrange("(n two p) f -> n two p f",
                                 n=NCORE, two=2, p=128)

    def cdump(row0, srcap, nrows=512):
        w0 = tuple((s, b.now(s)) for s in ("t", "v", "s"))
        for a in range(nrows // 128):
            b.dma("gpsimd", y_out[row0 + a * 128:row0 + (a + 1) * 128, :],
                  srcap[:, a, :], wait=w0 if a == 0 else ())
        b.barrier()

    g_m2p = b.dma("gpsimd", tbuf[:, 0:4160], tbl_in["M2P"][:, :],
                  wait=(("t", b.now("t")),))
    g_m2m = b.dma("gpsimd", tbuf[:, 4160:4160 + 4032], tbl_in["M2M"][:, :])
    for bl in (CONV_ORDER or range(NB)):
        if DEBUG_MEMSET:
            mtoks = []
            for buf, cols in ((big, 32768), (cbuf, 8192), (dbuf, 8192)):
                if DEBUG_MEMSET == 1 or (DEBUG_MEMSET == 2 and buf is big) or \
                   (DEBUG_MEMSET == 3 and buf is cbuf) or \
                   (DEBUG_MEMSET == 4 and buf is dbuf):
                    mt = b.emit("gpsimd", lambda e, b_=buf, c_=cols: e.memset(
                        b_[:, 0:c_], 0.0), inc=("p",))
                    mtoks.append(mt)
            if mtoks:
                b.emit("vector", lambda e: e.engine_nop(),
                       wait=(("p", mtoks[-1]),), inc=("v",))
                b.barrier()
        tr_toks = [0, 0]
        for j in range(NJ):
            bank = (j // 4) % 2
            slot = (j % 2) * 128
            if j % 2 == 0:
                ntok = b.emit("scalar", lambda e, j_=j, sl_=slot, bl_=bl:
                              e.activation(
                    nst[:, sl_:sl_ + 128],
                    x_tm[:, j_ * D + bl_ * CB:j_ * D + bl_ * CB + 128],
                    AF.Identity, scale=st_[:, S_QA + j_:S_QA + j_ + 1],
                    bias=st_[:, S_QB + j_:S_QB + j_ + 1]),
                    wait=(("t", tr_toks[0]),), inc=("s",))
                wsem = ("s", ntok)
            else:
                ntok = b.emit("vector", lambda e, j_=j, sl_=slot, bl_=bl:
                              e.tensor_scalar(
                                  nst[:, sl_:sl_ + 128],
                                  x_tm[:, j_ * D + bl_ * CB:
                                       j_ * D + bl_ * CB + 128],
                                  st_[:, S_QA + j_:S_QA + j_ + 1],
                                  st_[:, S_QB + j_:S_QB + j_ + 1],
                                  ALU.mult, ALU.add),
                              wait=(("t", tr_toks[1]),), inc=("v",))
                wsem = ("v", ntok)
            tok = b.emit("tensor", lambda e, j_=j, sl_=slot, bk_=bank: e.matmul(
                ps[bk_][:, (j_ % 4) * 128:(j_ % 4) * 128 + 128],
                nst[:, sl_:sl_ + 128], idn[:, :], start=True, stop=True),
                wait=(wsem,) + (pool.wait_free(bank) if j % 4 == 0 else ()),
                inc=("t",))
            pool.set_mm(bank, tok)
            tr_toks[j % 2] = tok
            if j % 4 == 3:
                vtok = b.emit("scalar", lambda e, j_=j, bk_=bank: e.copy(
                    midv[:, j_ - 3:j_ + 1, :, :],
                    ps[bk_].ap().rearrange("p (jj tlh t2) -> p jj tlh t2",
                                           jj=4, tlh=2, t2=64)),
                    wait=pool.wait_mm(bank), inc=("s",))
                pool.set_evac(bank, "s", vtok)
        v_mid = b.now("s")
        if DUMP == "fb" and bl == 0:
            w0 = tuple((s, b.now(s)) for s in ("t", "v", "s", "c"))
            fv = fball.ap().rearrange("(n two p) f -> n two p f",
                                      n=NCORE, two=2, p=128)
            for k, slot in enumerate((6, 7)):
                for a in range(8):
                    b.dma("gpsimd", y_out[k * 1024 + a * 128:
                                          k * 1024 + (a + 1) * 128, :],
                          fv[slot, 0, :, a * 1024:(a + 1) * 1024],
                          wait=w0 if (k == 0 and a == 0) else ())
            b.barrier()
        if False and DUMP == "fb" and bl == 0:
            for two in range(2):
                b.dma("gpsimd",
                      y_out[two * 1024:(two + 1) * 1024, :]
                      .rearrange("(a p) c -> a p c", a=8, p=128)
                      .rearrange("a p c -> p a c"),
                      fbmy.ap().rearrange("(two p) f -> two p f", two=2, p=128)
                      [two, :, :].rearrange("p (a c) -> p a c", a=8, c=1024),
                      wait=tuple((s, b.now(s)) for s in ("t", "v", "s", "c")))
                b.dma("gpsimd",
                      y_out[2048 + two * 1024:2048 + (two + 1) * 1024, :]
                      .rearrange("(a p) c -> a p c", a=8, p=128)
                      .rearrange("a p c -> p a c"),
                      fball.ap().rearrange("(n two p) f -> n two p f",
                                           n=NCORE, two=2, p=128)
                      [0, two, :, :].rearrange("p (a c) -> p a c", a=8, c=1024))
            b.barrier()
        if DUMP == "conv" and bl == 0:
            cdump(0, dbuf[:, 0:4096].rearrange("p (a c) -> p a c", a=4, c=1024))
        s4_tok = fwd_fft(tbuf, 0, 4160, (("s", v_mid), ("g", g_m2m)))
        if DUMP == "conv" and bl == 0:
            cdump(512, dbuf[:, 4096:8192].rearrange("p (a c) -> p a c", a=4, c=1024))
            cdump(1024, cbuf.ap().rearrange("p (a c) -> p a c", a=8, c=1024),
                  nrows=1024)
        b.barrier()

        for hf in range(2):
            co = hf * 4096
            fo = (hf % 2) * 0
            g_fre = b.dma("sync", wbuf[:, 0:4096],
                          fallv[bl, 0, :, co:co + 4096],
                          wait=((("c", c_ag),) if bl == 0 and hf == 0 else
                                (("v", b.now("v")),)) +
                               (("t", b.now("t")),))
            g_fim = b.dma("sync", wbuf[:, 4096:8192],
                          fallv[bl, 1, :, co:co + 4096])
            g_s1 = b.dma("sync", sshb.ap()[0:32, co:co + 4096],
                         cbuf[32:64, co:co + 4096],
                         wait=(("s", s4_tok),))
            g_s2 = b.dma("sync", sshb.ap()[32:64, co:co + 4096],
                         cbuf[0:32, co:co + 4096])
            g_s3 = b.dma("sync", sshb.ap()[64:96, co:co + 4096],
                         cbuf[96:128, co:co + 4096])
            g_s4 = b.dma("sync", sshb.ap()[96:128, co:co + 4096],
                         cbuf[64:96, co:co + 4096])
            g_s5 = b.dma("sync", dbuf[:, co:co + 4096],
                         sshb.ap()[:, co:co + 4096],
                         wait=(("d", g_s4),))
            v_w1 = b.emit("vector", lambda e, co_=co: e.tensor_tensor(
                cbuf[:, co_:co_ + 4096], cbuf[:, co_:co_ + 4096],
                wbuf[:, 0:4096], ALU.mult),
                wait=(("d", g_fim), ("d", g_s5)), inc=("v",))
            v_w2 = b.emit("vector", lambda e, co_=co: e.tensor_tensor(
                bigv[:, ZTC:ZTC + 4096], dbuf[:, co_:co_ + 4096],
                wbuf[:, 4096:8192], ALU.mult), inc=("v",))
            v_w3 = b.emit("vector", lambda e, co_=co: e.tensor_tensor(
                cbuf[:, co_:co_ + 4096], cbuf[:, co_:co_ + 4096],
                bigv[:, ZTC:ZTC + 4096], ALU.add), inc=("v",))
        v_pw = b.now("v")
        b.barrier()
        if DUMP == "conv" and bl == 0:
            cdump(2048, cbuf.ap().rearrange("p (a c) -> p a c", a=8, c=1024),
                  nrows=1024)

        for gh in range(64):
            bkA = (gh // 4) % 2
            bankA = 0 + 2 * bkA
            bankB = 1 + 2 * bkA
            cA = (gh % 4) * 128
            if gh == 0:
                tok = b.emit("tensor", lambda e: e.matmul(
                    ps[0][:, 0:128], esa[0:64, :], cbuf[0:64, 0:128],
                    start=True, stop=False, tile_position=(0, 0)),
                    wait=pool.wait_free(0) + pool.wait_free(5) + (("v", v_pw),),
                    inc=("t",))
                tok = b.emit("tensor", lambda e: e.matmul(
                    ps[0][:, 0:128], e4096[0:2, :], s4096[0:2, :],
                    start=False, stop=True, tile_position=(0, 0)), inc=("t",))
                pool.set_mm(0, tok)
            else:
                tok = b.emit("tensor", lambda e, g_=gh, c_=cA, bk_=bankA: e.matmul(
                    ps[bk_][:, c_:c_ + 128], esa[0:64, :],
                    cbuf[0:64, g_ * 128:g_ * 128 + 128],
                    start=True, stop=True, tile_position=(0, 0)),
                    wait=(pool.wait_free(bankA) if gh % 4 == 0 else ()) +
                         ((("v", v_pw),) if gh == 1 else ()),
                    inc=("t",))
                pool.set_mm(bankA, tok)
            tok = b.emit("tensor", lambda e, g_=gh, c_=cA, bk_=bankB: e.matmul(
                ps[bk_][:, c_:c_ + 128], esa[64:128, :],
                cbuf[64:128, g_ * 128:g_ * 128 + 128],
                start=True, stop=True, tile_position=(64, 0)),
                wait=(pool.wait_free(bankB) if gh % 4 == 0 else ()) +
                     ((("v", v_pw),) if gh == 0 else ()),
                inc=("t",))
            pool.set_mm(bankB, tok)
            if gh % 4 == 3:
                g0 = gh - 3
                vtok = b.emit("scalar", lambda e, g0_=g0, bk_=bankA: e.copy(
                    vAview[:, g0_:g0_ + 4, :],
                    ps[bk_].ap().rearrange("p (a c) -> p a c", a=4, c=128)),
                    wait=pool.wait_mm(bankA), inc=("s",))
                pool.set_evac(bankA, "s", vtok)
                vtok = b.emit("scalar", lambda e, g0_=g0, bk_=bankB: e.copy(
                    vAview[:, 64 + g0_:64 + g0_ + 4, :],
                    ps[bk_].ap().rearrange("p (a c) -> p a c", a=4, c=128)),
                    wait=pool.wait_mm(bankB), inc=("s",))
                pool.set_evac(bankB, "s", vtok)
        v_sa = ("s", b.now("s"))
        if DUMP == "conv" and bl == 0:
            cdump(3072, big[:, 0:4096].rearrange("p (a c) -> p a c", a=4, c=1024))
        g_mbr = b.dma("gpsimd", wbuf[:, 0:4096], tbl_in["MBre"][:, :],
                      wait=(("v", v_pw),))
        g_mbi = b.dma("gpsimd", wbuf[:, 4096:8192], tbl_in["MBim"][:, :])

        for c in range(128):
            bank = 4 + (c // 4) % 2
            tok = b.emit("tensor", lambda e, c_=c, bk_=bank: e.matmul(
                ps[bk_][:, (c_ % 4) * 128:(c_ % 4) * 128 + 128],
                bigv[:, c_ * 128:c_ * 128 + 128], idn[:, :],
                start=True, stop=True),
                wait=(pool.wait_free(bank) if c % 4 == 0 else ()) +
                     ((v_sa,) if c == 0 else ()),
                inc=("t",))
            pool.set_mm(bank, tok)
            if c % 4 == 3:
                c0 = c - 3
                vtok = b.emit("scalar", lambda e, c0_=c0, bk_=bank: e.copy(
                    vTview[:, :, :, c0_:c0_ + 4],
                    ps[bk_].ap().rearrange("p (a pl t2p) -> p pl t2p a",
                                           a=4, pl=2, t2p=64)),
                    wait=pool.wait_mm(bank), inc=("s",))
                pool.set_evac(bank, "s", vtok)
        v_vt = ("s", b.now("s"))

        for u in range(32):
            bank = 6 + (u // 4) % 2
            for pr in range(2):
                t2p = 2 * u + pr
                r0 = 64 * pr
                tok = b.emit("tensor", lambda e, t_=t2p, r_=r0, u_=u, bk_=bank:
                             e.matmul(
                                 ps[bk_][r_:r_ + 64, (u_ % 4) * 128:
                                         (u_ % 4) * 128 + 128],
                                 wbuf[:, t_ * 64:t_ * 64 + 64],
                                 bigv[:, ZTC + 0 * 8192 + t_ * 128:
                                      ZTC + t_ * 128 + 128],
                                 start=True, stop=False, tile_position=(0, r_)),
                             wait=(pool.wait_free(bank) if u % 4 == 0 and pr == 0
                                   else ()) +
                                  ((v_vt, ("g", g_mbi)) if u == 0 and
                                   pr == 0 else ()),
                             inc=("t",))
                tok = b.emit("tensor", lambda e, t_=t2p, r_=r0, u_=u, bk_=bank:
                             e.matmul(
                                 ps[bk_][r_:r_ + 64, (u_ % 4) * 128:
                                         (u_ % 4) * 128 + 128],
                                 wbuf[:, 4096 + t_ * 64:4096 + t_ * 64 + 64],
                                 bigv[:, ZTC + 8192 + t_ * 128:
                                      ZTC + 8192 + t_ * 128 + 128],
                                 start=False, stop=True, tile_position=(0, r_)),
                             inc=("t",))
                pool.set_mm(bank, tok)
            if u % 4 == 3:
                u0 = u - 3
                vtok = b.emit("scalar", lambda e, u0_=u0, bk_=bank: e.copy(
                    dbuf[:, u0_ * 128:u0_ * 128 + 512], ps[bk_][:, 0:512]),
                    wait=pool.wait_mm(bank), inc=("s",))
                pool.set_evac(bank, "s", vtok)
        v_ym = ("s", b.now("s"))
        if DUMP == "conv" and bl == 0:
            cdump(3584, dbuf[:, 0:4096].rearrange("p (a c) -> p a c", a=4, c=1024))

        for u in range(32):
            bank = (u // 4) % 2
            tok = b.emit("tensor", lambda e, u_=u, bk_=bank: e.matmul(
                ps[bk_][:, (u_ % 4) * 128:(u_ % 4) * 128 + 128],
                dbuf[:, u_ * 128:u_ * 128 + 128], idn[:, :],
                start=True, stop=True),
                wait=(pool.wait_free(bank) if u % 4 == 0 else ()) +
                     ((v_ym,) if u == 0 else ()),
                inc=("t",))
            pool.set_mm(bank, tok)
            if u % 4 == 3:
                u0 = u - 3
                vtok = b.emit("scalar", lambda e, u0_=u0, bk_=bank: e.copy(
                    ymcv[:, u0_:u0_ + 4, :, :],
                    ps[bk_].ap().rearrange("p (a par t1p) -> p a par t1p",
                                           a=4, par=2, t1p=64)),
                    wait=pool.wait_mm(bank), inc=("s",))
                pool.set_evac(bank, "s", vtok)
        v_ymc = ("s", b.now("s"))

        for j in range(NJ):
            bank = 2 + (j // 4) % 2
            tok = b.emit("tensor", lambda e, j_=j, bk_=bank: e.matmul(
                ps[bk_][:, (j_ % 4) * 128:(j_ % 4) * 128 + 128],
                dbuf[:, 4096 + j_ * 128:4096 + j_ * 128 + 128], idn[:, :],
                start=True, stop=True),
                wait=(pool.wait_free(bank) if j % 4 == 0 else ()) +
                     ((v_ymc,) if j == 0 else ()),
                inc=("t",))
            pool.set_mm(bank, tok)
            if j % 4 == 3:
                j0 = j - 3
                vtok = b.emit("vector", lambda e, j0_=j0, bk_=bank, bl_=bl:
                              e.tensor_tensor(
                                  x_tmv[:, j0_:j0_ + 4, bl_ * CB:bl_ * CB + 128],
                                  x_tmv[:, j0_:j0_ + 4, bl_ * CB:bl_ * CB + 128],
                                  ps[bk_].ap().rearrange("p (a c) -> p a c",
                                                         a=4, c=128), ALU.add),
                              wait=pool.wait_mm(bank), inc=("v",))
                pool.set_evac(bank, "v", vtok)
        b.barrier()

    if BUILD_PHASES < 3:
        gd = b.dma("gpsimd",
                   y_out.rearrange("(j p) c -> p j c", p=128),
                   x_tm.ap().rearrange("p (j c) -> p j c", j=NJ, c=D),
                   wait=tuple((s, b.now(s)) for s in ("t", "v", "s")))
        fin = tuple((s, b.now(s)) for s in ("d", "g", "t", "v", "s", "p", "c"))
        b.emit("gpsimd", lambda e: e.nop(), wait=fin)
        with nc.Block() as block:
            b.replay(block, sems)
        es.close()
        return nc

    s_sq3 = v_mx3 = p_mn3 = 0
    for j in range(NJ):
        b.emit("scalar", lambda e, j_=j: e.activation(
            sqv[:, 0:1024], x_tm[:, j_ * D:(j_ + 1) * D], AF.Copy,
            accum_out=st_[:, S_SUMX + j_:S_SUMX + j_ + 1]), inc=("s",))
        s_sq3 = b.emit("scalar", lambda e, j_=j: e.activation(
            sqv[:, 0:1024], x_tm[:, j_ * D:(j_ + 1) * D], AF.Square,
            accum_out=st_[:, S_SUMSQ + j_:S_SUMSQ + j_ + 1]), inc=("s",))
        v_mx3 = b.emit("vector", lambda e, j_=j: e.tensor_reduce(
            st_[:, S_MAX + j_:S_MAX + j_ + 1], x_tm[:, j_ * D:(j_ + 1) * D],
            axis=AX, op=ALU.max), inc=("v",), fence=True)
        p_mn3 = b.emit("vector", lambda e, j_=j: e.tensor_reduce(
            st_[:, S_MIN + j_:S_MIN + j_ + 1], x_tm[:, j_ * D:(j_ + 1) * D],
            axis=AX, op=ALU.min), inc=("v",), fence=True)
    s_fin3 = ln_finalize((("s", s_sq3), ("v", v_mx3)))
    v_lm3 = ln_localmax((("v", p_mn3), ("s", s_fin3)))
    g_cw2 = b.dma("gpsimd", cc2_in.ap()[0:1, 0:8], row1[0:1, 0:8],
                  wait=(("v", v_lm3),))
    c_cc2 = b.emit("gpsimd", lambda e: e.collective_compute(
        "AllReduce", ALU.max, replica_groups=[list(range(NCORE))],
        ins=[cc2_in.ap()], outs=[cc2_out.ap()]),
        wait=(("g", g_cw2),), inc=("c",))
    g_cr2 = b.dma("gpsimd", row1[0:1, 8:16], cc2_out.ap()[0:1, 0:8],
                  wait=(("c", c_cc2),))
    g_qw = b.dma("gpsimd", tbuf[:, 0:8192], qw_spill.ap()[:, :],
                 wait=(("t", b.now("t")),))
    p_sc2 = ffn_scales(g_cr2)
    p_gate = b.emit("vector", lambda e: e.engine_nop(),
                    wait=(("g", g_qw), ("v", p_sc2)), inc=("v",))
    ffn(p_gate, final=True)

    fin = tuple((s, b.now(s)) for s in ("d", "g", "t", "v", "s", "p", "c"))
    b.emit("gpsimd", lambda e: e.nop(), wait=fin)

    with nc.Block() as block:
        b.replay(block, sems)
    es.close()
    return nc


def kernel(L0_attn_x, L0_attn_W1, L0_attn_b1, L0_attn_W2, L0_attn_b2,
           L0_ffn_weight, L0_ffn_bias):
    tb = _tables()
    nc = build_nc()
    wt = np.ascontiguousarray(np.asarray(L0_ffn_weight).T)
    in_maps = []
    for core in range(NCORE):
        m = {"x": np.ascontiguousarray(np.asarray(L0_attn_x)[core]),
             "wt": wt,
             "w2blk": np.ascontiguousarray(
                 np.asarray(L0_attn_W2)[:, core * CB:(core + 1) * CB]),
             "w1": np.ascontiguousarray(np.asarray(L0_attn_W1)),
             "b1": np.ascontiguousarray(
                 np.asarray(L0_attn_b1).reshape(8, 128).T),
             "b2blk": np.ascontiguousarray(
                 np.asarray(L0_attn_b2)[core * CB:(core + 1) * CB]).reshape(128, 1),
             "fbias": np.ascontiguousarray(
                 np.asarray(L0_ffn_bias).reshape(8, 128).T)}
        for k, v in tb.items():
            m["t_" + k] = v
        in_maps.append(m)
    res = run_bass_kernel_spmd(nc, in_maps, core_ids=list(range(NCORE)))
    return np.stack([res.results[i]["y"] for i in range(NCORE)], axis=0)
